# version 03
# baseline (speedup 1.0000x reference)
import contextlib
import numpy as np
import ml_dtypes
import concourse.bass as bass
import concourse.mybir as mybir
from concourse.bass_utils import run_bass_kernel_spmd

F32 = mybir.dt.float32
BF16 = mybir.dt.bfloat16
AF = mybir.ActivationFunctionType
ALU = mybir.AluOpType
AXX = mybir.AxisListType.X
BF = ml_dtypes.bfloat16

TOK = 2048
CK = 256
NCH = 8
NH = 16
HD = 128
DS = 64
D = 1024
KD = 8
DI = 2048
XBCD = 2192
NEG = -1.0e30
EPS = 1e-5
EXPH = 2816
KH = 22
SHH = 1536
KSH = 12
HALF = 1024

ENGS = ("sync", "scalar", "vector", "tensor", "gpsimd")


class Plan:

    def __init__(self):
        self.ops = {e: [] for e in ENGS}
        self.cnt = {e: 0 for e in ENGS}
        self.qcnt = {}
        self.guards = {}

    def gwait(self, *keys):
        out = []
        for k in keys:
            out += self.guards.get(k, [])
        return out

    def gset(self, key, eng, val, kind="s"):
        self.guards[key] = [(kind, eng, val)]

    def _mk(self, W, body, eng):
        def run(e, sems, qsems):
            for item in W:
                kind, en, v = item
                if v > 0:
                    e.wait_ge((sems if kind == "s" else qsems)[en], v)
            if body is not None:
                body(e).then_inc((sems if eng[0] == "s" else qsems)[eng[1]],
                                 1 if eng[0] == "s" else 16)
        return run

    def qchan(self, eng, ch):
        return self.qcnt.get((eng, ch), 0)

    def stage(self, eng, body, waits=()):
        prev = self.cnt[eng]
        self.cnt[eng] += 1
        tgt = self.cnt[eng]
        W = tuple(waits)
        if eng != "tensor" and prev > 0:
            W = W + (("s", eng, prev),)
        self.ops[eng].append(self._mk(W, body, ("s", eng)))
        return tgt

    def dma(self, eng, out, in_, waits=(), ch=0):
        key = (eng, ch)
        prev = self.qcnt.get(key, 0)
        self.qcnt[key] = prev + 16
        tgt = self.qcnt[key]

        def body(e, out=out, in_=in_):
            return e.dma_start(out=out, in_=in_)

        W = tuple(waits) + ((("q", key, prev),) if prev > 0 else ())
        self.ops[eng].append(self._mk(W, body, ("q", key)))
        return tgt

    def wait(self, eng, waits=()):
        self.ops[eng].append(self._mk(tuple(waits), None, ("s", eng)))

    def barrier(self):
        snap = [("s", x, self.cnt[x]) for x in ENGS]
        qsnap = [("q", k, v) for k, v in self.qcnt.items()]
        for e in ENGS:
            self.wait(e, waits=[w for w in snap if w[1] != e] + qsnap)


def S(eng, v):
    return ("s", eng, v)


def Q(eng, v, ch=0):
    return ("q", (eng, ch), v)


def build_program():
    nc = bass.Bass("TRN2", target_bir_lowering=False)
    P = Plan()

    def din(name, shape, dt):
        return nc.dram_tensor(name, list(shape), dt, kind="ExternalInput").ap()

    uTb = din("uTb", (D, TOK + 3), BF16)
    ures = din("ures", (TOK, D), F32)
    winx = din("winx", (D, XBCD), BF16)
    winz = din("winz", (D, DI), BF16)
    woutT = din("woutT", (DI, D), BF16)
    gateT = din("gateT", (D, 8), BF16)
    cwh = din("cwh", (128, 68), F32)
    cbh = din("cbh", (128, 17), F32)
    nwh = din("nwh", (128, 16), F32)
    dtbh = din("dtbh", (16, 1), F32)
    alogh = din("alogh", (16, 1), F32)
    dprh = din("dprh", (1, 16), F32)
    maskh = din("maskh", (128, 512), F32)
    identfh = din("identfh", (128, 128), F32)
    identbh = din("identbh", (128, 128), BF16)
    onesfh = din("onesfh", (1, 4096), F32)
    onesch = din("onesch", (1, 128), F32)
    pmath = din("pmath", (4, 4), F32)
    mvech = din("mvech", (4, 1), F32)
    konsth = din("konsth", (128, 4), F32)
    w1h = din("w1h", (8, D, EXPH), BF16)
    w2h = din("w2h", (8, EXPH, D), BF16)
    w3h = din("w3h", (8, D, EXPH), BF16)
    sw1h = din("sw1h", (D, SHH), BF16)
    sw2h = din("sw2h", (SHH, D), BF16)
    sw3h = din("sw3h", (D, SHH), BF16)

    outh = nc.dram_tensor("out", [TOK, D], F32, kind="ExternalOutput").ap()

    yacch = nc.dram_tensor("yacch", [16, 128, DI], BF16).ap()
    sh = nc.dram_tensor("sh", [NCH, DS, NH * HD], BF16).ap()
    ccin = nc.dram_tensor("ccin", [65, 2048], F32).ap()
    ccout = nc.dram_tensor("ccout", [260, 2048], F32).ap()

    w1v = w1h.rearrange("e (k p) h -> e p k h", p=128)
    w3v = w3h.rearrange("e (k p) h -> e p k h", p=128)
    w2v = w2h.rearrange("e (k p) d -> e p k d", p=128)
    sw1v = sw1h.rearrange("(k p) h -> p k h", p=128)
    sw3v = sw3h.rearrange("(k p) h -> p k h", p=128)
    sw2v = sw2h.rearrange("(k p) d -> p k d", p=128)
    uTv = uTb.rearrange("(k p) t -> p k t", p=128)
    winxv = winx.rearrange("(k p) c -> p k c", p=128)
    winzv = winz.rearrange("(k p) c -> p k c", p=128)
    woutv = woutT.rearrange("(k p) d -> p k d", p=128)
    gatev = gateT.rearrange("(k p) e -> p k e", p=128)

    stack = contextlib.ExitStack()

    def sb(name, shape, dt):
        return stack.enter_context(nc.sbuf_tensor(name, list(shape), dt))

    cw_s = sb("cw_s", (128, 68), F32)
    cb_s = sb("cb_s", (128, 17), F32)
    nw_s = sb("nw_s", (128, 16), F32)
    dtb_s = sb("dtb_s", (16, 1), F32)
    A_s = sb("A_s", (16, 1), F32)
    tA_s = sb("tA_s", (16, 1), F32)
    dpr_s = sb("dpr_s", (1, 16), F32)
    dpbc_s = sb("dpbc_s", (128, 16), F32)
    mask_s = sb("mask_s", (128, 512), F32)
    idf_s = sb("idf_s", (128, 128), F32)
    idb_s = sb("idb_s", (128, 128), BF16)
    onec_s = sb("onec_s", (1, 128), F32)
    gatet_s = sb("gatet_s", (128, 64), BF16)
    x1_s = sb("x1_s", (128, 16 * D), BF16)
    prev_s = sb("prev_s", (DS, NH * HD), F32)
    prevb_s = sb("prevb_s", (DS, NH * HD), BF16)
    cdbc_s = sb("cdbc_s", (DS, NCH * NH), F32)
    eaT_s = sb("eaT_s", (128, 16 * NH), F32)
    alacc_s = sb("alacc_s", (16, 1), F32)
    Cb_s = sb("Cb_s", (DS, NCH * CK), BF16)
    sumsq_s = sb("sumsq_s", (128, 2), F32)
    pmat_s = sb("pmat_s", (4, 4), F32)
    ko_s = sb("ko_s", (128, 4), F32)
    mvec_s = sb("mvec_s", (4, 1), F32)

    pA = stack.enter_context(nc.psum_tensor("pA", [128, 1024], F32))
    pB = stack.enter_context(nc.psum_tensor("pB", [128, 1024], F32))
    pC = stack.enter_context(nc.psum_tensor("pC", [128, 1024], F32))
    pD = stack.enter_context(nc.psum_tensor("pD", [128, 1024], F32))
    PT = {"A": pA, "B": pB, "C": pC, "D": pD}

    def bank(t, lo):
        return (t, lo // 512)

    for dst, src in ((cw_s, cwh), (cb_s, cbh), (nw_s, nwh), (dtb_s, dtbh),
                     (mask_s, maskh), (idf_s, identfh), (idb_s, identbh),
                     (onec_s, onesch), (pmat_s, pmath), (mvec_s, mvech),
                     (tA_s, alogh), (dpr_s, dprh), (ko_s, konsth)):
        P.dma("sync", dst[:, :], src[:, :])
    P.dma("sync", gatet_s[:, :].rearrange("p (k e) -> p k e", k=KD), gatev[:, :, :])
    q_init = P.qchan("sync", 0)

    sA1 = P.stage("scalar",
                  lambda e: e.activation(A_s[:, :], tA_s[:, :], AF.Exp),
                  waits=[Q("sync", q_init)])
    vA1 = P.stage("vector",
                  lambda e: e.tensor_scalar(A_s[:, :], A_s[:, :], -1.0, None, ALU.mult),
                  waits=[S("scalar", sA1)])
    tDp = P.stage("tensor",
                  lambda e: e.matmul(pC[0:128, 0:16], onec_s[0:1, 0:128],
                                     dpr_s[0:1, 0:16], start=True, stop=True),
                  waits=[Q("sync", q_init)])
    vDp = P.stage("vector", lambda e: e.tensor_copy(dpbc_s[:, :], pC[0:128, 0:16]),
                  waits=[S("tensor", tDp)])
    P.gset(bank("C", 0), "vector", vDp)
    vz_al = P.stage("vector", lambda e: e.memset(alacc_s[:, :], 0.0))
    vz_pr = P.stage("vector", lambda e: e.memset(prev_s[:, :], 0.0))

    with contextlib.ExitStack() as scopeA:
        def sba(name, shape, dt):
            return scopeA.enter_context(nc.sbuf_tensor(name, list(shape), dt))

        winx_s = sba("winx_s", (128, KD * XBCD), BF16)
        uta_s = sba("uta_s", (128, 2 * KD * 259), BF16)
        lh2_s = sba("lh2_s", (2, 4096), F32)
        rh2_s = sba("rh2_s", (2, 4096), F32)
        acc_s = sba("acc_s", (128, 512), F32)
        xcv_s = sba("xcv_s", (128, 512), F32)
        bc_s = sba("bc_s", (128, 256), F32)
        bt_s = sba("bt_s", (128, 128), F32)
        bb_s = sba("bb_s", (DS, 256), BF16)
        xt_s = sba("xt_s", (128, 2 * DI), BF16)
        t16_s = sba("t16_s", (16, 256), F32)
        dtv_s = sba("dtv_s", (16, 256), F32)
        a_s = sba("a_s", (16, 256), F32)
        acum_s = sba("acum_s", (16, 256), F32)
        nacum_s = sba("nacum_s", (16, 256), F32)
        ws_s = sba("ws_s", (16, 256), F32)
        zz16_s = sba("zz16_s", (16, 256), F32)
        dtvT_s = sba("dtvT_s", (128, 32), F32)
        wsT_s = sba("wsT_s", (128, 32), F32)
        sc_s = sba("sc_s", (128, 512), F32)
        d0_s = sba("d0_s", (128, 2 * 384), F32)
        e_s = sba("e_s", (128, 2 * 384), F32)
        mt_s = sba("mt_s", (128, 2 * 384), BF16)
        ya_s = sba("ya_s", (128, 2 * DI), BF16)
        ss_s = sba("ss_s", (DS, NH * HD), BF16)
        wbt_s = sba("wbt_s", (128, 128), BF16)
        cdr_s = sba("cdr_s", (1, 16), F32)
        alrow_s = sba("alrow_s", (1, 2048), F32)

        P.dma("sync", winx_s[:, :].rearrange("p (k c) -> p k c", k=KD), winxv[:, :, :])
        P.dma("sync", lh2_s[0:1, :], onesfh[:, :])
        P.dma("sync", rh2_s[1:2, :], onesfh[:, :])
        q_wx = P.qchan("sync", 0)
        vz16 = P.stage("vector", lambda e: e.memset(zz16_s[:, :], 0.0))
        P.stage("vector", lambda e: e.memset(alrow_s[:, :], 0.0))

        def uta_dst(c):
            o = (c % 2) * (KD * 259)
            return uta_s[:, o:o + KD * 259].rearrange("p (k t) -> p k t", k=KD)

        q_uta = [0] * (NCH + 1)
        q_uta[0] = P.dma("sync", uta_dst(0), uTv[:, :, 0:259], ch=5)

        uta_tguard = [0, 0]
        ya_qd = [0] * NCH
        ss_qd = [0] * NCH
        pack_tguard = 0
        xt_tguard = 0
        bb_tguard = 0
        sm_tr_guard = 0
        prev_cdr_q = 0
        prev_pk1_q = 0
        v_acum_readers = 0

        for c in range(NCH):
            off = (c % 2) * (KD * 259)
            if c + 1 < NCH:
                q_uta[c + 1] = P.dma("sync", uta_dst(c + 1),
                                     uTv[:, :, (c + 1) * 256:(c + 1) * 256 + 259],
                                     waits=[S("tensor", uta_tguard[(c + 1) % 2])],
                                     ch=5 + (c + 1) % 2)

            trc = 0
            ssl_guard = [0, 0]
            xcv_tguard = [0, 0]
            last_bc_v = 0
            last_tmm = 0
            for ct in range(17):
                half = ct % 2

                def mm_ip(e, ct=ct, off=off, half=half):
                    for k in range(KD):
                        inst = e.matmul(
                            pA[:, half * 512:half * 512 + 259],
                            winx_s[:, k * XBCD + ct * 128:k * XBCD + (ct + 1) * 128],
                            uta_s[:, off + k * 259:off + (k + 1) * 259],
                            start=(k == 0), stop=(k == KD - 1))
                    return inst

                tmm = P.stage("tensor", mm_ip,
                              waits=P.gwait(bank("A", half * 512)) +
                              [Q("sync", q_uta[c], 5 + c % 2), Q("sync", q_wx)])
                last_tmm = tmm
                uta_tguard[c % 2] = tmm

                ps = pA[:, half * 512:half * 512 + 259]
                ac = acc_s[:, half * 256:half * 256 + 256]
                P.stage("vector",
                        lambda e, ps=ps, ac=ac, ct=ct: e.tensor_scalar(
                            ac, ps[:, 0:256], cw_s[:, ct:ct + 1],
                            cb_s[:, ct:ct + 1], ALU.mult, ALU.add),
                        waits=[S("tensor", tmm), S("scalar", ssl_guard[half])])
                for w in (1, 2, 3):
                    vconv = P.stage("vector",
                                    lambda e, ps=ps, ac=ac, ct=ct, w=w:
                                    e.scalar_tensor_tensor(
                                        ac, ps[:, w:w + 256],
                                        cw_s[:, w * 17 + ct:w * 17 + ct + 1],
                                        ac, ALU.mult, ALU.add))
                P.gset(bank("A", half * 512), "vector", vconv)

                if ct < 16:
                    ssl = P.stage("scalar",
                                  lambda e, half=half: e.activation(
                                      xcv_s[:, half * 256:half * 256 + 256],
                                      acc_s[:, half * 256:half * 256 + 256], AF.Silu),
                                  waits=[S("vector", vconv),
                                         S("tensor", xcv_tguard[half])])
                    ssl_guard[half] = ssl
                    for qt in range(2):
                        tb = trc % 2
                        ttr = P.stage("tensor",
                                      lambda e, half=half, qt=qt, tb=tb: e.transpose(
                                          pB[:, tb * 512:tb * 512 + 128],
                                          xcv_s[:, half * 256 + qt * 128:half * 256 + qt * 128 + 128],
                                          idf_s[:, :]),
                                      waits=P.gwait(bank("B", tb * 512)) +
                                      [S("scalar", ssl)])
                        vtr = P.stage("vector",
                                      lambda e, qt=qt, ct=ct, tb=tb: e.tensor_copy(
                                          xt_s[:, qt * DI + ct * 128:qt * DI + (ct + 1) * 128],
                                          pB[:, tb * 512:tb * 512 + 128]),
                                      waits=[S("tensor", ttr), S("tensor", xt_tguard)])
                        P.gset(bank("B", tb * 512), "vector", vtr)
                        xcv_tguard[half] = ttr
                        trc += 1
                else:
                    ssl = P.stage("scalar",
                                  lambda e: e.activation(bc_s[:, :],
                                                         acc_s[:, half * 256:half * 256 + 256],
                                                         AF.Silu),
                                  waits=[S("vector", vconv), S("tensor", bb_tguard)])
                    ssl_guard[half] = ssl
                    vbc = P.stage("vector",
                                  lambda e: e.tensor_copy(bb_s[:, :], bc_s[0:64, :]),
                                  waits=[S("scalar", ssl), S("tensor", bb_tguard)])
                    vcc = P.stage("vector",
                                  lambda e, c=c: e.tensor_copy(
                                      Cb_s[:, c * CK:(c + 1) * CK], bc_s[64:128, :]),
                                  waits=[S("scalar", ssl)])
                    for qt in range(2):
                        tb = trc % 2
                        ttr = P.stage("tensor",
                                      lambda e, qt=qt, tb=tb: e.transpose(
                                          pB[0:128, tb * 512:tb * 512 + 64],
                                          bc_s[0:64, qt * 128:qt * 128 + 128],
                                          idf_s[0:64, 0:64]),
                                      waits=P.gwait(bank("B", tb * 512)) +
                                      [S("scalar", ssl)])
                        vtr = P.stage("vector",
                                      lambda e, qt=qt, tb=tb: e.tensor_copy(
                                          bt_s[:, qt * 64:qt * 64 + 64],
                                          pB[0:128, tb * 512:tb * 512 + 64]),
                                      waits=[S("tensor", ttr)])
                        P.gset(bank("B", tb * 512), "vector", vtr)
                        bb_tguard = ttr
                        trc += 1
                    last_bc_v = max(vbc, vcc)

            def mm_dt(e, off=off):
                for k in range(KD):
                    inst = e.matmul(
                        pA[0:16, 512:512 + 256],
                        winx_s[:, k * XBCD + 2176:k * XBCD + 2192],
                        uta_s[:, off + k * 259 + 3:off + (k + 1) * 259],
                        start=(k == 0), stop=(k == KD - 1))
                return inst

            tdt = P.stage("tensor", mm_dt, waits=P.gwait(bank("A", 512)))
            uta_tguard[c % 2] = tdt

            s_sp1 = P.stage("scalar",
                            lambda e: e.activation(t16_s[:, :], pA[0:16, 512:512 + 256],
                                                   AF.Exp, bias=dtb_s[:, 0:1]),
                            waits=[S("tensor", tdt), Q("gpsimd", prev_cdr_q)])
            P.gset(bank("A", 512), "scalar", s_sp1)
            v_sp = P.stage("vector",
                           lambda e: e.tensor_scalar(t16_s[:, :], t16_s[:, :],
                                                     1.0, None, ALU.add),
                           waits=[S("scalar", s_sp1)])
            s_sp2 = P.stage("scalar",
                            lambda e: e.activation(dtv_s[:, :], t16_s[:, :], AF.Ln),
                            waits=[S("vector", v_sp), S("tensor", sm_tr_guard)])
            v_a = P.stage("vector",
                          lambda e: e.tensor_scalar(a_s[:, :], dtv_s[:, :],
                                                    A_s[:, 0:1], None, ALU.mult),
                          waits=[S("scalar", s_sp2), S("vector", vA1)])
            v_cum = P.stage("vector",
                            lambda e: e.tensor_tensor_scan(acum_s[:, :], a_s[:, :],
                                                           zz16_s[:, :], 0.0,
                                                           ALU.add, ALU.add),
                            waits=[S("vector", vz16), S("tensor", sm_tr_guard),
                                   Q("gpsimd", v_acum_readers)])
            v_nac = P.stage("vector",
                            lambda e: e.tensor_scalar(nacum_s[:, :], acum_s[:, :],
                                                      -1.0, None, ALU.mult),
                            waits=[Q("gpsimd", prev_pk1_q)])
            s_ws = P.stage("scalar",
                           lambda e: e.activation(ws_s[:, :], acum_s[:, :], AF.Exp,
                                                  bias=acum_s[:, 255:256],
                                                  scale=ko_s[0:16, 1:2]),
                           waits=[S("vector", v_cum), S("tensor", sm_tr_guard)])
            v_ws = P.stage("vector",
                           lambda e: e.tensor_tensor(ws_s[:, :], ws_s[:, :],
                                                     dtv_s[:, :], ALU.mult),
                           waits=[S("scalar", s_ws)])
            s_cd = P.stage("scalar",
                           lambda e: e.activation(t16_s[:, 0:1], acum_s[:, 255:256],
                                                  AF.Exp),
                           waits=[S("vector", v_cum), S("vector", v_sp)])
            v_al = P.stage("vector",
                           lambda e: e.tensor_tensor(alacc_s[:, :], alacc_s[:, :],
                                                     acum_s[:, 255:256], ALU.add),
                           waits=[S("vector", vz_al)])
            q_cdr = P.dma("gpsimd", cdr_s[0:1, :], t16_s[:, 0:1],
                          waits=[S("scalar", s_cd)])
            prev_cdr_q = q_cdr
            t_cd = P.stage("tensor",
                           lambda e: e.matmul(pC[0:64, 16:32], onec_s[0:1, 0:64],
                                              cdr_s[0:1, 0:16], start=True, stop=True),
                           waits=P.gwait(bank("C", 0)) + [Q("gpsimd", q_cdr)])
            v_cd = P.stage("vector",
                           lambda e, c=c: e.tensor_copy(cdbc_s[:, c * NH:(c + 1) * NH],
                                                        pC[0:64, 16:32]),
                           waits=[S("tensor", t_cd)])
            P.gset(bank("C", 0), "vector", v_cd)

            q_pk1 = P.dma("gpsimd",
                          lh2_s[1:2, :].rearrange("o (h t) -> o h t", h=16),
                          nacum_s[:, :],
                          waits=[S("vector", v_nac), S("tensor", pack_tguard)])
            q_pk2 = P.dma("gpsimd",
                          rh2_s[0:1, :].rearrange("o (h t) -> o h t", h=16),
                          acum_s[:, :],
                          waits=[S("vector", v_cum), S("tensor", pack_tguard)])
            q_pack = q_pk2
            v_acum_readers = q_pk2
            prev_pk1_q = q_pk1

            sm_done = {}
            for name, srcT in (("dtv", dtv_s), ("ws", ws_s), ("acum", acum_s)):
                for qt in range(2):
                    tb = trc % 2
                    wdeps = {"dtv": [S("scalar", s_sp2)],
                             "ws": [S("vector", v_ws)],
                             "acum": [S("vector", v_cum)]}[name]
                    ttr = P.stage("tensor",
                                  lambda e, srcT=srcT, qt=qt, tb=tb: e.transpose(
                                      pB[0:128, tb * 512 + 256:tb * 512 + 256 + 16],
                                      srcT[:, qt * 128:qt * 128 + 128],
                                      idf_s[0:16, 0:16]),
                                  waits=P.gwait(bank("B", tb * 512)) + wdeps)
                    sm_tr_guard = ttr
                    if name == "acum":
                        ev = P.stage("scalar",
                                     lambda e, c=c, qt=qt, tb=tb: e.activation(
                                         eaT_s[:, (c * 2 + qt) * 16:(c * 2 + qt) * 16 + 16],
                                         pB[0:128, tb * 512 + 256:tb * 512 + 272],
                                         AF.Exp),
                                     waits=[S("tensor", ttr)])
                        P.gset(bank("B", tb * 512), "scalar", ev)
                    else:
                        dst = (dtvT_s if name == "dtv" else wsT_s)
                        ev = P.stage("vector",
                                     lambda e, dst=dst, qt=qt, tb=tb: e.tensor_copy(
                                         dst[:, qt * 16:qt * 16 + 16],
                                         pB[0:128, tb * 512 + 256:tb * 512 + 272]),
                                     waits=[S("tensor", ttr)])
                        P.gset(bank("B", tb * 512), "vector", ev)
                    sm_done[(name, qt)] = ev
                    trc += 1

            def mm_sc(e, c=c):
                e.matmul(pC[:, 256:512], bb_s[:, 0:128],
                         Cb_s[:, c * CK:(c + 1) * CK], start=True, stop=True)
                return e.matmul(pC[:, 512:768], bb_s[:, 128:256],
                                Cb_s[:, c * CK:(c + 1) * CK], start=True, stop=True)

            t_sc = P.stage("tensor", mm_sc,
                           waits=P.gwait(bank("C", 0), bank("C", 512)) +
                           [S("vector", last_bc_v)])
            bb_tguard = t_sc
            v_sc = P.stage("vector",
                           lambda e: (e.tensor_tensor(sc_s[:, 0:256], pC[:, 256:512],
                                                      mask_s[:, 0:256], ALU.mult),
                                      e.tensor_tensor(sc_s[:, 256:512], pC[:, 512:768],
                                                      mask_s[:, 256:512], ALU.mult))[-1],
                           waits=[S("tensor", t_sc)])
            P.gset(bank("C", 0), "vector", v_sc)
            P.gset(bank("C", 512), "vector", v_sc)

            mt_tguard = [0, 0]
            sexp_guard = [0, 0]
            vd0_sguard = [0, 0]
            last_vya = 0
            last_vst = 0
            for h in range(NH):
                hp = h % 2
                pd = pD[:, hp * 512:hp * 512 + 384]

                def mm_diff(e, h=h, pd=pd):
                    e.matmul(pd[:, 0:256], lh2_s[0:2, h * 256:h * 256 + 128],
                             rh2_s[0:2, h * 256:h * 256 + 256],
                             start=True, stop=True)
                    return e.matmul(pd[:, 256:384],
                                    lh2_s[0:2, h * 256 + 128:h * 256 + 256],
                                    rh2_s[0:2, h * 256 + 128:h * 256 + 256],
                                    start=True, stop=True)

                t_df = P.stage("tensor", mm_diff,
                               waits=P.gwait(bank("D", hp * 512)) +
                               [Q("gpsimd", q_pack), Q("sync", q_wx)])
                pack_tguard = t_df
                v_d0 = P.stage("vector",
                               lambda e, pd=pd, hp=hp: e.tensor_scalar(
                                   d0_s[:, hp * 384:hp * 384 + 384], pd[:, 0:384],
                                   0.0, None, ALU.min),
                               waits=[S("tensor", t_df), S("scalar", vd0_sguard[hp])])
                P.gset(bank("D", hp * 512), "vector", v_d0)
                s_ex = P.stage("scalar",
                               lambda e, hp=hp: e.activation(
                                   e_s[:, hp * 384:hp * 384 + 384],
                                   d0_s[:, hp * 384:hp * 384 + 384], AF.Exp),
                               waits=[S("vector", v_d0), S("vector", sexp_guard[hp])])
                vd0_sguard[hp] = s_ex
                v_mt = P.stage("vector",
                               lambda e, h=h, hp=hp: (
                                   e.scalar_tensor_tensor(
                                       mt_s[:, hp * 384:hp * 384 + 256],
                                       e_s[:, hp * 384:hp * 384 + 256],
                                       dtvT_s[:, h:h + 1], sc_s[:, 0:256],
                                       ALU.mult, ALU.mult),
                                   e.scalar_tensor_tensor(
                                       mt_s[:, hp * 384 + 256:hp * 384 + 384],
                                       e_s[:, hp * 384 + 256:hp * 384 + 384],
                                       dtvT_s[:, 16 + h:16 + h + 1], sc_s[:, 384:512],
                                       ALU.mult, ALU.mult))[-1],
                               waits=[S("scalar", s_ex), S("vector", v_sc),
                                      S("vector", sm_done[("dtv", 1)]),
                                      S("tensor", mt_tguard[hp])])
                sexp_guard[hp] = v_mt

                py = pC[:, hp * 512:hp * 512 + 256]

                def mm_yd(e, h=h, hp=hp, py=py):
                    e.matmul(py[:, 0:128], mt_s[:, hp * 384:hp * 384 + 128],
                             xt_s[:, h * 128:(h + 1) * 128], start=True, stop=True)
                    e.matmul(py[:, 128:256], mt_s[:, hp * 384 + 128:hp * 384 + 256],
                             xt_s[:, h * 128:(h + 1) * 128], start=True, stop=False)
                    return e.matmul(py[:, 128:256], mt_s[:, hp * 384 + 256:hp * 384 + 384],
                                    xt_s[:, DI + h * 128:DI + (h + 1) * 128],
                                    start=False, stop=True)

                t_yd = P.stage("tensor", mm_yd,
                               waits=P.gwait(bank("C", hp * 512)) + [S("vector", v_mt)])
                mt_tguard[hp] = t_yd
                xt_tguard = t_yd
                v_ya = P.stage("vector",
                               lambda e, h=h, hp=hp, py=py: (
                                   e.scalar_tensor_tensor(
                                       ya_s[:, h * 128:(h + 1) * 128],
                                       xt_s[:, h * 128:(h + 1) * 128],
                                       dpbc_s[:, h:h + 1], py[:, 0:128],
                                       ALU.mult, ALU.add),
                                   e.scalar_tensor_tensor(
                                       ya_s[:, DI + h * 128:DI + (h + 1) * 128],
                                       xt_s[:, DI + h * 128:DI + (h + 1) * 128],
                                       dpbc_s[:, h:h + 1], py[:, 128:256],
                                       ALU.mult, ALU.add))[-1],
                               waits=[S("tensor", t_yd), S("vector", vDp),
                                      Q("gpsimd", ya_qd[c - 1] if c >= 1 else 0, 1)])
                P.gset(bank("C", hp * 512), "vector", v_ya)
                last_vya = v_ya

                v_wb = P.stage("vector",
                               lambda e, h=h: (
                                   e.tensor_scalar(wbt_s[:, 0:64], bt_s[:, 0:64],
                                                   wsT_s[:, h:h + 1], None, ALU.mult),
                                   e.tensor_scalar(wbt_s[:, 64:128], bt_s[:, 64:128],
                                                   wsT_s[:, 16 + h:16 + h + 1],
                                                   None, ALU.mult))[-1],
                               waits=[S("vector", sm_done[("ws", 1)]),
                                      S("vector", last_vst),
                                      S("tensor", P.cnt["tensor"] if h else 0)])

                pst = pB[0:64, (h % 2) * 512:(h % 2) * 512 + 128]

                def mm_st(e, h=h, pst=pst):
                    e.matmul(pst, wbt_s[:, 0:64], xt_s[:, h * 128:(h + 1) * 128],
                             start=True, stop=False)
                    return e.matmul(pst, wbt_s[:, 64:128],
                                    xt_s[:, DI + h * 128:DI + (h + 1) * 128],
                                    start=False, stop=True)

                t_st = P.stage("tensor", mm_st,
                               waits=P.gwait(bank("B", (h % 2) * 512)) +
                               [S("vector", v_wb)])
                xt_tguard = t_st
                v_st = P.stage("vector",
                               lambda e, h=h, pst=pst, c=c: (
                                   e.tensor_copy(ss_s[:, h * 128:(h + 1) * 128], pst),
                                   e.scalar_tensor_tensor(
                                       prev_s[:, h * 128:(h + 1) * 128],
                                       prev_s[:, h * 128:(h + 1) * 128],
                                       cdbc_s[:, c * NH + h:c * NH + h + 1],
                                       pst, ALU.mult, ALU.add))[-1],
                               waits=[S("tensor", t_st), S("vector", v_cd),
                                      S("vector", vz_pr),
                                      Q("gpsimd", ss_qd[c - 1] if c >= 1 else 0, 2)])
                P.gset(bank("B", (h % 2) * 512), "vector", v_st)
                last_vst = v_st

            ya_qd[c] = P.dma("gpsimd",
                             yacch[c * 2:(c + 1) * 2, :, :].rearrange("a q d -> q a d"),
                             ya_s[:, :].rearrange("q (a d) -> q a d", a=2),
                             waits=[S("vector", last_vya)], ch=1)
            ss_qd[c] = P.dma("gpsimd", sh[c, :, :], ss_s[:, :],
                             waits=[S("vector", last_vst)], ch=2)

        q_cc1 = P.dma("gpsimd", ccin[0:64, :], prev_s[:, :],
                      waits=[S("vector", P.cnt["vector"])])
        P.dma("gpsimd", alrow_s[0:1, 0:16], alacc_s[:, 0:1],
              waits=[S("vector", P.cnt["vector"])])
        q_cc2 = P.dma("gpsimd", ccin[64:65, :], alrow_s[:, :])
        g_cc = P.stage("gpsimd",
                       lambda e: e.collective_compute(
                           "AllGather", ALU.bypass,
                           replica_groups=[[0, 1, 2, 3], [4, 5, 6, 7]],
                           ins=[ccin.opt()], outs=[ccout.opt()]),
                       waits=[Q("gpsimd", max(q_cc1, q_cc2))])

    P.barrier()

    with contextlib.ExitStack() as scopeB:
        def sbb(name, shape, dt):
            return scopeB.enter_context(nc.sbuf_tensor(name, list(shape), dt))

        winz_s = sbb("winz_s", (128, KD * DI), BF16)
        utb_s = sbb("utb_s", (128, 2 * KD * 256), BF16)
        wout_s = sbb("wout_s", (128, 16 * 1024), BF16)
        sz_s = sbb("sz_s", (128, DI), F32)
        y_s = sbb("y_s", (128, DI), F32)
        g_s = sbb("g_s", (128, DI), F32)
        gt_s = sbb("gt_s", (128, DI), BF16)
        ur_s = sbb("ur_s", (128, 2 * 2 * D), F32)
        yin_s = sbb("yin_s", (128, 2 * 2 * DI), BF16)
        sin_s = sbb("sin_s", (DS, 2 * NH * HD), BF16)
        xpre_s = sbb("xpre_s", (128, D), F32)
        col_s = sbb("col_s", (128, 8), F32)
        alogs_s = sbb("alogs_s", (4, 16), F32)
        coef_s = sbb("coef_s", (4, 16), F32)
        coefr_s = sbb("coefr_s", (1, 16), F32)
        coefbc_s = sbb("coefbc_s", (DS, 4 * 16), F32)

        P.dma("sync", winz_s[:, :].rearrange("p (k c) -> p k c", k=KD), winzv[:, :, :])
        P.dma("sync", wout_s[:, :].rearrange("p (k d) -> p k d", k=16), woutv[:, :, :])
        q_wzo = P.qchan("sync", 0)

        g_after_cc = P.cnt["gpsimd"]
        for j in range(4):
            P.dma("gpsimd", alogs_s[j:j + 1, :], ccout[65 * j + 64:65 * j + 65, 0:16],
                  waits=[S("gpsimd", g_after_cc)])
        q_al = P.qchan("gpsimd", 0)
        t_co = P.stage("tensor",
                       lambda e: e.matmul(pC[0:4, 0:16], pmat_s[:, :], alogs_s[:, :],
                                          start=True, stop=True),
                       waits=P.gwait(bank("C", 0)) + [Q("gpsimd", q_al)])
        s_co = P.stage("scalar",
                       lambda e: e.activation(coef_s[:, :], pC[0:4, 0:16], AF.Exp),
                       waits=[S("tensor", t_co)])
        P.gset(bank("C", 0), "scalar", s_co)
        v_co = P.stage("vector",
                       lambda e: e.tensor_scalar(coef_s[:, :], coef_s[:, :],
                                                 mvec_s[:, 0:1], None, ALU.mult),
                       waits=[S("scalar", s_co)])
        v_pz = P.stage("vector", lambda e: e.memset(prev_s[:, :], 0.0),
                       waits=[Q("gpsimd", q_al)])
        prev_v = v_pz
        q_sj_prev = 0
        for j in range(4):
            q_cr = P.dma("gpsimd", coefr_s[:, :], coef_s[j:j + 1, :],
                         waits=[S("vector", v_co),
                                S("tensor", P.cnt["tensor"])])
            t_bc = P.stage("tensor",
                           lambda e: e.matmul(pC[0:64, 32:48], onec_s[0:1, 0:64],
                                              coefr_s[0:1, :], start=True, stop=True),
                           waits=P.gwait(bank("C", 0)) + [Q("gpsimd", q_cr)])
            v_bc = P.stage("vector",
                           lambda e, j=j: e.tensor_copy(
                               coefbc_s[:, j * 16:(j + 1) * 16], pC[0:64, 32:48]),
                           waits=[S("tensor", t_bc)])
            P.gset(bank("C", 0), "vector", v_bc)
            q_sj = P.dma("gpsimd", y_s[0:64, 0:2048], ccout[65 * j:65 * j + 64, :],
                         waits=[S("gpsimd", g_after_cc), S("vector", prev_v)])

            def mk_carry(j=j):
                def body(e):
                    for h in range(NH):
                        inst = e.scalar_tensor_tensor(
                            prev_s[:, h * 128:(h + 1) * 128],
                            y_s[0:64, h * 128:(h + 1) * 128],
                            coefbc_s[:, j * 16 + h:j * 16 + h + 1],
                            prev_s[:, h * 128:(h + 1) * 128], ALU.mult, ALU.add)
                    return inst
                return body

            prev_v = P.stage("vector", mk_carry(),
                             waits=[S("vector", max(prev_v, v_bc)), Q("gpsimd", q_sj)])
        prevb_v = P.stage("vector",
                          lambda e: e.tensor_copy(prevb_s[:, :], prev_s[:, :]),
                          waits=[S("vector", prev_v)])

        def utb_dst(c):
            o = (c % 2) * (KD * 256)
            return utb_s[:, o:o + KD * 256].rearrange("p (k t) -> p k t", k=KD)

        q_utb = [0] * NCH
        q_yin = [0] * NCH
        q_sin = [0] * NCH
        q_ur = [0] * NCH
        q_utb[0] = P.dma("sync", utb_dst(0), uTv[:, :, 3:259], ch=5)
        q_yin[0] = P.dma("gpsimd",
                         yin_s[:, 0:2 * DI].rearrange("q (a d) -> q a d", a=2),
                         yacch[0:2, :, :].rearrange("a q d -> q a d"),
                         waits=[Q("gpsimd", P.qchan("gpsimd", 1), 1)], ch=3)
        q_sin[0] = P.dma("gpsimd", sin_s[:, 0:NH * HD], sh[0, :, :],
                         waits=[Q("gpsimd", P.qchan("gpsimd", 2), 2)], ch=5)
        q_ur[0] = P.dma("sync",
                        ur_s[:, 0:2 * D].rearrange("p (a d) -> p a d", a=2),
                        ures[0:256, :].rearrange("(a p) d -> p a d", p=128), ch=7)

        utb_tguard = [0, 0]
        yin_vguard = [0, 0]
        sin_vguard = [0, 0]
        ur_vguard = [0, 0]
        v_g_prev = 0
        s_sq_prev = 0
        s_xq_prev = 0
        v_xp_prev = 0
        v_x1_prev = 0
        t_gt_prev = 0
        t_op_prev = 0
        for c in range(NCH):
            par = c % 2
            if c + 1 < NCH:
                np_ = (c + 1) % 2
                q_utb[c + 1] = P.dma("sync", utb_dst(c + 1),
                                     uTv[:, :, (c + 1) * 256 + 3:(c + 1) * 256 + 259],
                                     waits=[S("tensor", utb_tguard[np_])], ch=5 + np_)
                q_yin[c + 1] = P.dma("gpsimd",
                                     yin_s[:, np_ * 2 * DI:(np_ + 1) * 2 * DI]
                                     .rearrange("q (a d) -> q a d", a=2),
                                     yacch[(c + 1) * 2:(c + 2) * 2, :, :]
                                     .rearrange("a q d -> q a d"),
                                     waits=[S("vector", yin_vguard[np_])], ch=3 + np_)
                q_sin[c + 1] = P.dma("gpsimd",
                                     sin_s[:, np_ * NH * HD:(np_ + 1) * NH * HD],
                                     sh[c + 1, :, :],
                                     waits=[S("vector", sin_vguard[np_])], ch=5 + np_)
                q_ur[c + 1] = P.dma("sync",
                                    ur_s[:, np_ * 2 * D:(np_ + 1) * 2 * D]
                                    .rearrange("p (a d) -> p a d", a=2),
                                    ures[(c + 1) * 256:(c + 2) * 256, :]
                                    .rearrange("(a p) d -> p a d", p=128),
                                    waits=[S("vector", ur_vguard[np_])], ch=7 + np_)

            for qt in range(2):
                last_sz = 0
                for eb in range(4):
                    ebp = eb % 2

                    def mm_z(e, qt=qt, eb=eb, par=par, ebp=ebp):
                        for k in range(KD):
                            inst = e.matmul(
                                pA[:, ebp * 512:ebp * 512 + 512],
                                utb_s[:, par * KD * 256 + k * 256 + qt * 128:
                                      par * KD * 256 + k * 256 + qt * 128 + 128],
                                winz_s[:, k * DI + eb * 512:k * DI + (eb + 1) * 512],
                                start=(k == 0), stop=(k == KD - 1))
                        return inst

                    t_z = P.stage("tensor", mm_z,
                                  waits=P.gwait(bank("A", ebp * 512)) +
                                  [Q("sync", q_utb[c], 5 + par), Q("sync", q_wzo)])
                    utb_tguard[par] = t_z
                    s_z = P.stage("scalar",
                                  lambda e, eb=eb, ebp=ebp: e.activation(
                                      sz_s[:, eb * 512:(eb + 1) * 512],
                                      pA[:, ebp * 512:ebp * 512 + 512], AF.Silu),
                                  waits=[S("tensor", t_z), S("vector", v_g_prev)])
                    P.gset(bank("A", ebp * 512), "scalar", s_z)
                    last_sz = s_z

                last_vy = 0
                for h in range(NH):
                    hp = h % 2
                    po = pB[:, hp * 512:hp * 512 + 128]
                    t_yo = P.stage("tensor",
                                   lambda e, h=h, qt=qt, c=c, po=po: e.matmul(
                                       po, Cb_s[:, c * CK + qt * 128:c * CK + qt * 128 + 128],
                                       prevb_s[:, h * 128:(h + 1) * 128],
                                       start=True, stop=True),
                                   waits=P.gwait(bank("B", hp * 512)) +
                                   [S("vector", prevb_v)])
                    v_y = P.stage("vector",
                                  lambda e, h=h, qt=qt, c=c, par=par, po=po:
                                  e.scalar_tensor_tensor(
                                      y_s[:, h * 128:(h + 1) * 128], po,
                                      eaT_s[:, (c * 2 + qt) * 16 + h:(c * 2 + qt) * 16 + h + 1],
                                      yin_s[:, par * 2 * DI + qt * DI + h * 128:
                                            par * 2 * DI + qt * DI + (h + 1) * 128],
                                      ALU.mult, ALU.add),
                                  waits=[S("tensor", t_yo),
                                         Q("gpsimd", q_yin[c], 3 + par),
                                         S("scalar", s_sq_prev)])
                    P.gset(bank("B", hp * 512), "vector", v_y)
                    last_vy = v_y
                if qt == 1:
                    yin_vguard[par] = last_vy

                v_g = P.stage("vector",
                              lambda e: e.tensor_tensor(g_s[:, :], y_s[:, :],
                                                        sz_s[:, :], ALU.mult),
                              waits=[S("vector", last_vy), S("scalar", last_sz),
                                     S("scalar", s_sq_prev),
                                     S("tensor", t_gt_prev)])
                v_g_prev = v_g
                s_sq = P.stage("scalar",
                               lambda e, qt=qt: e.activation(
                                   y_s[:, :], g_s[:, :], AF.Square,
                                   accum_out=sumsq_s[:, qt:qt + 1]),
                               waits=[S("vector", v_g)])
                s_sq_prev = s_sq
                last_gt = 0
                for et in range(16):
                    ep = et % 2
                    t_gt = P.stage("tensor",
                                   lambda e, et=et, ep=ep: e.transpose(
                                       pC[:, ep * 512:ep * 512 + 128],
                                       g_s[:, et * 128:(et + 1) * 128], idf_s[:, :]),
                                   waits=P.gwait(bank("C", ep * 512)) +
                                   [S("vector", v_g)])
                    v_gt = P.stage("vector",
                                   lambda e, et=et, ep=ep: e.tensor_scalar(
                                       gt_s[:, et * 128:(et + 1) * 128],
                                       pC[:, ep * 512:ep * 512 + 128],
                                       nw_s[:, et:et + 1], None, ALU.mult),
                                   waits=[S("tensor", t_gt), S("tensor", t_op_prev)])
                    P.gset(bank("C", ep * 512), "vector", v_gt)
                    last_gt = v_gt
                t_gt_prev = P.cnt["tensor"]

                def mm_op(e):
                    for db in range(2):
                        for k in range(16):
                            inst = e.matmul(
                                pD[:, db * 512:(db + 1) * 512],
                                gt_s[:, k * 128:(k + 1) * 128],
                                wout_s[:, k * 1024 + db * 512:k * 1024 + (db + 1) * 512],
                                start=(k == 0), stop=(k == 15))
                    return inst

                t_op = P.stage("tensor", mm_op,
                               waits=P.gwait(bank("D", 0), bank("D", 512)) +
                               [S("vector", last_gt), Q("sync", q_wzo)])
                t_op_prev = t_op

                s_rm1 = P.stage("scalar",
                                lambda e, qt=qt: e.activation(
                                    col_s[:, 0:1], sumsq_s[:, qt:qt + 1], AF.Ln,
                                    bias=ko_s[:, 0:1], scale=ko_s[:, 3:4]),
                                waits=[S("scalar", s_sq), S("vector", v_x1_prev)])
                s_rm2 = P.stage("scalar",
                                lambda e: e.activation(col_s[:, 1:2], col_s[:, 0:1],
                                                       AF.Exp, scale=ko_s[:, 2:3]),
                                waits=[S("scalar", s_rm1)])
                v_xp = P.stage("vector",
                               lambda e, qt=qt, par=par: e.scalar_tensor_tensor(
                                   xpre_s[:, :], pD[:, 0:1024], col_s[:, 1:2],
                                   ur_s[:, par * 2 * D + qt * D:par * 2 * D + (qt + 1) * D],
                                   ALU.mult, ALU.add, accum_out=col_s[:, 2:3]),
                               waits=[S("tensor", t_op), S("scalar", s_rm2),
                                      Q("sync", q_ur[c], 7 + par),
                                      S("scalar", s_xq_prev)])
                P.gset(bank("D", 0), "vector", v_xp)
                P.gset(bank("D", 512), "vector", v_xp)
                v_xp_prev = v_xp
                if qt == 1:
                    ur_vguard[par] = v_xp
                s_xq = P.stage("scalar",
                               lambda e: e.activation(sz_s[:, 0:1024], xpre_s[:, :],
                                                      AF.Square,
                                                      accum_out=col_s[:, 3:4]),
                               waits=[S("vector", v_xp)])
                s_xq_prev = s_xq
                P.stage("vector",
                        lambda e: e.tensor_scalar(col_s[:, 4:5], col_s[:, 2:3],
                                                  1.0 / D, None, ALU.mult),
                        waits=[S("scalar", s_xq)])
                P.stage("vector",
                        lambda e: e.tensor_scalar(col_s[:, 5:6], col_s[:, 3:4],
                                                  1.0 / D, EPS, ALU.mult, ALU.add))
                P.stage("vector",
                        lambda e: e.tensor_scalar(col_s[:, 6:7], col_s[:, 4:5],
                                                  col_s[:, 4:5], None, ALU.mult))
                v_c1 = P.stage("vector",
                               lambda e: e.tensor_tensor(col_s[:, 5:6], col_s[:, 5:6],
                                                         col_s[:, 6:7], ALU.subtract))
                s_l1 = P.stage("scalar",
                               lambda e: e.activation(col_s[:, 6:7], col_s[:, 5:6],
                                                      AF.Ln),
                               waits=[S("vector", v_c1)])
                s_l2 = P.stage("scalar",
                               lambda e: e.activation(col_s[:, 7:8], col_s[:, 6:7],
                                                      AF.Exp, scale=ko_s[:, 2:3]),
                               waits=[S("scalar", s_l1)])
                v_x1 = P.stage("vector",
                               lambda e, c=c, qt=qt: e.tensor_scalar(
                                   x1_s[:, (c * 2 + qt) * D:(c * 2 + qt + 1) * D],
                                   xpre_s[:, :], col_s[:, 4:5], col_s[:, 7:8],
                                   ALU.subtract, ALU.mult),
                               waits=[S("scalar", s_l2)])
                v_x1_prev = v_x1

            def mk_pu(c=c, par=par):
                def body(e):
                    for h in range(NH):
                        inst = e.scalar_tensor_tensor(
                            prev_s[:, h * 128:(h + 1) * 128],
                            prev_s[:, h * 128:(h + 1) * 128],
                            cdbc_s[:, c * NH + h:c * NH + h + 1],
                            sin_s[:, par * NH * HD + h * 128:par * NH * HD + (h + 1) * 128],
                            ALU.mult, ALU.add)
                    return inst
                return body

            v_pu = P.stage("vector", mk_pu(),
                           waits=[S("tensor", P.cnt["tensor"]),
                                  Q("gpsimd", q_sin[c], 5 + par)])
            sin_vguard[par] = v_pu
            prevb_v = P.stage("vector",
                              lambda e: e.tensor_copy(prevb_s[:, :], prev_s[:, :]),
                              waits=[S("vector", v_pu)])

    P.barrier()

    with contextlib.ExitStack() as scopeF:
        def sbf(name, shape, dt):
            return scopeF.enter_context(nc.sbuf_tensor(name, list(shape), dt))

        xf_s = sbf("xf_s", (128, KD * HALF), BF16)
        h_s = sbf("h_s", (128, KH * HALF), BF16)
        w13_s = sbf("w13_s", (128, 2 * 2 * KD * 256), BF16)
        w2b_s = sbf("w2b_s", (128, 2 * 4 * 1024), BF16)
        yf_s = sbf("yf_s", (128, 8 * D), F32)
        sl_s = sbf("sl_s", (128, 1024), F32)
        comb_s = sbf("comb_s", (128, 64), F32)
        sm_s = sbf("sm_s", (128, 8), F32)
        m8_s = sbf("m8_s", (128, 8), F32)
        wrk_s = sbf("wrk_s", (128, 8), F32)
        colf_s = sbf("colf_s", (128, 8), F32)
        xw_s = sbf("xw_s", (128, D), F32)
        x2_s = sbf("x2_s", (128, D), F32)

        def w13_dst(buf, mat):
            o = (buf * 2 + mat) * KD * 256
            return w13_s[:, o:o + KD * 256].rearrange("p (k t) -> p k t", k=KD)

        def w13_sl(buf, mat, k, lo, sz):
            o = (buf * 2 + mat) * KD * 256
            return w13_s[:, o + k * 256 + lo:o + k * 256 + lo + sz]

        def w2_dst(buf, nk):
            o = buf * 4 * 1024
            return w2b_s[:, o:o + nk * 1024].rearrange("p (k t) -> p k t", k=nk)

        w13_tguard = [0, 0]
        w2_tguard = [0, 0]
        vh_guard = 0
        s_si_prev = 0
        v_h_prev = 0
        s_g_prev = 0
        v_x2a_sguard = 0
        q_out_prev = 0
        xf_tguard = 0
        last_h_v = 0

        for hf in range(2):
            last_xf = 0
            trc = 0
            for tt in range(8):
                gtile = hf * 8 + tt
                for k in range(KD):
                    tb = trc % 2
                    t_tx = P.stage("tensor",
                                   lambda e, gtile=gtile, k=k, tb=tb: e.transpose(
                                       pA[:, tb * 512:tb * 512 + 64].bitcast(BF16),
                                       x1_s[:, gtile * D + k * 128:gtile * D + (k + 1) * 128],
                                       idb_s[:, :]),
                                   waits=P.gwait(bank("A", tb * 512)))
                    v_tx = P.stage("vector",
                                   lambda e, tt=tt, k=k, tb=tb: e.tensor_copy(
                                       xf_s[:, k * HALF + tt * 128:k * HALF + (tt + 1) * 128],
                                       pA[:, tb * 512:tb * 512 + 64].bitcast(BF16)),
                                   waits=[S("tensor", t_tx), S("tensor", xf_tguard)])
                    P.gset(bank("A", tb * 512), "vector", v_tx)
                    last_xf = v_tx
                    trc += 1

            for tt in range(8):
                def mm_g(e, tt=tt):
                    for k in range(KD):
                        inst = e.matmul(
                            pC[:, 0:8],
                            xf_s[:, k * HALF + tt * 128:k * HALF + (tt + 1) * 128],
                            gatet_s[:, k * 8:(k + 1) * 8],
                            start=(k == 0), stop=(k == KD - 1))
                    return inst

                t_g = P.stage("tensor", mm_g,
                              waits=P.gwait(bank("C", 0)) + [S("vector", last_xf)])
                v_g1 = P.stage("vector",
                               lambda e: e.reduce_max(colf_s[:, 0:1], pC[:, 0:8],
                                                      AXX, negate=True),
                               waits=[S("tensor", t_g), S("scalar", s_g_prev)])
                s_g1 = P.stage("scalar",
                               lambda e: e.activation(sm_s[:, :], pC[:, 0:8], AF.Exp,
                                                      bias=colf_s[:, 0:1],
                                                      accum_out=colf_s[:, 1:2]),
                               waits=[S("vector", v_g1)])
                s_g_prev = s_g1
                P.gset(bank("C", 0), "scalar", s_g1)
                P.stage("vector",
                        lambda e: e.reciprocal(colf_s[:, 2:3], colf_s[:, 1:2]),
                        waits=[S("scalar", s_g1)])
                P.stage("vector",
                        lambda e: e.tensor_scalar(sm_s[:, :], sm_s[:, :],
                                                  colf_s[:, 2:3], None, ALU.mult))
                P.stage("vector", lambda e: e.max(out=m8_s[:, :], in_=sm_s[:, :]))
                P.stage("vector", lambda e: e.memset(m8_s[:, 2:8], NEG))
                P.stage("vector",
                        lambda e: e.match_replace(out=wrk_s[:, :],
                                                  in_to_replace=m8_s[:, :],
                                                  in_values=sm_s[:, :],
                                                  imm_value=NEG))
                P.stage("vector",
                        lambda e: e.tensor_single_scalar(wrk_s[:, :], wrk_s[:, :],
                                                         NEG, ALU.is_equal))
                v_g2 = P.stage("vector",
                               lambda e, tt=tt: e.tensor_tensor(
                                   comb_s[:, tt * 8:(tt + 1) * 8],
                                   wrk_s[:, :], sm_s[:, :], ALU.mult))
            last_comb = P.cnt["vector"]

            v_yz = P.stage("vector", lambda e: e.memset(yf_s[:, :], 0.0))

            for ei in list(range(8)) + ["shared"]:
                nht = KH if ei != "shared" else KSH
                ngrp = (nht + 1) // 2

                def w13_load(g, buf, ei=ei, tg=None):
                    tgv = w13_tguard[buf] if tg is None else tg
                    a, b = (sw1v, sw3v) if ei == "shared" else (w1v[ei], w3v[ei])
                    q1 = P.dma("sync", w13_dst(buf, 0),
                               a[:, :, g * 256:g * 256 + 256],
                               waits=[S("tensor", tgv)], ch=1 + buf)
                    q3 = P.dma("sync", w13_dst(buf, 1),
                               b[:, :, g * 256:g * 256 + 256], ch=1 + buf)
                    return q3

                qld = [0] * ngrp
                qld[0] = w13_load(0, 0)
                if ngrp > 1:
                    qld[1] = w13_load(1, 1)
                pab = 0
                for g in range(ngrp):
                    for hi in range(2):
                        ht = g * 2 + hi
                        if ht >= nht:
                            break
                        if pab % 2 == 0:
                            pt1, pt3, n1, n3 = pA, pB, "A", "B"
                        else:
                            pt1, pt3, n1, n3 = pC, pD, "C", "D"

                        def mm_h(e, hi=hi, g=g, pt1=pt1, pt3=pt3):
                            buf = g % 2
                            for mat, pt in ((0, pt1), (1, pt3)):
                                for k in range(KD):
                                    for blk in range(2):
                                        inst = e.matmul(
                                            pt[:, blk * 512:blk * 512 + 512],
                                            w13_sl(buf, mat, k, hi * 128, 128),
                                            xf_s[:, k * HALF + blk * 512:k * HALF + (blk + 1) * 512],
                                            start=(k == 0), stop=(k == KD - 1),
                                            skip_group_check=True)
                            return inst

                        t_mh = P.stage("tensor", mm_h,
                                       waits=P.gwait(bank(n1, 0), bank(n1, 512),
                                                     bank(n3, 0), bank(n3, 512)) +
                                       [Q("sync", qld[g], 1 + g % 2),
                                        S("vector", last_xf)])
                        s_si = P.stage("scalar",
                                       lambda e, pt1=pt1: e.activation(
                                           sl_s[:, 0:1024], pt1[:, 0:1024], AF.Silu),
                                       waits=[S("tensor", t_mh),
                                              S("vector", v_h_prev)])
                        P.gset(bank(n1, 0), "scalar", s_si)
                        P.gset(bank(n1, 512), "scalar", s_si)
                        s_si_prev = s_si
                        v_h = P.stage("vector",
                                      lambda e, ht=ht, pt3=pt3: e.tensor_tensor(
                                          h_s[:, ht * HALF:(ht + 1) * HALF],
                                          sl_s[:, 0:1024], pt3[:, 0:1024], ALU.mult),
                                      waits=[S("scalar", s_si),
                                             S("tensor", vh_guard)])
                        P.gset(bank(n3, 0), "vector", v_h)
                        P.gset(bank(n3, 512), "vector", v_h)
                        v_h_prev = v_h
                        pab += 1
                    if g + 2 < ngrp:
                        qld[g + 2] = w13_load(g + 2, g % 2, tg=P.cnt["tensor"])
                w13_tguard[0] = P.cnt["tensor"]
                w13_tguard[1] = P.cnt["tensor"]
                last_h_v = v_h_prev

                kgroups = []
                k0 = 0
                while k0 < nht:
                    kgroups.append((k0, min(4, nht - k0)))
                    k0 += 4
                for th in range(2):
                    qw2 = [0] * len(kgroups)
                    qw2[0] = P.dma("sync", w2_dst(0, kgroups[0][1]),
                                   (sw2v if ei == "shared" else w2v[ei])
                                   [:, kgroups[0][0]:kgroups[0][0] + kgroups[0][1], :],
                                   waits=[S("tensor", w2_tguard[0])], ch=3)
                    if len(kgroups) > 1:
                        qw2[1] = P.dma("sync", w2_dst(1, kgroups[1][1]),
                                       (sw2v if ei == "shared" else w2v[ei])
                                       [:, kgroups[1][0]:kgroups[1][0] + kgroups[1][1], :],
                                       waits=[S("tensor", w2_tguard[1])], ch=4)
                    pts = (pA, pB, pC, pD)
                    for gi, (k0g, nk) in enumerate(kgroups):
                        buf = gi % 2

                        def mm_y(e, gi=gi, k0g=k0g, nk=nk, buf=buf, th=th, nht=nht):
                            for tt4 in range(4):
                                tt = th * 4 + tt4
                                pt = pts[tt4]
                                for ki in range(nk):
                                    k = k0g + ki
                                    st = (k == 0)
                                    sp = (k == nht - 1)
                                    e.matmul(pt[:, 0:512],
                                             h_s[:, k * HALF + tt * 128:k * HALF + (tt + 1) * 128],
                                             w2b_s[:, buf * 4096 + ki * 1024:buf * 4096 + ki * 1024 + 512],
                                             start=st, stop=sp,
                                             skip_group_check=True)
                                    inst = e.matmul(pt[:, 512:1024],
                                                    h_s[:, k * HALF + tt * 128:k * HALF + (tt + 1) * 128],
                                                    w2b_s[:, buf * 4096 + ki * 1024 + 512:buf * 4096 + (ki + 1) * 1024],
                                                    start=st, stop=sp,
                                                    skip_group_check=True)
                            return inst

                        gw = []
                        if gi == 0:
                            for pt_ in ("A", "B", "C", "D"):
                                gw += P.gwait(bank(pt_, 0), bank(pt_, 512))
                            gw += [S("vector", last_h_v)]
                        t_my = P.stage("tensor", mm_y,
                                       waits=gw + [Q("sync", qw2[gi], 3 + gi % 2)])
                        w2_tguard[buf] = t_my
                        vh_guard = t_my
                        if gi + 2 < len(kgroups):
                            qw2[gi + 2] = P.dma(
                                "sync", w2_dst(buf, kgroups[gi + 2][1]),
                                (sw2v if ei == "shared" else w2v[ei])
                                [:, kgroups[gi + 2][0]:kgroups[gi + 2][0] + kgroups[gi + 2][1], :],
                                waits=[S("tensor", t_my)], ch=3 + buf)
                    for tt4 in range(4):
                        tt = th * 4 + tt4
                        pt = pts[tt4]
                        tn = "ABCD"[tt4]
                        if ei == "shared":
                            v_yy = P.stage("vector",
                                           lambda e, tt=tt, pt=pt: e.scalar_tensor_tensor(
                                               yf_s[:, tt * D:(tt + 1) * D],
                                               pt[:, 0:1024], 1.0,
                                               yf_s[:, tt * D:(tt + 1) * D],
                                               ALU.mult, ALU.add),
                                           waits=[S("tensor", t_my),
                                                  S("vector", max(v_yz, last_comb))])
                        else:
                            v_yy = P.stage("vector",
                                           lambda e, tt=tt, pt=pt, ei=ei: e.scalar_tensor_tensor(
                                               yf_s[:, tt * D:(tt + 1) * D],
                                               pt[:, 0:1024],
                                               comb_s[:, tt * 8 + ei:tt * 8 + ei + 1],
                                               yf_s[:, tt * D:(tt + 1) * D],
                                               ALU.mult, ALU.add),
                                           waits=[S("tensor", t_my),
                                                  S("vector", max(v_yz, last_comb))])
                        P.gset(bank(tn, 0), "vector", v_yy)
                        P.gset(bank(tn, 512), "vector", v_yy)

            for tt in range(8):
                gtile = hf * 8 + tt
                v_x2a = P.stage("vector",
                                lambda e, tt=tt, gtile=gtile: e.scalar_tensor_tensor(
                                    xw_s[:, :], x1_s[:, gtile * D:(gtile + 1) * D],
                                    1.0, yf_s[:, tt * D:(tt + 1) * D],
                                    ALU.mult, ALU.add, accum_out=colf_s[:, 3:4]),
                                waits=[S("scalar", v_x2a_sguard)])
                s_x2 = P.stage("scalar",
                               lambda e: e.activation(sl_s[:, 0:1024], xw_s[:, :],
                                                      AF.Square,
                                                      accum_out=colf_s[:, 4:5]),
                               waits=[S("vector", v_x2a)])
                v_x2a_sguard = s_x2
                P.stage("vector",
                        lambda e: e.tensor_scalar(colf_s[:, 5:6], colf_s[:, 3:4],
                                                  1.0 / D, None, ALU.mult),
                        waits=[S("scalar", s_x2)])
                P.stage("vector",
                        lambda e: e.tensor_scalar(colf_s[:, 6:7], colf_s[:, 4:5],
                                                  1.0 / D, EPS, ALU.mult, ALU.add))
                P.stage("vector",
                        lambda e: e.tensor_scalar(colf_s[:, 7:8], colf_s[:, 5:6],
                                                  colf_s[:, 5:6], None, ALU.mult))
                v_x2b = P.stage("vector",
                                lambda e: e.tensor_tensor(colf_s[:, 6:7], colf_s[:, 6:7],
                                                          colf_s[:, 7:8], ALU.subtract))
                s_x3 = P.stage("scalar",
                               lambda e: e.activation(colf_s[:, 7:8], colf_s[:, 6:7],
                                                      AF.Ln),
                               waits=[S("vector", v_x2b)])
                s_x4 = P.stage("scalar",
                               lambda e: e.activation(colf_s[:, 6:7], colf_s[:, 7:8],
                                                      AF.Exp, scale=ko_s[:, 2:3]),
                               waits=[S("scalar", s_x3)])
                v_x2c = P.stage("vector",
                                lambda e: e.tensor_scalar(
                                    x2_s[:, :], xw_s[:, :], colf_s[:, 5:6],
                                    colf_s[:, 6:7], ALU.subtract, ALU.mult),
                                waits=[S("scalar", s_x4),
                                       Q("gpsimd", q_out_prev, 7)])
                q_out_prev = P.dma("gpsimd", outh[gtile * 128:(gtile + 1) * 128, :],
                                   x2_s[:, :], waits=[S("vector", v_x2c)], ch=7)
            xf_tguard = P.cnt["tensor"]

        P.wait("gpsimd", waits=[("q", k, v) for k, v in P.qcnt.items()
                                if k[0] == "gpsimd"])

    stack.close()

    emstack = contextlib.ExitStack()
    sems = {e: emstack.enter_context(nc.semaphore("se_" + e)) for e in ENGS}
    qsems = {k: emstack.enter_context(nc.semaphore("qu_%s_%d" % k))
             for k in sorted(P.qcnt.keys())}
    with nc.Block() as block:

        @block.sync
        def _(eng):
            for f in P.ops["sync"]:
                f(eng, sems, qsems)

        @block.scalar
        def _(eng):
            for f in P.ops["scalar"]:
                f(eng, sems, qsems)

        @block.vector
        def _(eng):
            for f in P.ops["vector"]:
                f(eng, sems, qsems)

        @block.tensor
        def _(eng):
            for f in P.ops["tensor"]:
                f(eng, sems, qsems)

        @block.gpsimd
        def _(eng):
            for f in P.ops["gpsimd"]:
                f(eng, sems, qsems)

    emstack.close()
    return nc


def kernel(L0_attn_u, L0_attn_in_proj_w, L0_attn_conv_w, L0_attn_conv_b,
           L0_attn_dt_bias, L0_attn_A_log, L0_attn_Dp, L0_attn_norm_w,
           L0_attn_out_proj_w, L0_ffn_gate_w, L0_ffn_w1, L0_ffn_w2, L0_ffn_w3,
           L0_ffn_sw1, L0_ffn_sw2, L0_ffn_sw3):
    u = np.asarray(L0_attn_u, np.float32)
    Win = np.asarray(L0_attn_in_proj_w, np.float32)
    WinT = np.ascontiguousarray(Win.T)
    mk = np.zeros((128, 512), np.float32)
    qq = np.arange(256)
    for k in range(128):
        mk[k, 0:256] = (qq >= k)
        mk[k, 256:512] = (qq >= 128 + k)
    shared = {
        "winx": np.ascontiguousarray(WinT[:, 2048:4240].astype(BF)),
        "winz": np.ascontiguousarray(WinT[:, 0:2048].astype(BF)),
        "woutT": np.ascontiguousarray(np.asarray(L0_attn_out_proj_w, np.float32).T.astype(BF)),
        "gateT": np.ascontiguousarray(np.asarray(L0_ffn_gate_w, np.float32).T.astype(BF)),
        "cwh": np.ascontiguousarray(
            np.asarray(L0_attn_conv_w, np.float32).reshape(17, 128, 4)
            .transpose(1, 2, 0).reshape(128, 68)),
        "cbh": np.ascontiguousarray(np.asarray(L0_attn_conv_b, np.float32).reshape(17, 128).T),
        "nwh": np.ascontiguousarray(np.asarray(L0_attn_norm_w, np.float32).reshape(16, 128).T),
        "dtbh": np.asarray(L0_attn_dt_bias, np.float32).reshape(16, 1),
        "alogh": np.asarray(L0_attn_A_log, np.float32).reshape(16, 1),
        "dprh": np.asarray(L0_attn_Dp, np.float32).reshape(1, 16),
        "identfh": np.eye(128, dtype=np.float32),
        "identbh": np.eye(128, dtype=BF),
        "onesfh": np.ones((1, 4096), np.float32),
        "onesch": np.ones((1, 128), np.float32),
        "konsth": np.stack([np.full(128, EPS, np.float32),
                            np.full(128, -1.0, np.float32),
                            np.full(128, -0.5, np.float32),
                            np.full(128, 1.0 / DI, np.float32)], axis=1),
        "maskh": mk,
        "w1h": np.ascontiguousarray(np.asarray(L0_ffn_w1, np.float32).astype(BF)),
        "w2h": np.ascontiguousarray(np.asarray(L0_ffn_w2, np.float32).astype(BF)),
        "w3h": np.ascontiguousarray(np.asarray(L0_ffn_w3, np.float32).astype(BF)),
        "sw1h": np.ascontiguousarray(np.asarray(L0_ffn_sw1, np.float32).astype(BF)),
        "sw2h": np.ascontiguousarray(np.asarray(L0_ffn_sw2, np.float32).astype(BF)),
        "sw3h": np.ascontiguousarray(np.asarray(L0_ffn_sw3, np.float32).astype(BF)),
    }

    in_maps = []
    for core in range(8):
        b_, q_ = core // 4, core % 4
        t0 = q_ * TOK
        useg = u[b_, t0:t0 + TOK]
        halo = u[b_, t0 - 3:t0] if q_ else np.zeros((3, D), np.float32)
        uh = np.concatenate([halo, useg], axis=0)
        pm = np.zeros((4, 4), np.float32)
        mv = np.zeros((4, 1), np.float32)
        for j in range(4):
            if j < q_:
                mv[j, 0] = 1.0
            for i in range(4):
                if j < i < q_:
                    pm[i, j] = 1.0
        m = dict(shared)
        m["uTb"] = np.ascontiguousarray(uh.T.astype(BF))
        m["ures"] = np.ascontiguousarray(useg)
        m["pmath"] = pm
        m["mvech"] = mv
        in_maps.append(m)

    nc = build_program()
    res = run_bass_kernel_spmd(nc, in_maps, core_ids=list(range(8)))
    out = np.zeros((2, 8192, D), np.float32)
    for core in range(8):
        b_, q_ = core // 4, core % 4
        out[b_, q_ * TOK:(q_ + 1) * TOK] = res.results[core]["out"]
    return out


# version 04
# speedup vs baseline: 1.0110x; 1.0110x over previous
import contextlib
import numpy as np
import ml_dtypes
import concourse.bass as bass
import concourse.mybir as mybir
from concourse.bass_utils import run_bass_kernel_spmd

F32 = mybir.dt.float32
BF16 = mybir.dt.bfloat16
AF = mybir.ActivationFunctionType
ALU = mybir.AluOpType
AXX = mybir.AxisListType.X
BF = ml_dtypes.bfloat16

TOK = 2048
CK = 256
NCH = 8
NH = 16
HD = 128
DS = 64
D = 1024
KD = 8
DI = 2048
XBCD = 2192
NEG = -1.0e30
EPS = 1e-5
EXPH = 2816
KH = 22
SHH = 1536
KSH = 12
HALF = 1024

ENGS = ("sync", "scalar", "vector", "tensor", "gpsimd")


class Plan:

    def __init__(self):
        self.ops = {e: [] for e in ENGS}
        self.cnt = {e: 0 for e in ENGS}
        self.qcnt = {}
        self.guards = {}

    def gwait(self, *keys):
        out = []
        for k in keys:
            out += self.guards.get(k, [])
        return out

    def gset(self, key, eng, val, kind="s"):
        self.guards[key] = [(kind, eng, val)]

    def _mk(self, W, body, eng):
        def run(e, sems, qsems):
            for item in W:
                kind, en, v = item
                if v > 0:
                    e.wait_ge((sems if kind == "s" else qsems)[en], v)
            if body is not None:
                body(e).then_inc((sems if eng[0] == "s" else qsems)[eng[1]],
                                 1 if eng[0] == "s" else 16)
        return run

    def qchan(self, eng, ch):
        return self.qcnt.get((eng, ch), 0)

    def stage(self, eng, body, waits=()):
        prev = self.cnt[eng]
        self.cnt[eng] += 1
        tgt = self.cnt[eng]
        W = tuple(waits)
        if eng != "tensor" and prev > 0:
            W = W + (("s", eng, prev),)
        self.ops[eng].append(self._mk(W, body, ("s", eng)))
        return tgt

    def dma(self, eng, out, in_, waits=(), ch=0):
        key = (eng, ch)
        prev = self.qcnt.get(key, 0)
        self.qcnt[key] = prev + 16
        tgt = self.qcnt[key]

        def body(e, out=out, in_=in_):
            return e.dma_start(out=out, in_=in_)

        W = tuple(waits) + ((("q", key, prev),) if prev > 0 else ())
        self.ops[eng].append(self._mk(W, body, ("q", key)))
        return tgt

    def wait(self, eng, waits=()):
        self.ops[eng].append(self._mk(tuple(waits), None, ("s", eng)))

    def barrier(self):
        snap = [("s", x, self.cnt[x]) for x in ENGS]
        qsnap = [("q", k, v) for k, v in self.qcnt.items()]
        for e in ENGS:
            self.wait(e, waits=[w for w in snap if w[1] != e] + qsnap)


def S(eng, v):
    return ("s", eng, v)


def Q(eng, v, ch=0):
    return ("q", (eng, ch), v)


def build_program():
    nc = bass.Bass("TRN2", target_bir_lowering=False)
    P = Plan()

    def din(name, shape, dt):
        return nc.dram_tensor(name, list(shape), dt, kind="ExternalInput").ap()

    uTb = din("uTb", (D, TOK + 3), BF16)
    ures = din("ures", (TOK, D), F32)
    winx = din("winx", (D, XBCD), BF16)
    winz = din("winz", (D, DI), BF16)
    woutT = din("woutT", (DI, D), BF16)
    gateT = din("gateT", (D, 8), BF16)
    cwh = din("cwh", (128, 68), F32)
    cbh = din("cbh", (128, 17), F32)
    nwh = din("nwh", (128, 16), F32)
    dtbh = din("dtbh", (16, 1), F32)
    alogh = din("alogh", (16, 1), F32)
    dprh = din("dprh", (1, 16), F32)
    maskh = din("maskh", (128, 512), F32)
    identfh = din("identfh", (128, 128), F32)
    identbh = din("identbh", (128, 128), BF16)
    onesfh = din("onesfh", (1, 4096), F32)
    onesch = din("onesch", (1, 128), F32)
    pmath = din("pmath", (4, 4), F32)
    mvech = din("mvech", (4, 1), F32)
    konsth = din("konsth", (128, 4), F32)
    w1h = din("w1h", (8, D, EXPH), BF16)
    w2h = din("w2h", (8, EXPH, D), BF16)
    w3h = din("w3h", (8, D, EXPH), BF16)
    sw1h = din("sw1h", (D, SHH), BF16)
    sw2h = din("sw2h", (SHH, D), BF16)
    sw3h = din("sw3h", (D, SHH), BF16)

    outh = nc.dram_tensor("out", [TOK, D], F32, kind="ExternalOutput").ap()

    yacch = nc.dram_tensor("yacch", [16, 128, DI], BF16).ap()
    sh = nc.dram_tensor("sh", [NCH, DS, NH * HD], BF16).ap()
    ccin = nc.dram_tensor("ccin", [65, 2048], F32).ap()
    ccout = nc.dram_tensor("ccout", [260, 2048], F32).ap()

    w1v = w1h.rearrange("e (k p) h -> e p k h", p=128)
    w3v = w3h.rearrange("e (k p) h -> e p k h", p=128)
    w2v = w2h.rearrange("e (k p) d -> e p k d", p=128)
    sw1v = sw1h.rearrange("(k p) h -> p k h", p=128)
    sw3v = sw3h.rearrange("(k p) h -> p k h", p=128)
    sw2v = sw2h.rearrange("(k p) d -> p k d", p=128)
    uTv = uTb.rearrange("(k p) t -> p k t", p=128)
    winxv = winx.rearrange("(k p) c -> p k c", p=128)
    winzv = winz.rearrange("(k p) c -> p k c", p=128)
    woutv = woutT.rearrange("(k p) d -> p k d", p=128)
    gatev = gateT.rearrange("(k p) e -> p k e", p=128)

    stack = contextlib.ExitStack()

    def sb(name, shape, dt):
        return stack.enter_context(nc.sbuf_tensor(name, list(shape), dt))

    cw_s = sb("cw_s", (128, 68), F32)
    cb_s = sb("cb_s", (128, 17), F32)
    nw_s = sb("nw_s", (128, 16), F32)
    dtb_s = sb("dtb_s", (16, 1), F32)
    A_s = sb("A_s", (16, 1), F32)
    tA_s = sb("tA_s", (16, 1), F32)
    dpr_s = sb("dpr_s", (1, 16), F32)
    dpbc_s = sb("dpbc_s", (128, 16), F32)
    mask_s = sb("mask_s", (128, 512), F32)
    idf_s = sb("idf_s", (128, 128), F32)
    idb_s = sb("idb_s", (128, 128), BF16)
    onec_s = sb("onec_s", (1, 128), F32)
    gatet_s = sb("gatet_s", (128, 64), BF16)
    x1_s = sb("x1_s", (128, 16 * D), BF16)
    prev_s = sb("prev_s", (DS, NH * HD), F32)
    prevb_s = sb("prevb_s", (DS, NH * HD), BF16)
    cdbc_s = sb("cdbc_s", (DS, NCH * NH), F32)
    eaT_s = sb("eaT_s", (128, 16 * NH), F32)
    alacc_s = sb("alacc_s", (16, 1), F32)
    Cb_s = sb("Cb_s", (DS, NCH * CK), BF16)
    sumsq_s = sb("sumsq_s", (128, 2), F32)
    pmat_s = sb("pmat_s", (4, 4), F32)
    ko_s = sb("ko_s", (128, 4), F32)
    mvec_s = sb("mvec_s", (4, 1), F32)

    pA = stack.enter_context(nc.psum_tensor("pA", [128, 1024], F32))
    pB = stack.enter_context(nc.psum_tensor("pB", [128, 1024], F32))
    pC = stack.enter_context(nc.psum_tensor("pC", [128, 1024], F32))
    pD = stack.enter_context(nc.psum_tensor("pD", [128, 1024], F32))
    PT = {"A": pA, "B": pB, "C": pC, "D": pD}

    def bank(t, lo):
        return (t, lo // 512)

    for dst, src in ((cw_s, cwh), (cb_s, cbh), (nw_s, nwh), (dtb_s, dtbh),
                     (mask_s, maskh), (idf_s, identfh), (idb_s, identbh),
                     (onec_s, onesch), (pmat_s, pmath), (mvec_s, mvech),
                     (tA_s, alogh), (dpr_s, dprh), (ko_s, konsth)):
        P.dma("sync", dst[:, :], src[:, :])
    P.dma("sync", gatet_s[:, :].rearrange("p (k e) -> p k e", k=KD), gatev[:, :, :])
    q_init = P.qchan("sync", 0)

    sA1 = P.stage("scalar",
                  lambda e: e.activation(A_s[:, :], tA_s[:, :], AF.Exp),
                  waits=[Q("sync", q_init)])
    vA1 = P.stage("vector",
                  lambda e: e.tensor_scalar(A_s[:, :], A_s[:, :], -1.0, None, ALU.mult),
                  waits=[S("scalar", sA1)])
    tDp = P.stage("tensor",
                  lambda e: e.matmul(pC[0:128, 0:16], onec_s[0:1, 0:128],
                                     dpr_s[0:1, 0:16], start=True, stop=True),
                  waits=[Q("sync", q_init)])
    vDp = P.stage("vector", lambda e: e.tensor_copy(dpbc_s[:, :], pC[0:128, 0:16]),
                  waits=[S("tensor", tDp)])
    P.gset(bank("C", 0), "vector", vDp)
    vz_al = P.stage("vector", lambda e: e.memset(alacc_s[:, :], 0.0))
    vz_pr = P.stage("vector", lambda e: e.memset(prev_s[:, :], 0.0))

    with contextlib.ExitStack() as scopeA:
        def sba(name, shape, dt):
            return scopeA.enter_context(nc.sbuf_tensor(name, list(shape), dt))

        winx_s = sba("winx_s", (128, KD * XBCD), BF16)
        uta_s = sba("uta_s", (128, 2 * KD * 259), BF16)
        lh2_s = sba("lh2_s", (2, 4096), F32)
        rh2_s = sba("rh2_s", (2, 4096), F32)
        acc_s = sba("acc_s", (128, 512), F32)
        xcv_s = sba("xcv_s", (128, 512), F32)
        bc_s = sba("bc_s", (128, 256), F32)
        bt_s = sba("bt_s", (128, 128), F32)
        bb_s = sba("bb_s", (DS, 256), BF16)
        xt_s = sba("xt_s", (128, 2 * DI), BF16)
        t16_s = sba("t16_s", (16, 256), F32)
        dtv_s = sba("dtv_s", (16, 256), F32)
        a_s = sba("a_s", (16, 256), F32)
        acum_s = sba("acum_s", (16, 256), F32)
        nacum_s = sba("nacum_s", (16, 256), F32)
        ws_s = sba("ws_s", (16, 256), F32)
        zz16_s = sba("zz16_s", (16, 256), F32)
        dtvT_s = sba("dtvT_s", (128, 32), F32)
        wsT_s = sba("wsT_s", (128, 32), F32)
        sc_s = sba("sc_s", (128, 512), F32)
        d0_s = sba("d0_s", (128, 2 * 768), F32)
        e_s = sba("e_s", (128, 2 * 768), F32)
        mt_s = sba("mt_s", (128, 4 * 384), BF16)
        ya_s = sba("ya_s", (128, 2 * DI), BF16)
        ss_s = sba("ss_s", (DS, NH * HD), BF16)
        wbt_s = sba("wbt_s", (128, 128), BF16)
        cdr_s = sba("cdr_s", (1, 16), F32)
        alrow_s = sba("alrow_s", (1, 2048), F32)

        P.dma("sync", winx_s[:, :].rearrange("p (k c) -> p k c", k=KD), winxv[:, :, :])
        P.dma("sync", lh2_s[0:1, :], onesfh[:, :])
        P.dma("sync", rh2_s[1:2, :], onesfh[:, :])
        q_wx = P.qchan("sync", 0)
        vz16 = P.stage("vector", lambda e: e.memset(zz16_s[:, :], 0.0))
        P.stage("vector", lambda e: e.memset(alrow_s[:, :], 0.0))

        def uta_dst(c):
            o = (c % 2) * (KD * 259)
            return uta_s[:, o:o + KD * 259].rearrange("p (k t) -> p k t", k=KD)

        q_uta = [0] * (NCH + 1)
        q_uta[0] = P.dma("sync", uta_dst(0), uTv[:, :, 0:259], ch=5)

        uta_tguard = [0, 0]
        ya_qd = [0] * NCH
        ss_qd = [0] * NCH
        pack_tguard = 0
        xt_tguard = 0
        bb_tguard = 0
        sm_tr_guard = 0
        prev_cdr_q = 0
        prev_pk1_q = 0
        v_acum_readers = 0

        for c in range(NCH):
            off = (c % 2) * (KD * 259)
            if c + 1 < NCH:
                q_uta[c + 1] = P.dma("sync", uta_dst(c + 1),
                                     uTv[:, :, (c + 1) * 256:(c + 1) * 256 + 259],
                                     waits=[S("tensor", uta_tguard[(c + 1) % 2])],
                                     ch=5 + (c + 1) % 2)

            trc = 0
            ssl_guard = [0, 0]
            xcv_tguard = [0, 0]
            last_bc_v = 0
            last_tmm = 0
            for ct in range(17):
                half = ct % 2

                def mm_ip(e, ct=ct, off=off, half=half):
                    for k in range(KD):
                        inst = e.matmul(
                            pA[:, half * 512:half * 512 + 259],
                            winx_s[:, k * XBCD + ct * 128:k * XBCD + (ct + 1) * 128],
                            uta_s[:, off + k * 259:off + (k + 1) * 259],
                            start=(k == 0), stop=(k == KD - 1))
                    return inst

                tmm = P.stage("tensor", mm_ip,
                              waits=P.gwait(bank("A", half * 512)) +
                              [Q("sync", q_uta[c], 5 + c % 2), Q("sync", q_wx)])
                last_tmm = tmm
                uta_tguard[c % 2] = tmm

                ps = pA[:, half * 512:half * 512 + 259]
                ac = acc_s[:, half * 256:half * 256 + 256]
                P.stage("vector",
                        lambda e, ps=ps, ac=ac, ct=ct: e.tensor_scalar(
                            ac, ps[:, 0:256], cw_s[:, ct:ct + 1],
                            cb_s[:, ct:ct + 1], ALU.mult, ALU.add),
                        waits=[S("tensor", tmm), S("scalar", ssl_guard[half])])
                for w in (1, 2, 3):
                    vconv = P.stage("vector",
                                    lambda e, ps=ps, ac=ac, ct=ct, w=w:
                                    e.scalar_tensor_tensor(
                                        ac, ps[:, w:w + 256],
                                        cw_s[:, w * 17 + ct:w * 17 + ct + 1],
                                        ac, ALU.mult, ALU.add))
                P.gset(bank("A", half * 512), "vector", vconv)

                if ct < 16:
                    ssl = P.stage("scalar",
                                  lambda e, half=half: e.activation(
                                      xcv_s[:, half * 256:half * 256 + 256],
                                      acc_s[:, half * 256:half * 256 + 256], AF.Silu),
                                  waits=[S("vector", vconv),
                                         S("tensor", xcv_tguard[half])])
                    ssl_guard[half] = ssl
                    for qt in range(2):
                        tb = trc % 2
                        ttr = P.stage("tensor",
                                      lambda e, half=half, qt=qt, tb=tb: e.transpose(
                                          pB[:, tb * 512:tb * 512 + 128],
                                          xcv_s[:, half * 256 + qt * 128:half * 256 + qt * 128 + 128],
                                          idf_s[:, :]),
                                      waits=P.gwait(bank("B", tb * 512)) +
                                      [S("scalar", ssl)])
                        vtr = P.stage("vector",
                                      lambda e, qt=qt, ct=ct, tb=tb: e.tensor_copy(
                                          xt_s[:, qt * DI + ct * 128:qt * DI + (ct + 1) * 128],
                                          pB[:, tb * 512:tb * 512 + 128]),
                                      waits=[S("tensor", ttr), S("tensor", xt_tguard)])
                        P.gset(bank("B", tb * 512), "vector", vtr)
                        xcv_tguard[half] = ttr
                        trc += 1
                else:
                    ssl = P.stage("scalar",
                                  lambda e: e.activation(bc_s[:, :],
                                                         acc_s[:, half * 256:half * 256 + 256],
                                                         AF.Silu),
                                  waits=[S("vector", vconv), S("tensor", bb_tguard)])
                    ssl_guard[half] = ssl
                    vbc = P.stage("vector",
                                  lambda e: e.tensor_copy(bb_s[:, :], bc_s[0:64, :]),
                                  waits=[S("scalar", ssl), S("tensor", bb_tguard)])
                    vcc = P.stage("vector",
                                  lambda e, c=c: e.tensor_copy(
                                      Cb_s[:, c * CK:(c + 1) * CK], bc_s[64:128, :]),
                                  waits=[S("scalar", ssl)])
                    for qt in range(2):
                        tb = trc % 2
                        ttr = P.stage("tensor",
                                      lambda e, qt=qt, tb=tb: e.transpose(
                                          pB[0:128, tb * 512:tb * 512 + 64],
                                          bc_s[0:64, qt * 128:qt * 128 + 128],
                                          idf_s[0:64, 0:64]),
                                      waits=P.gwait(bank("B", tb * 512)) +
                                      [S("scalar", ssl)])
                        vtr = P.stage("vector",
                                      lambda e, qt=qt, tb=tb: e.tensor_copy(
                                          bt_s[:, qt * 64:qt * 64 + 64],
                                          pB[0:128, tb * 512:tb * 512 + 64]),
                                      waits=[S("tensor", ttr)])
                        P.gset(bank("B", tb * 512), "vector", vtr)
                        bb_tguard = ttr
                        trc += 1
                    last_bc_v = max(vbc, vcc)

            def mm_dt(e, off=off):
                for k in range(KD):
                    inst = e.matmul(
                        pA[0:16, 512:512 + 256],
                        winx_s[:, k * XBCD + 2176:k * XBCD + 2192],
                        uta_s[:, off + k * 259 + 3:off + (k + 1) * 259],
                        start=(k == 0), stop=(k == KD - 1))
                return inst

            tdt = P.stage("tensor", mm_dt, waits=P.gwait(bank("A", 512)))
            uta_tguard[c % 2] = tdt

            s_sp1 = P.stage("scalar",
                            lambda e: e.activation(t16_s[:, :], pA[0:16, 512:512 + 256],
                                                   AF.Exp, bias=dtb_s[:, 0:1]),
                            waits=[S("tensor", tdt), Q("gpsimd", prev_cdr_q)])
            P.gset(bank("A", 512), "scalar", s_sp1)
            v_sp = P.stage("vector",
                           lambda e: e.tensor_scalar(t16_s[:, :], t16_s[:, :],
                                                     1.0, None, ALU.add),
                           waits=[S("scalar", s_sp1)])
            s_sp2 = P.stage("scalar",
                            lambda e: e.activation(dtv_s[:, :], t16_s[:, :], AF.Ln),
                            waits=[S("vector", v_sp), S("tensor", sm_tr_guard)])
            v_a = P.stage("vector",
                          lambda e: e.tensor_scalar(a_s[:, :], dtv_s[:, :],
                                                    A_s[:, 0:1], None, ALU.mult),
                          waits=[S("scalar", s_sp2), S("vector", vA1)])
            v_cum = P.stage("vector",
                            lambda e: e.tensor_tensor_scan(acum_s[:, :], a_s[:, :],
                                                           zz16_s[:, :], 0.0,
                                                           ALU.add, ALU.add),
                            waits=[S("vector", vz16), S("tensor", sm_tr_guard),
                                   Q("gpsimd", v_acum_readers)])
            v_nac = P.stage("vector",
                            lambda e: e.tensor_scalar(nacum_s[:, :], acum_s[:, :],
                                                      -1.0, None, ALU.mult),
                            waits=[Q("gpsimd", prev_pk1_q)])
            s_ws = P.stage("scalar",
                           lambda e: e.activation(ws_s[:, :], acum_s[:, :], AF.Exp,
                                                  bias=acum_s[:, 255:256],
                                                  scale=ko_s[0:16, 1:2]),
                           waits=[S("vector", v_cum), S("tensor", sm_tr_guard)])
            v_ws = P.stage("vector",
                           lambda e: e.tensor_tensor(ws_s[:, :], ws_s[:, :],
                                                     dtv_s[:, :], ALU.mult),
                           waits=[S("scalar", s_ws)])
            s_cd = P.stage("scalar",
                           lambda e: e.activation(t16_s[:, 0:1], acum_s[:, 255:256],
                                                  AF.Exp),
                           waits=[S("vector", v_cum), S("vector", v_sp)])
            v_al = P.stage("vector",
                           lambda e: e.tensor_tensor(alacc_s[:, :], alacc_s[:, :],
                                                     acum_s[:, 255:256], ALU.add),
                           waits=[S("vector", vz_al)])
            q_cdr = P.dma("gpsimd", cdr_s[0:1, :], t16_s[:, 0:1],
                          waits=[S("scalar", s_cd)])
            prev_cdr_q = q_cdr
            t_cd = P.stage("tensor",
                           lambda e: e.matmul(pC[0:64, 512:528], onec_s[0:1, 0:64],
                                              cdr_s[0:1, 0:16], start=True, stop=True),
                           waits=P.gwait(bank("C", 512)) + [Q("gpsimd", q_cdr)])
            v_cd = P.stage("vector",
                           lambda e, c=c: e.tensor_copy(cdbc_s[:, c * NH:(c + 1) * NH],
                                                        pC[0:64, 512:528]),
                           waits=[S("tensor", t_cd)])
            P.gset(bank("C", 512), "vector", v_cd)

            q_pk1 = P.dma("gpsimd",
                          lh2_s[1:2, :].rearrange("o (h t) -> o h t", h=16),
                          nacum_s[:, :],
                          waits=[S("vector", v_nac), S("tensor", pack_tguard)])
            q_pk2 = P.dma("gpsimd",
                          rh2_s[0:1, :].rearrange("o (h t) -> o h t", h=16),
                          acum_s[:, :],
                          waits=[S("vector", v_cum), S("tensor", pack_tguard)])
            q_pack = q_pk2
            v_acum_readers = q_pk2
            prev_pk1_q = q_pk1

            sm_done = {}
            for name, srcT in (("dtv", dtv_s), ("ws", ws_s), ("acum", acum_s)):
                for qt in range(2):
                    tb = trc % 2
                    wdeps = {"dtv": [S("scalar", s_sp2)],
                             "ws": [S("vector", v_ws)],
                             "acum": [S("vector", v_cum)]}[name]
                    ttr = P.stage("tensor",
                                  lambda e, srcT=srcT, qt=qt, tb=tb: e.transpose(
                                      pB[0:128, tb * 512 + 256:tb * 512 + 256 + 16],
                                      srcT[:, qt * 128:qt * 128 + 128],
                                      idf_s[0:16, 0:16]),
                                  waits=P.gwait(bank("B", tb * 512)) + wdeps)
                    sm_tr_guard = ttr
                    if name == "acum":
                        ev = P.stage("scalar",
                                     lambda e, c=c, qt=qt, tb=tb: e.activation(
                                         eaT_s[:, (c * 2 + qt) * 16:(c * 2 + qt) * 16 + 16],
                                         pB[0:128, tb * 512 + 256:tb * 512 + 272],
                                         AF.Exp),
                                     waits=[S("tensor", ttr)])
                        P.gset(bank("B", tb * 512), "scalar", ev)
                    else:
                        dst = (dtvT_s if name == "dtv" else wsT_s)
                        ev = P.stage("vector",
                                     lambda e, dst=dst, qt=qt, tb=tb: e.tensor_copy(
                                         dst[:, qt * 16:qt * 16 + 16],
                                         pB[0:128, tb * 512 + 256:tb * 512 + 272]),
                                     waits=[S("tensor", ttr)])
                        P.gset(bank("B", tb * 512), "vector", ev)
                    sm_done[(name, qt)] = ev
                    trc += 1

            def mm_sc(e, c=c):
                e.matmul(pC[:, 0:256], bb_s[:, 0:128],
                         Cb_s[:, c * CK:(c + 1) * CK], start=True, stop=True)
                return e.matmul(pC[:, 256:512], bb_s[:, 128:256],
                                Cb_s[:, c * CK:(c + 1) * CK], start=True, stop=True)

            t_sc = P.stage("tensor", mm_sc,
                           waits=P.gwait(bank("C", 0)) + [S("vector", last_bc_v)])
            bb_tguard = t_sc
            v_sc = P.stage("vector",
                           lambda e: e.tensor_tensor(sc_s[:, 0:512], pC[:, 0:512],
                                                     mask_s[:, 0:512], ALU.mult),
                           waits=[S("tensor", t_sc)])
            P.gset(bank("C", 0), "vector", v_sc)

            mt_tguard = [0, 0, 0, 0]
            sexp_guard = [0, 0]
            vd0_sguard = [0, 0]
            last_vya = 0
            last_vst = 0
            for pr in range(NH // 2):
                h0 = 2 * pr
                pp = pr % 2
                pt = pC if pp == 0 else pD
                tn = "C" if pp == 0 else "D"
                ptv = pt[:, :].rearrange("p (b x) -> p b x", x=512)[:, :, 0:384]

                def mm_diff(e, h0=h0, pt=pt):
                    for hh in range(2):
                        h = h0 + hh
                        e.matmul(pt[:, hh * 512:hh * 512 + 256],
                                 lh2_s[0:2, h * 256:h * 256 + 128],
                                 rh2_s[0:2, h * 256:h * 256 + 256],
                                 start=True, stop=True)
                        inst = e.matmul(pt[:, hh * 512 + 256:hh * 512 + 384],
                                        lh2_s[0:2, h * 256 + 128:h * 256 + 256],
                                        rh2_s[0:2, h * 256 + 128:h * 256 + 256],
                                        start=True, stop=True)
                    return inst

                t_df = P.stage("tensor", mm_diff,
                               waits=P.gwait(bank(tn, 0), bank(tn, 512)) +
                               [Q("gpsimd", q_pack), Q("sync", q_wx)])
                pack_tguard = t_df
                v_d0 = P.stage("vector",
                               lambda e, ptv=ptv, pp=pp: e.tensor_scalar(
                                   d0_s[:, pp * 768:pp * 768 + 768]
                                   .rearrange("p (b x) -> p b x", x=384),
                                   ptv, 0.0, None, ALU.min),
                               waits=[S("tensor", t_df), S("scalar", vd0_sguard[pp])])
                P.gset(bank(tn, 0), "vector", v_d0)
                P.gset(bank(tn, 512), "vector", v_d0)
                s_ex = P.stage("scalar",
                               lambda e, pp=pp: e.activation(
                                   e_s[:, pp * 768:pp * 768 + 768],
                                   d0_s[:, pp * 768:pp * 768 + 768], AF.Exp),
                               waits=[S("vector", v_d0), S("vector", sexp_guard[pp])])
                vd0_sguard[pp] = s_ex
                v_mt = 0
                for hh in range(2):
                    h = h0 + hh
                    eo = pp * 768 + hh * 384
                    v_mt = P.stage("vector",
                                   lambda e, h=h, eo=eo: (
                                       e.scalar_tensor_tensor(
                                           mt_s[:, h % 4 * 384:h % 4 * 384 + 256],
                                           e_s[:, eo:eo + 256],
                                           dtvT_s[:, h:h + 1], sc_s[:, 0:256],
                                           ALU.mult, ALU.mult),
                                       e.scalar_tensor_tensor(
                                           mt_s[:, h % 4 * 384 + 256:h % 4 * 384 + 384],
                                           e_s[:, eo + 256:eo + 384],
                                           dtvT_s[:, 16 + h:16 + h + 1], sc_s[:, 384:512],
                                           ALU.mult, ALU.mult))[-1],
                                   waits=[S("scalar", s_ex), S("vector", v_sc),
                                          S("vector", sm_done[("dtv", 1)]),
                                          S("tensor", mt_tguard[h % 4])])
                sexp_guard[pp] = v_mt

                for hh in range(2):
                    h = h0 + hh
                    mo = h % 4 * 384
                    py = pA[:, hh * 512:hh * 512 + 256]

                    def mm_yd(e, h=h, mo=mo, py=py):
                        e.matmul(py[:, 0:128], mt_s[:, mo:mo + 128],
                                 xt_s[:, h * 128:(h + 1) * 128], start=True, stop=True)
                        e.matmul(py[:, 128:256], mt_s[:, mo + 128:mo + 256],
                                 xt_s[:, h * 128:(h + 1) * 128], start=True, stop=False)
                        return e.matmul(py[:, 128:256], mt_s[:, mo + 256:mo + 384],
                                        xt_s[:, DI + h * 128:DI + (h + 1) * 128],
                                        start=False, stop=True)

                    t_yd = P.stage("tensor", mm_yd,
                                   waits=P.gwait(bank("A", hh * 512)) +
                                   [S("vector", v_mt)])
                    mt_tguard[h % 4] = t_yd
                    xt_tguard = t_yd
                    v_ya = P.stage("vector",
                                   lambda e, h=h, py=py: e.scalar_tensor_tensor(
                                       ya_s[:, :].rearrange("p (b x) -> p b x", x=DI)
                                       [:, :, h * 128:(h + 1) * 128],
                                       xt_s[:, :].rearrange("p (b x) -> p b x", x=DI)
                                       [:, :, h * 128:(h + 1) * 128],
                                       dpbc_s[:, h:h + 1],
                                       py.rearrange("p (b x) -> p b x", x=128),
                                       ALU.mult, ALU.add),
                                   waits=[S("tensor", t_yd), S("vector", vDp),
                                          Q("gpsimd", ya_qd[c - 1] if c >= 1 else 0, 1)])
                    P.gset(bank("A", hh * 512), "vector", v_ya)
                    last_vya = v_ya

                    v_wb = P.stage("vector",
                                   lambda e, h=h: (
                                       e.tensor_scalar(wbt_s[:, 0:64], bt_s[:, 0:64],
                                                       wsT_s[:, h:h + 1], None, ALU.mult),
                                       e.tensor_scalar(wbt_s[:, 64:128], bt_s[:, 64:128],
                                                       wsT_s[:, 16 + h:16 + h + 1],
                                                       None, ALU.mult))[-1],
                                   waits=[S("vector", sm_done[("ws", 1)]),
                                          S("vector", last_vst),
                                          S("tensor", P.cnt["tensor"] if h else 0)])

                    pst = pB[0:64, hh * 512:hh * 512 + 128]

                    def mm_st(e, h=h, pst=pst):
                        e.matmul(pst, wbt_s[:, 0:64], xt_s[:, h * 128:(h + 1) * 128],
                                 start=True, stop=False)
                        return e.matmul(pst, wbt_s[:, 64:128],
                                        xt_s[:, DI + h * 128:DI + (h + 1) * 128],
                                        start=False, stop=True)

                    t_st = P.stage("tensor", mm_st,
                                   waits=P.gwait(bank("B", hh * 512)) +
                                   [S("vector", v_wb)])
                    xt_tguard = t_st
                    v_st = P.stage("vector",
                                   lambda e, h=h, pst=pst, c=c: (
                                       e.tensor_copy(ss_s[:, h * 128:(h + 1) * 128], pst),
                                       e.scalar_tensor_tensor(
                                           prev_s[:, h * 128:(h + 1) * 128],
                                           prev_s[:, h * 128:(h + 1) * 128],
                                           cdbc_s[:, c * NH + h:c * NH + h + 1],
                                           pst, ALU.mult, ALU.add))[-1],
                                   waits=[S("tensor", t_st), S("vector", v_cd),
                                          S("vector", vz_pr),
                                          Q("gpsimd", ss_qd[c - 1] if c >= 1 else 0, 2)])
                    P.gset(bank("B", hh * 512), "vector", v_st)
                    last_vst = v_st

            ya_qd[c] = P.dma("gpsimd",
                             yacch[c * 2:(c + 1) * 2, :, :].rearrange("a q d -> q a d"),
                             ya_s[:, :].rearrange("q (a d) -> q a d", a=2),
                             waits=[S("vector", last_vya)], ch=1)
            ss_qd[c] = P.dma("gpsimd", sh[c, :, :], ss_s[:, :],
                             waits=[S("vector", last_vst)], ch=2)

        q_cc1 = P.dma("gpsimd", ccin[0:64, :], prev_s[:, :],
                      waits=[S("vector", P.cnt["vector"])])
        P.dma("gpsimd", alrow_s[0:1, 0:16], alacc_s[:, 0:1],
              waits=[S("vector", P.cnt["vector"])])
        q_cc2 = P.dma("gpsimd", ccin[64:65, :], alrow_s[:, :])
        g_cc = P.stage("gpsimd",
                       lambda e: e.collective_compute(
                           "AllGather", ALU.bypass,
                           replica_groups=[[0, 1, 2, 3], [4, 5, 6, 7]],
                           ins=[ccin.opt()], outs=[ccout.opt()]),
                       waits=[Q("gpsimd", max(q_cc1, q_cc2))])

    P.barrier()

    with contextlib.ExitStack() as scopeB:
        def sbb(name, shape, dt):
            return scopeB.enter_context(nc.sbuf_tensor(name, list(shape), dt))

        winz_s = sbb("winz_s", (128, KD * DI), BF16)
        utb_s = sbb("utb_s", (128, 2 * KD * 256), BF16)
        wout_s = sbb("wout_s", (128, 16 * 1024), BF16)
        sz_s = sbb("sz_s", (128, DI), F32)
        y_s = sbb("y_s", (128, DI), F32)
        g_s = sbb("g_s", (128, DI), F32)
        gt_s = sbb("gt_s", (128, DI), BF16)
        ur_s = sbb("ur_s", (128, 2 * 2 * D), F32)
        yin_s = sbb("yin_s", (128, 2 * 2 * DI), BF16)
        sin_s = sbb("sin_s", (DS, 2 * NH * HD), BF16)
        xpre_s = sbb("xpre_s", (128, D), F32)
        col_s = sbb("col_s", (128, 8), F32)
        alogs_s = sbb("alogs_s", (4, 16), F32)
        coef_s = sbb("coef_s", (4, 16), F32)
        coefr_s = sbb("coefr_s", (1, 16), F32)
        coefbc_s = sbb("coefbc_s", (DS, 4 * 16), F32)

        P.dma("sync", winz_s[:, :].rearrange("p (k c) -> p k c", k=KD), winzv[:, :, :])
        P.dma("sync", wout_s[:, :].rearrange("p (k d) -> p k d", k=16), woutv[:, :, :])
        q_wzo = P.qchan("sync", 0)

        g_after_cc = P.cnt["gpsimd"]
        for j in range(4):
            P.dma("gpsimd", alogs_s[j:j + 1, :], ccout[65 * j + 64:65 * j + 65, 0:16],
                  waits=[S("gpsimd", g_after_cc)])
        q_al = P.qchan("gpsimd", 0)
        t_co = P.stage("tensor",
                       lambda e: e.matmul(pC[0:4, 0:16], pmat_s[:, :], alogs_s[:, :],
                                          start=True, stop=True),
                       waits=P.gwait(bank("C", 0)) + [Q("gpsimd", q_al)])
        s_co = P.stage("scalar",
                       lambda e: e.activation(coef_s[:, :], pC[0:4, 0:16], AF.Exp),
                       waits=[S("tensor", t_co)])
        P.gset(bank("C", 0), "scalar", s_co)
        v_co = P.stage("vector",
                       lambda e: e.tensor_scalar(coef_s[:, :], coef_s[:, :],
                                                 mvec_s[:, 0:1], None, ALU.mult),
                       waits=[S("scalar", s_co)])
        v_pz = P.stage("vector", lambda e: e.memset(prev_s[:, :], 0.0),
                       waits=[Q("gpsimd", q_al)])
        prev_v = v_pz
        q_sj_prev = 0
        for j in range(4):
            q_cr = P.dma("gpsimd", coefr_s[:, :], coef_s[j:j + 1, :],
                         waits=[S("vector", v_co),
                                S("tensor", P.cnt["tensor"])])
            t_bc = P.stage("tensor",
                           lambda e: e.matmul(pC[0:64, 32:48], onec_s[0:1, 0:64],
                                              coefr_s[0:1, :], start=True, stop=True),
                           waits=P.gwait(bank("C", 0)) + [Q("gpsimd", q_cr)])
            v_bc = P.stage("vector",
                           lambda e, j=j: e.tensor_copy(
                               coefbc_s[:, j * 16:(j + 1) * 16], pC[0:64, 32:48]),
                           waits=[S("tensor", t_bc)])
            P.gset(bank("C", 0), "vector", v_bc)
            q_sj = P.dma("gpsimd", y_s[0:64, 0:2048], ccout[65 * j:65 * j + 64, :],
                         waits=[S("gpsimd", g_after_cc), S("vector", prev_v)])

            def mk_carry(j=j):
                def body(e):
                    for h in range(NH):
                        inst = e.scalar_tensor_tensor(
                            prev_s[:, h * 128:(h + 1) * 128],
                            y_s[0:64, h * 128:(h + 1) * 128],
                            coefbc_s[:, j * 16 + h:j * 16 + h + 1],
                            prev_s[:, h * 128:(h + 1) * 128], ALU.mult, ALU.add)
                    return inst
                return body

            prev_v = P.stage("vector", mk_carry(),
                             waits=[S("vector", max(prev_v, v_bc)), Q("gpsimd", q_sj)])
        prevb_v = P.stage("vector",
                          lambda e: e.tensor_copy(prevb_s[:, :], prev_s[:, :]),
                          waits=[S("vector", prev_v)])

        def utb_dst(c):
            o = (c % 2) * (KD * 256)
            return utb_s[:, o:o + KD * 256].rearrange("p (k t) -> p k t", k=KD)

        q_utb = [0] * NCH
        q_yin = [0] * NCH
        q_sin = [0] * NCH
        q_ur = [0] * NCH
        q_utb[0] = P.dma("sync", utb_dst(0), uTv[:, :, 3:259], ch=5)
        q_yin[0] = P.dma("gpsimd",
                         yin_s[:, 0:2 * DI].rearrange("q (a d) -> q a d", a=2),
                         yacch[0:2, :, :].rearrange("a q d -> q a d"),
                         waits=[Q("gpsimd", P.qchan("gpsimd", 1), 1)], ch=3)
        q_sin[0] = P.dma("gpsimd", sin_s[:, 0:NH * HD], sh[0, :, :],
                         waits=[Q("gpsimd", P.qchan("gpsimd", 2), 2)], ch=5)
        q_ur[0] = P.dma("sync",
                        ur_s[:, 0:2 * D].rearrange("p (a d) -> p a d", a=2),
                        ures[0:256, :].rearrange("(a p) d -> p a d", p=128), ch=7)

        utb_tguard = [0, 0]
        yin_vguard = [0, 0]
        sin_vguard = [0, 0]
        ur_vguard = [0, 0]
        v_g_prev = 0
        s_sq_prev = 0
        s_xq_prev = 0
        v_xp_prev = 0
        v_x1_prev = 0
        t_gt_prev = 0
        t_op_prev = 0
        for c in range(NCH):
            par = c % 2
            if c + 1 < NCH:
                np_ = (c + 1) % 2
                q_utb[c + 1] = P.dma("sync", utb_dst(c + 1),
                                     uTv[:, :, (c + 1) * 256 + 3:(c + 1) * 256 + 259],
                                     waits=[S("tensor", utb_tguard[np_])], ch=5 + np_)
                q_yin[c + 1] = P.dma("gpsimd",
                                     yin_s[:, np_ * 2 * DI:(np_ + 1) * 2 * DI]
                                     .rearrange("q (a d) -> q a d", a=2),
                                     yacch[(c + 1) * 2:(c + 2) * 2, :, :]
                                     .rearrange("a q d -> q a d"),
                                     waits=[S("vector", yin_vguard[np_])], ch=3 + np_)
                q_sin[c + 1] = P.dma("gpsimd",
                                     sin_s[:, np_ * NH * HD:(np_ + 1) * NH * HD],
                                     sh[c + 1, :, :],
                                     waits=[S("vector", sin_vguard[np_])], ch=5 + np_)
                q_ur[c + 1] = P.dma("sync",
                                    ur_s[:, np_ * 2 * D:(np_ + 1) * 2 * D]
                                    .rearrange("p (a d) -> p a d", a=2),
                                    ures[(c + 1) * 256:(c + 2) * 256, :]
                                    .rearrange("(a p) d -> p a d", p=128),
                                    waits=[S("vector", ur_vguard[np_])], ch=7 + np_)

            for qt in range(2):
                last_sz = 0
                for eb in range(4):
                    ebp = eb % 2

                    def mm_z(e, qt=qt, eb=eb, par=par, ebp=ebp):
                        for k in range(KD):
                            inst = e.matmul(
                                pA[:, ebp * 512:ebp * 512 + 512],
                                utb_s[:, par * KD * 256 + k * 256 + qt * 128:
                                      par * KD * 256 + k * 256 + qt * 128 + 128],
                                winz_s[:, k * DI + eb * 512:k * DI + (eb + 1) * 512],
                                start=(k == 0), stop=(k == KD - 1))
                        return inst

                    t_z = P.stage("tensor", mm_z,
                                  waits=P.gwait(bank("A", ebp * 512)) +
                                  [Q("sync", q_utb[c], 5 + par), Q("sync", q_wzo)])
                    utb_tguard[par] = t_z
                    s_z = P.stage("scalar",
                                  lambda e, eb=eb, ebp=ebp: e.activation(
                                      sz_s[:, eb * 512:(eb + 1) * 512],
                                      pA[:, ebp * 512:ebp * 512 + 512], AF.Silu),
                                  waits=[S("tensor", t_z), S("vector", v_g_prev)])
                    P.gset(bank("A", ebp * 512), "scalar", s_z)
                    last_sz = s_z

                def mm_yo(e, qt=qt, c=c):
                    for seg in range(4):
                        pt = pB if seg < 2 else pC
                        off = (seg % 2) * 512
                        inst = e.matmul(
                            pt[:, off:off + 512],
                            Cb_s[:, c * CK + qt * 128:c * CK + qt * 128 + 128],
                            prevb_s[:, seg * 512:(seg + 1) * 512],
                            start=True, stop=True)
                    return inst

                t_yo = P.stage("tensor", mm_yo,
                               waits=P.gwait(bank("B", 0), bank("B", 512),
                                             bank("C", 0), bank("C", 512)) +
                               [S("vector", prevb_v)])
                last_vy = 0
                for h in range(NH):
                    seg = h // 4
                    pt = pB if seg < 2 else pC
                    off = (seg % 2) * 512 + (h % 4) * 128
                    v_y = P.stage("vector",
                                  lambda e, h=h, qt=qt, c=c, par=par, pt=pt, off=off:
                                  e.scalar_tensor_tensor(
                                      y_s[:, h * 128:(h + 1) * 128],
                                      pt[:, off:off + 128],
                                      eaT_s[:, (c * 2 + qt) * 16 + h:(c * 2 + qt) * 16 + h + 1],
                                      yin_s[:, par * 2 * DI + qt * DI + h * 128:
                                            par * 2 * DI + qt * DI + (h + 1) * 128],
                                      ALU.mult, ALU.add),
                                  waits=[S("tensor", t_yo),
                                         Q("gpsimd", q_yin[c], 3 + par),
                                         S("scalar", s_sq_prev)])
                    last_vy = v_y
                for tn2 in ("B", "C"):
                    P.gset(bank(tn2, 0), "vector", last_vy)
                    P.gset(bank(tn2, 512), "vector", last_vy)
                if qt == 1:
                    yin_vguard[par] = last_vy

                v_g = P.stage("vector",
                              lambda e: e.tensor_tensor(g_s[:, :], y_s[:, :],
                                                        sz_s[:, :], ALU.mult),
                              waits=[S("vector", last_vy), S("scalar", last_sz),
                                     S("scalar", s_sq_prev),
                                     S("tensor", t_gt_prev)])
                v_g_prev = v_g
                s_sq = P.stage("scalar",
                               lambda e, qt=qt: e.activation(
                                   y_s[:, :], g_s[:, :], AF.Square,
                                   accum_out=sumsq_s[:, qt:qt + 1]),
                               waits=[S("vector", v_g)])
                s_sq_prev = s_sq
                last_gt = 0
                for et in range(16):
                    ep = et % 2
                    t_gt = P.stage("tensor",
                                   lambda e, et=et, ep=ep: e.transpose(
                                       pC[:, ep * 512:ep * 512 + 128],
                                       g_s[:, et * 128:(et + 1) * 128], idf_s[:, :]),
                                   waits=P.gwait(bank("C", ep * 512)) +
                                   [S("vector", v_g)])
                    v_gt = P.stage("vector",
                                   lambda e, et=et, ep=ep: e.tensor_scalar(
                                       gt_s[:, et * 128:(et + 1) * 128],
                                       pC[:, ep * 512:ep * 512 + 128],
                                       nw_s[:, et:et + 1], None, ALU.mult),
                                   waits=[S("tensor", t_gt), S("tensor", t_op_prev)])
                    P.gset(bank("C", ep * 512), "vector", v_gt)
                    last_gt = v_gt
                t_gt_prev = P.cnt["tensor"]

                def mm_op(e):
                    for db in range(2):
                        for k in range(16):
                            inst = e.matmul(
                                pD[:, db * 512:(db + 1) * 512],
                                gt_s[:, k * 128:(k + 1) * 128],
                                wout_s[:, k * 1024 + db * 512:k * 1024 + (db + 1) * 512],
                                start=(k == 0), stop=(k == 15))
                    return inst

                t_op = P.stage("tensor", mm_op,
                               waits=P.gwait(bank("D", 0), bank("D", 512)) +
                               [S("vector", last_gt), Q("sync", q_wzo)])
                t_op_prev = t_op

                s_rm1 = P.stage("scalar",
                                lambda e, qt=qt: e.activation(
                                    col_s[:, 0:1], sumsq_s[:, qt:qt + 1], AF.Ln,
                                    bias=ko_s[:, 0:1], scale=ko_s[:, 3:4]),
                                waits=[S("scalar", s_sq), S("vector", v_x1_prev)])
                s_rm2 = P.stage("scalar",
                                lambda e: e.activation(col_s[:, 1:2], col_s[:, 0:1],
                                                       AF.Exp, scale=ko_s[:, 2:3]),
                                waits=[S("scalar", s_rm1)])
                v_xp = P.stage("vector",
                               lambda e, qt=qt, par=par: e.scalar_tensor_tensor(
                                   xpre_s[:, :], pD[:, 0:1024], col_s[:, 1:2],
                                   ur_s[:, par * 2 * D + qt * D:par * 2 * D + (qt + 1) * D],
                                   ALU.mult, ALU.add, accum_out=col_s[:, 2:3]),
                               waits=[S("tensor", t_op), S("scalar", s_rm2),
                                      Q("sync", q_ur[c], 7 + par),
                                      S("scalar", s_xq_prev)])
                P.gset(bank("D", 0), "vector", v_xp)
                P.gset(bank("D", 512), "vector", v_xp)
                v_xp_prev = v_xp
                if qt == 1:
                    ur_vguard[par] = v_xp
                s_xq = P.stage("scalar",
                               lambda e: e.activation(sz_s[:, 0:1024], xpre_s[:, :],
                                                      AF.Square,
                                                      accum_out=col_s[:, 3:4]),
                               waits=[S("vector", v_xp)])
                s_xq_prev = s_xq
                P.stage("vector",
                        lambda e: e.tensor_scalar(col_s[:, 4:5], col_s[:, 2:3],
                                                  1.0 / D, None, ALU.mult),
                        waits=[S("scalar", s_xq)])
                P.stage("vector",
                        lambda e: e.tensor_scalar(col_s[:, 5:6], col_s[:, 3:4],
                                                  1.0 / D, EPS, ALU.mult, ALU.add))
                P.stage("vector",
                        lambda e: e.tensor_scalar(col_s[:, 6:7], col_s[:, 4:5],
                                                  col_s[:, 4:5], None, ALU.mult))
                v_c1 = P.stage("vector",
                               lambda e: e.tensor_tensor(col_s[:, 5:6], col_s[:, 5:6],
                                                         col_s[:, 6:7], ALU.subtract))
                s_l1 = P.stage("scalar",
                               lambda e: e.activation(col_s[:, 6:7], col_s[:, 5:6],
                                                      AF.Ln),
                               waits=[S("vector", v_c1)])
                s_l2 = P.stage("scalar",
                               lambda e: e.activation(col_s[:, 7:8], col_s[:, 6:7],
                                                      AF.Exp, scale=ko_s[:, 2:3]),
                               waits=[S("scalar", s_l1)])
                v_x1 = P.stage("vector",
                               lambda e, c=c, qt=qt: e.tensor_scalar(
                                   x1_s[:, (c * 2 + qt) * D:(c * 2 + qt + 1) * D],
                                   xpre_s[:, :], col_s[:, 4:5], col_s[:, 7:8],
                                   ALU.subtract, ALU.mult),
                               waits=[S("scalar", s_l2)])
                v_x1_prev = v_x1

            def mk_pu(c=c, par=par):
                def body(e):
                    for h in range(NH):
                        inst = e.scalar_tensor_tensor(
                            prev_s[:, h * 128:(h + 1) * 128],
                            prev_s[:, h * 128:(h + 1) * 128],
                            cdbc_s[:, c * NH + h:c * NH + h + 1],
                            sin_s[:, par * NH * HD + h * 128:par * NH * HD + (h + 1) * 128],
                            ALU.mult, ALU.add)
                    return inst
                return body

            v_pu = P.stage("vector", mk_pu(),
                           waits=[S("tensor", P.cnt["tensor"]),
                                  Q("gpsimd", q_sin[c], 5 + par)])
            sin_vguard[par] = v_pu
            prevb_v = P.stage("vector",
                              lambda e: e.tensor_copy(prevb_s[:, :], prev_s[:, :]),
                              waits=[S("vector", v_pu)])

    P.barrier()

    with contextlib.ExitStack() as scopeF:
        def sbf(name, shape, dt):
            return scopeF.enter_context(nc.sbuf_tensor(name, list(shape), dt))

        xf_s = sbf("xf_s", (128, KD * HALF), BF16)
        h_s = sbf("h_s", (128, KH * HALF), BF16)
        w13_s = sbf("w13_s", (128, 2 * 2 * KD * 256), BF16)
        w2b_s = sbf("w2b_s", (128, 2 * 4 * 1024), BF16)
        yf_s = sbf("yf_s", (128, 8 * D), F32)
        sl_s = sbf("sl_s", (128, 1024), F32)
        comb_s = sbf("comb_s", (128, 64), F32)
        sm_s = sbf("sm_s", (128, 8), F32)
        m8_s = sbf("m8_s", (128, 8), F32)
        wrk_s = sbf("wrk_s", (128, 8), F32)
        colf_s = sbf("colf_s", (128, 8), F32)
        xw_s = sbf("xw_s", (128, D), F32)
        x2_s = sbf("x2_s", (128, D), F32)

        def w13_dst(buf, mat):
            o = (buf * 2 + mat) * KD * 256
            return w13_s[:, o:o + KD * 256].rearrange("p (k t) -> p k t", k=KD)

        def w13_sl(buf, mat, k, lo, sz):
            o = (buf * 2 + mat) * KD * 256
            return w13_s[:, o + k * 256 + lo:o + k * 256 + lo + sz]

        def w2_dst(buf, nk):
            o = buf * 4 * 1024
            return w2b_s[:, o:o + nk * 1024].rearrange("p (k t) -> p k t", k=nk)

        w13_tguard = [0, 0]
        w2_tguard = [0, 0]
        vh_guard = 0
        s_si_prev = 0
        v_h_prev = 0
        s_g_prev = 0
        v_x2a_sguard = 0
        q_out_prev = 0
        xf_tguard = 0
        last_h_v = 0

        for hf in range(2):
            last_xf = 0
            trc = 0
            for tt in range(8):
                gtile = hf * 8 + tt
                for k in range(KD):
                    tb = trc % 2
                    t_tx = P.stage("tensor",
                                   lambda e, gtile=gtile, k=k, tb=tb: e.transpose(
                                       pA[:, tb * 512:tb * 512 + 64].bitcast(BF16),
                                       x1_s[:, gtile * D + k * 128:gtile * D + (k + 1) * 128],
                                       idb_s[:, :]),
                                   waits=P.gwait(bank("A", tb * 512)))
                    v_tx = P.stage("vector",
                                   lambda e, tt=tt, k=k, tb=tb: e.tensor_copy(
                                       xf_s[:, k * HALF + tt * 128:k * HALF + (tt + 1) * 128],
                                       pA[:, tb * 512:tb * 512 + 64].bitcast(BF16)),
                                   waits=[S("tensor", t_tx), S("tensor", xf_tguard)])
                    P.gset(bank("A", tb * 512), "vector", v_tx)
                    last_xf = v_tx
                    trc += 1

            for tt in range(8):
                def mm_g(e, tt=tt):
                    for k in range(KD):
                        inst = e.matmul(
                            pC[:, 0:8],
                            xf_s[:, k * HALF + tt * 128:k * HALF + (tt + 1) * 128],
                            gatet_s[:, k * 8:(k + 1) * 8],
                            start=(k == 0), stop=(k == KD - 1))
                    return inst

                t_g = P.stage("tensor", mm_g,
                              waits=P.gwait(bank("C", 0)) + [S("vector", last_xf)])
                v_g1 = P.stage("vector",
                               lambda e: e.reduce_max(colf_s[:, 0:1], pC[:, 0:8],
                                                      AXX, negate=True),
                               waits=[S("tensor", t_g), S("scalar", s_g_prev)])
                s_g1 = P.stage("scalar",
                               lambda e: e.activation(sm_s[:, :], pC[:, 0:8], AF.Exp,
                                                      bias=colf_s[:, 0:1],
                                                      accum_out=colf_s[:, 1:2]),
                               waits=[S("vector", v_g1)])
                s_g_prev = s_g1
                P.gset(bank("C", 0), "scalar", s_g1)
                P.stage("vector",
                        lambda e: e.reciprocal(colf_s[:, 2:3], colf_s[:, 1:2]),
                        waits=[S("scalar", s_g1)])
                P.stage("vector",
                        lambda e: e.tensor_scalar(sm_s[:, :], sm_s[:, :],
                                                  colf_s[:, 2:3], None, ALU.mult))
                P.stage("vector", lambda e: e.max(out=m8_s[:, :], in_=sm_s[:, :]))
                P.stage("vector", lambda e: e.memset(m8_s[:, 2:8], NEG))
                P.stage("vector",
                        lambda e: e.match_replace(out=wrk_s[:, :],
                                                  in_to_replace=m8_s[:, :],
                                                  in_values=sm_s[:, :],
                                                  imm_value=NEG))
                P.stage("vector",
                        lambda e: e.tensor_single_scalar(wrk_s[:, :], wrk_s[:, :],
                                                         NEG, ALU.is_equal))
                v_g2 = P.stage("vector",
                               lambda e, tt=tt: e.tensor_tensor(
                                   comb_s[:, tt * 8:(tt + 1) * 8],
                                   wrk_s[:, :], sm_s[:, :], ALU.mult))
            last_comb = P.cnt["vector"]

            v_yz = P.stage("vector", lambda e: e.memset(yf_s[:, :], 0.0))

            for ei in list(range(8)) + ["shared"]:
                nht = KH if ei != "shared" else KSH
                ngrp = (nht + 1) // 2

                def w13_load(g, buf, ei=ei, tg=None):
                    tgv = w13_tguard[buf] if tg is None else tg
                    a, b = (sw1v, sw3v) if ei == "shared" else (w1v[ei], w3v[ei])
                    q1 = P.dma("sync", w13_dst(buf, 0),
                               a[:, :, g * 256:g * 256 + 256],
                               waits=[S("tensor", tgv)], ch=1 + buf)
                    q3 = P.dma("sync", w13_dst(buf, 1),
                               b[:, :, g * 256:g * 256 + 256], ch=1 + buf)
                    return q3

                qld = [0] * ngrp
                qld[0] = w13_load(0, 0)
                if ngrp > 1:
                    qld[1] = w13_load(1, 1)
                pab = 0
                for g in range(ngrp):
                    for hi in range(2):
                        ht = g * 2 + hi
                        if ht >= nht:
                            break
                        if pab % 2 == 0:
                            pt1, pt3, n1, n3 = pA, pB, "A", "B"
                        else:
                            pt1, pt3, n1, n3 = pC, pD, "C", "D"

                        def mm_h(e, hi=hi, g=g, pt1=pt1, pt3=pt3):
                            buf = g % 2
                            for mat, pt in ((0, pt1), (1, pt3)):
                                for k in range(KD):
                                    for blk in range(2):
                                        inst = e.matmul(
                                            pt[:, blk * 512:blk * 512 + 512],
                                            w13_sl(buf, mat, k, hi * 128, 128),
                                            xf_s[:, k * HALF + blk * 512:k * HALF + (blk + 1) * 512],
                                            start=(k == 0), stop=(k == KD - 1),
                                            skip_group_check=True)
                            return inst

                        t_mh = P.stage("tensor", mm_h,
                                       waits=P.gwait(bank(n1, 0), bank(n1, 512),
                                                     bank(n3, 0), bank(n3, 512)) +
                                       [Q("sync", qld[g], 1 + g % 2),
                                        S("vector", last_xf)])
                        s_si = P.stage("scalar",
                                       lambda e, pt1=pt1: e.activation(
                                           sl_s[:, 0:1024], pt1[:, 0:1024], AF.Silu),
                                       waits=[S("tensor", t_mh),
                                              S("vector", v_h_prev)])
                        P.gset(bank(n1, 0), "scalar", s_si)
                        P.gset(bank(n1, 512), "scalar", s_si)
                        s_si_prev = s_si
                        v_h = P.stage("vector",
                                      lambda e, ht=ht, pt3=pt3: e.tensor_tensor(
                                          h_s[:, ht * HALF:(ht + 1) * HALF],
                                          sl_s[:, 0:1024], pt3[:, 0:1024], ALU.mult),
                                      waits=[S("scalar", s_si),
                                             S("tensor", vh_guard)])
                        P.gset(bank(n3, 0), "vector", v_h)
                        P.gset(bank(n3, 512), "vector", v_h)
                        v_h_prev = v_h
                        pab += 1
                    if g + 2 < ngrp:
                        qld[g + 2] = w13_load(g + 2, g % 2, tg=P.cnt["tensor"])
                w13_tguard[0] = P.cnt["tensor"]
                w13_tguard[1] = P.cnt["tensor"]
                last_h_v = v_h_prev

                kgroups = []
                k0 = 0
                while k0 < nht:
                    kgroups.append((k0, min(4, nht - k0)))
                    k0 += 4
                for th in range(2):
                    qw2 = [0] * len(kgroups)
                    qw2[0] = P.dma("sync", w2_dst(0, kgroups[0][1]),
                                   (sw2v if ei == "shared" else w2v[ei])
                                   [:, kgroups[0][0]:kgroups[0][0] + kgroups[0][1], :],
                                   waits=[S("tensor", w2_tguard[0])], ch=3)
                    if len(kgroups) > 1:
                        qw2[1] = P.dma("sync", w2_dst(1, kgroups[1][1]),
                                       (sw2v if ei == "shared" else w2v[ei])
                                       [:, kgroups[1][0]:kgroups[1][0] + kgroups[1][1], :],
                                       waits=[S("tensor", w2_tguard[1])], ch=4)
                    pts = (pA, pB, pC, pD)
                    for gi, (k0g, nk) in enumerate(kgroups):
                        buf = gi % 2

                        def mm_y(e, gi=gi, k0g=k0g, nk=nk, buf=buf, th=th, nht=nht):
                            for tt4 in range(4):
                                tt = th * 4 + tt4
                                pt = pts[tt4]
                                for ki in range(nk):
                                    k = k0g + ki
                                    st = (k == 0)
                                    sp = (k == nht - 1)
                                    e.matmul(pt[:, 0:512],
                                             h_s[:, k * HALF + tt * 128:k * HALF + (tt + 1) * 128],
                                             w2b_s[:, buf * 4096 + ki * 1024:buf * 4096 + ki * 1024 + 512],
                                             start=st, stop=sp,
                                             skip_group_check=True)
                                    inst = e.matmul(pt[:, 512:1024],
                                                    h_s[:, k * HALF + tt * 128:k * HALF + (tt + 1) * 128],
                                                    w2b_s[:, buf * 4096 + ki * 1024 + 512:buf * 4096 + (ki + 1) * 1024],
                                                    start=st, stop=sp,
                                                    skip_group_check=True)
                            return inst

                        gw = []
                        if gi == 0:
                            for pt_ in ("A", "B", "C", "D"):
                                gw += P.gwait(bank(pt_, 0), bank(pt_, 512))
                            gw += [S("vector", last_h_v)]
                        t_my = P.stage("tensor", mm_y,
                                       waits=gw + [Q("sync", qw2[gi], 3 + gi % 2)])
                        w2_tguard[buf] = t_my
                        vh_guard = t_my
                        if gi + 2 < len(kgroups):
                            qw2[gi + 2] = P.dma(
                                "sync", w2_dst(buf, kgroups[gi + 2][1]),
                                (sw2v if ei == "shared" else w2v[ei])
                                [:, kgroups[gi + 2][0]:kgroups[gi + 2][0] + kgroups[gi + 2][1], :],
                                waits=[S("tensor", t_my)], ch=3 + buf)
                    for tt4 in range(4):
                        tt = th * 4 + tt4
                        pt = pts[tt4]
                        tn = "ABCD"[tt4]
                        if ei == "shared":
                            v_yy = P.stage("vector",
                                           lambda e, tt=tt, pt=pt: e.scalar_tensor_tensor(
                                               yf_s[:, tt * D:(tt + 1) * D],
                                               pt[:, 0:1024], 1.0,
                                               yf_s[:, tt * D:(tt + 1) * D],
                                               ALU.mult, ALU.add),
                                           waits=[S("tensor", t_my),
                                                  S("vector", max(v_yz, last_comb))])
                        else:
                            v_yy = P.stage("vector",
                                           lambda e, tt=tt, pt=pt, ei=ei: e.scalar_tensor_tensor(
                                               yf_s[:, tt * D:(tt + 1) * D],
                                               pt[:, 0:1024],
                                               comb_s[:, tt * 8 + ei:tt * 8 + ei + 1],
                                               yf_s[:, tt * D:(tt + 1) * D],
                                               ALU.mult, ALU.add),
                                           waits=[S("tensor", t_my),
                                                  S("vector", max(v_yz, last_comb))])
                        P.gset(bank(tn, 0), "vector", v_yy)
                        P.gset(bank(tn, 512), "vector", v_yy)

            for tt in range(8):
                gtile = hf * 8 + tt
                v_x2a = P.stage("vector",
                                lambda e, tt=tt, gtile=gtile: e.scalar_tensor_tensor(
                                    xw_s[:, :], x1_s[:, gtile * D:(gtile + 1) * D],
                                    1.0, yf_s[:, tt * D:(tt + 1) * D],
                                    ALU.mult, ALU.add, accum_out=colf_s[:, 3:4]),
                                waits=[S("scalar", v_x2a_sguard)])
                s_x2 = P.stage("scalar",
                               lambda e: e.activation(sl_s[:, 0:1024], xw_s[:, :],
                                                      AF.Square,
                                                      accum_out=colf_s[:, 4:5]),
                               waits=[S("vector", v_x2a)])
                v_x2a_sguard = s_x2
                P.stage("vector",
                        lambda e: e.tensor_scalar(colf_s[:, 5:6], colf_s[:, 3:4],
                                                  1.0 / D, None, ALU.mult),
                        waits=[S("scalar", s_x2)])
                P.stage("vector",
                        lambda e: e.tensor_scalar(colf_s[:, 6:7], colf_s[:, 4:5],
                                                  1.0 / D, EPS, ALU.mult, ALU.add))
                P.stage("vector",
                        lambda e: e.tensor_scalar(colf_s[:, 7:8], colf_s[:, 5:6],
                                                  colf_s[:, 5:6], None, ALU.mult))
                v_x2b = P.stage("vector",
                                lambda e: e.tensor_tensor(colf_s[:, 6:7], colf_s[:, 6:7],
                                                          colf_s[:, 7:8], ALU.subtract))
                s_x3 = P.stage("scalar",
                               lambda e: e.activation(colf_s[:, 7:8], colf_s[:, 6:7],
                                                      AF.Ln),
                               waits=[S("vector", v_x2b)])
                s_x4 = P.stage("scalar",
                               lambda e: e.activation(colf_s[:, 6:7], colf_s[:, 7:8],
                                                      AF.Exp, scale=ko_s[:, 2:3]),
                               waits=[S("scalar", s_x3)])
                v_x2c = P.stage("vector",
                                lambda e: e.tensor_scalar(
                                    x2_s[:, :], xw_s[:, :], colf_s[:, 5:6],
                                    colf_s[:, 6:7], ALU.subtract, ALU.mult),
                                waits=[S("scalar", s_x4),
                                       Q("gpsimd", q_out_prev, 7)])
                q_out_prev = P.dma("gpsimd", outh[gtile * 128:(gtile + 1) * 128, :],
                                   x2_s[:, :], waits=[S("vector", v_x2c)], ch=7)
            xf_tguard = P.cnt["tensor"]

        P.wait("gpsimd", waits=[("q", k, v) for k, v in P.qcnt.items()
                                if k[0] == "gpsimd"])

    stack.close()

    emstack = contextlib.ExitStack()
    sems = {e: emstack.enter_context(nc.semaphore("se_" + e)) for e in ENGS}
    qsems = {k: emstack.enter_context(nc.semaphore("qu_%s_%d" % k))
             for k in sorted(P.qcnt.keys())}
    with nc.Block() as block:

        @block.sync
        def _(eng):
            for f in P.ops["sync"]:
                f(eng, sems, qsems)

        @block.scalar
        def _(eng):
            for f in P.ops["scalar"]:
                f(eng, sems, qsems)

        @block.vector
        def _(eng):
            for f in P.ops["vector"]:
                f(eng, sems, qsems)

        @block.tensor
        def _(eng):
            for f in P.ops["tensor"]:
                f(eng, sems, qsems)

        @block.gpsimd
        def _(eng):
            for f in P.ops["gpsimd"]:
                f(eng, sems, qsems)

    emstack.close()
    return nc


def kernel(L0_attn_u, L0_attn_in_proj_w, L0_attn_conv_w, L0_attn_conv_b,
           L0_attn_dt_bias, L0_attn_A_log, L0_attn_Dp, L0_attn_norm_w,
           L0_attn_out_proj_w, L0_ffn_gate_w, L0_ffn_w1, L0_ffn_w2, L0_ffn_w3,
           L0_ffn_sw1, L0_ffn_sw2, L0_ffn_sw3):
    u = np.asarray(L0_attn_u, np.float32)
    Win = np.asarray(L0_attn_in_proj_w, np.float32)
    WinT = np.ascontiguousarray(Win.T)
    mk = np.zeros((128, 512), np.float32)
    qq = np.arange(256)
    for k in range(128):
        mk[k, 0:256] = (qq >= k)
        mk[k, 256:512] = (qq >= 128 + k)
    shared = {
        "winx": np.ascontiguousarray(WinT[:, 2048:4240].astype(BF)),
        "winz": np.ascontiguousarray(WinT[:, 0:2048].astype(BF)),
        "woutT": np.ascontiguousarray(np.asarray(L0_attn_out_proj_w, np.float32).T.astype(BF)),
        "gateT": np.ascontiguousarray(np.asarray(L0_ffn_gate_w, np.float32).T.astype(BF)),
        "cwh": np.ascontiguousarray(
            np.asarray(L0_attn_conv_w, np.float32).reshape(17, 128, 4)
            .transpose(1, 2, 0).reshape(128, 68)),
        "cbh": np.ascontiguousarray(np.asarray(L0_attn_conv_b, np.float32).reshape(17, 128).T),
        "nwh": np.ascontiguousarray(np.asarray(L0_attn_norm_w, np.float32).reshape(16, 128).T),
        "dtbh": np.asarray(L0_attn_dt_bias, np.float32).reshape(16, 1),
        "alogh": np.asarray(L0_attn_A_log, np.float32).reshape(16, 1),
        "dprh": np.asarray(L0_attn_Dp, np.float32).reshape(1, 16),
        "identfh": np.eye(128, dtype=np.float32),
        "identbh": np.eye(128, dtype=BF),
        "onesfh": np.ones((1, 4096), np.float32),
        "onesch": np.ones((1, 128), np.float32),
        "konsth": np.stack([np.full(128, EPS, np.float32),
                            np.full(128, -1.0, np.float32),
                            np.full(128, -0.5, np.float32),
                            np.full(128, 1.0 / DI, np.float32)], axis=1),
        "maskh": mk,
        "w1h": np.ascontiguousarray(np.asarray(L0_ffn_w1, np.float32).astype(BF)),
        "w2h": np.ascontiguousarray(np.asarray(L0_ffn_w2, np.float32).astype(BF)),
        "w3h": np.ascontiguousarray(np.asarray(L0_ffn_w3, np.float32).astype(BF)),
        "sw1h": np.ascontiguousarray(np.asarray(L0_ffn_sw1, np.float32).astype(BF)),
        "sw2h": np.ascontiguousarray(np.asarray(L0_ffn_sw2, np.float32).astype(BF)),
        "sw3h": np.ascontiguousarray(np.asarray(L0_ffn_sw3, np.float32).astype(BF)),
    }

    in_maps = []
    for core in range(8):
        b_, q_ = core // 4, core % 4
        t0 = q_ * TOK
        useg = u[b_, t0:t0 + TOK]
        halo = u[b_, t0 - 3:t0] if q_ else np.zeros((3, D), np.float32)
        uh = np.concatenate([halo, useg], axis=0)
        pm = np.zeros((4, 4), np.float32)
        mv = np.zeros((4, 1), np.float32)
        for j in range(4):
            if j < q_:
                mv[j, 0] = 1.0
            for i in range(4):
                if j < i < q_:
                    pm[i, j] = 1.0
        m = dict(shared)
        m["uTb"] = np.ascontiguousarray(uh.T.astype(BF))
        m["ures"] = np.ascontiguousarray(useg)
        m["pmath"] = pm
        m["mvech"] = mv
        in_maps.append(m)

    nc = build_program()
    res = run_bass_kernel_spmd(nc, in_maps, core_ids=list(range(8)))
    out = np.zeros((2, 8192, D), np.float32)
    for core in range(8):
        b_, q_ = core // 4, core % 4
        out[b_, q_ * TOK:(q_ + 1) * TOK] = res.results[core]["out"]
    return out


# version 05
# speedup vs baseline: 1.0998x; 1.0879x over previous
import contextlib
import numpy as np
import ml_dtypes
import concourse.bass as bass
import concourse.mybir as mybir
from concourse.bass_utils import run_bass_kernel_spmd

F32 = mybir.dt.float32
BF16 = mybir.dt.bfloat16
AF = mybir.ActivationFunctionType
ALU = mybir.AluOpType
AXX = mybir.AxisListType.X
BF = ml_dtypes.bfloat16

TOK = 2048
CK = 256
NCH = 8
NH = 16
HD = 128
DS = 64
D = 1024
KD = 8
DI = 2048
XBCD = 2192
NEG = -1.0e30
EPS = 1e-5
EXPH = 2816
KH = 22
SHH = 1536
KSH = 12
HALF = 1024

ENGS = ("sync", "scalar", "vector", "tensor", "gpsimd")


class Plan:

    def __init__(self):
        self.ops = {e: [] for e in ENGS}
        self.cnt = {e: 0 for e in ENGS}
        self.qcnt = {}
        self.guards = {}

    def gwait(self, *keys):
        out = []
        for k in keys:
            out += self.guards.get(k, [])
        return out

    def gset(self, key, eng, val, kind="s"):
        self.guards[key] = [(kind, eng, val)]

    def _mk(self, W, body, eng):
        def run(e, sems, qsems):
            for item in W:
                kind, en, v = item
                if v > 0:
                    e.wait_ge((sems if kind == "s" else qsems)[en], v)
            if body is not None:
                body(e).then_inc((sems if eng[0] == "s" else qsems)[eng[1]],
                                 1 if eng[0] == "s" else 16)
        return run

    def qchan(self, eng, ch):
        return self.qcnt.get((eng, ch), 0)

    def stage(self, eng, body, waits=()):
        prev = self.cnt[eng]
        self.cnt[eng] += 1
        tgt = self.cnt[eng]
        W = tuple(waits)
        if eng != "tensor" and prev > 0:
            W = W + (("s", eng, prev),)
        self.ops[eng].append(self._mk(W, body, ("s", eng)))
        return tgt

    def dma(self, eng, out, in_, waits=(), ch=0):
        key = (eng, ch)
        prev = self.qcnt.get(key, 0)
        self.qcnt[key] = prev + 16
        tgt = self.qcnt[key]

        def body(e, out=out, in_=in_):
            return e.dma_start(out=out, in_=in_)

        W = tuple(waits) + ((("q", key, prev),) if prev > 0 else ())
        self.ops[eng].append(self._mk(W, body, ("q", key)))
        return tgt

    def wait(self, eng, waits=()):
        self.ops[eng].append(self._mk(tuple(waits), None, ("s", eng)))

    def barrier(self):
        snap = [("s", x, self.cnt[x]) for x in ENGS]
        qsnap = [("q", k, v) for k, v in self.qcnt.items()]
        for e in ENGS:
            self.wait(e, waits=[w for w in snap if w[1] != e] + qsnap)


def S(eng, v):
    return ("s", eng, v)


def Q(eng, v, ch=0):
    return ("q", (eng, ch), v)


def build_program():
    nc = bass.Bass("TRN2", target_bir_lowering=False)
    P = Plan()

    def din(name, shape, dt):
        return nc.dram_tensor(name, list(shape), dt, kind="ExternalInput").ap()

    uTb = din("uTb", (D, TOK + 3), BF16)
    ures = din("ures", (TOK, D), F32)
    winx = din("winx", (D, XBCD), BF16)
    winz = din("winz", (D, DI), BF16)
    woutT = din("woutT", (DI, D), BF16)
    gateT = din("gateT", (D, 8), BF16)
    cwh = din("cwh", (128, 68), F32)
    cbh = din("cbh", (128, 17), F32)
    nwh = din("nwh", (128, 16), F32)
    dtbh = din("dtbh", (16, 1), F32)
    alogh = din("alogh", (16, 1), F32)
    dprh = din("dprh", (1, 16), F32)
    maskh = din("maskh", (128, 512), F32)
    identfh = din("identfh", (128, 128), F32)
    identbh = din("identbh", (128, 128), BF16)
    onesfh = din("onesfh", (1, 4096), F32)
    onesch = din("onesch", (1, 128), F32)
    pmath = din("pmath", (4, 4), F32)
    mvech = din("mvech", (4, 1), F32)
    konsth = din("konsth", (128, 4), F32)
    w1h = din("w1h", (8, D, EXPH), BF16)
    w2h = din("w2h", (8, EXPH, D), BF16)
    w3h = din("w3h", (8, D, EXPH), BF16)
    sw1h = din("sw1h", (D, SHH), BF16)
    sw2h = din("sw2h", (SHH, D), BF16)
    sw3h = din("sw3h", (D, SHH), BF16)

    outh = nc.dram_tensor("out", [TOK, D], F32, kind="ExternalOutput").ap()

    yacch = nc.dram_tensor("yacch", [16, 128, DI], BF16).ap()
    sh = nc.dram_tensor("sh", [NCH, DS, NH * HD], BF16).ap()
    ccin = nc.dram_tensor("ccin", [65, 2048], F32).ap()
    ccout = nc.dram_tensor("ccout", [260, 2048], F32).ap()

    w1v = w1h.rearrange("e (k p) h -> e p k h", p=128)
    w3v = w3h.rearrange("e (k p) h -> e p k h", p=128)
    w2v = w2h.rearrange("e (k p) d -> e p k d", p=128)
    sw1v = sw1h.rearrange("(k p) h -> p k h", p=128)
    sw3v = sw3h.rearrange("(k p) h -> p k h", p=128)
    sw2v = sw2h.rearrange("(k p) d -> p k d", p=128)
    uTv = uTb.rearrange("(k p) t -> p k t", p=128)
    winxv = winx.rearrange("(k p) c -> p k c", p=128)
    winzv = winz.rearrange("(k p) c -> p k c", p=128)
    woutv = woutT.rearrange("(k p) d -> p k d", p=128)
    gatev = gateT.rearrange("(k p) e -> p k e", p=128)

    stack = contextlib.ExitStack()

    def sb(name, shape, dt):
        return stack.enter_context(nc.sbuf_tensor(name, list(shape), dt))

    cw_s = sb("cw_s", (128, 68), F32)
    cb_s = sb("cb_s", (128, 17), F32)
    nw_s = sb("nw_s", (128, 16), F32)
    dtb_s = sb("dtb_s", (16, 1), F32)
    A_s = sb("A_s", (16, 1), F32)
    tA_s = sb("tA_s", (16, 1), F32)
    dpr_s = sb("dpr_s", (1, 16), F32)
    dpbc_s = sb("dpbc_s", (128, 16), F32)
    mask_s = sb("mask_s", (128, 512), F32)
    idf_s = sb("idf_s", (128, 128), F32)
    idb_s = sb("idb_s", (128, 128), BF16)
    onec_s = sb("onec_s", (1, 128), F32)
    gatet_s = sb("gatet_s", (128, 64), BF16)
    x1_s = sb("x1_s", (128, 16 * D), BF16)
    prev_s = sb("prev_s", (DS, NH * HD), F32)
    prevb_s = sb("prevb_s", (DS, NH * HD), BF16)
    cdbc_s = sb("cdbc_s", (DS, NCH * NH), F32)
    eaT_s = sb("eaT_s", (128, 16 * NH), F32)
    alacc_s = sb("alacc_s", (16, 1), F32)
    Cb_s = sb("Cb_s", (DS, NCH * CK), BF16)
    sumsq_s = sb("sumsq_s", (128, 2), F32)
    pmat_s = sb("pmat_s", (4, 4), F32)
    ko_s = sb("ko_s", (128, 4), F32)
    mvec_s = sb("mvec_s", (4, 1), F32)

    pA = stack.enter_context(nc.psum_tensor("pA", [128, 1024], F32))
    pB = stack.enter_context(nc.psum_tensor("pB", [128, 1024], F32))
    pC = stack.enter_context(nc.psum_tensor("pC", [128, 1024], F32))
    pD = stack.enter_context(nc.psum_tensor("pD", [128, 1024], F32))
    PT = {"A": pA, "B": pB, "C": pC, "D": pD}

    def bank(t, lo):
        return (t, lo // 512)

    for dst, src in ((cw_s, cwh), (cb_s, cbh), (nw_s, nwh), (dtb_s, dtbh),
                     (mask_s, maskh), (idf_s, identfh), (idb_s, identbh),
                     (onec_s, onesch), (pmat_s, pmath), (mvec_s, mvech),
                     (tA_s, alogh), (dpr_s, dprh), (ko_s, konsth)):
        P.dma("sync", dst[:, :], src[:, :])
    P.dma("sync", gatet_s[:, :].rearrange("p (k e) -> p k e", k=KD), gatev[:, :, :])
    q_init = P.qchan("sync", 0)

    sA1 = P.stage("scalar",
                  lambda e: e.activation(A_s[:, :], tA_s[:, :], AF.Exp),
                  waits=[Q("sync", q_init)])
    vA1 = P.stage("vector",
                  lambda e: e.tensor_scalar(A_s[:, :], A_s[:, :], -1.0, None, ALU.mult),
                  waits=[S("scalar", sA1)])
    tDp = P.stage("tensor",
                  lambda e: e.matmul(pC[0:128, 0:16], onec_s[0:1, 0:128],
                                     dpr_s[0:1, 0:16], start=True, stop=True),
                  waits=[Q("sync", q_init)])
    vDp = P.stage("vector", lambda e: e.tensor_copy(dpbc_s[:, :], pC[0:128, 0:16]),
                  waits=[S("tensor", tDp)])
    P.gset(bank("C", 0), "vector", vDp)
    vz_al = P.stage("vector", lambda e: e.memset(alacc_s[:, :], 0.0))
    vz_pr = P.stage("vector", lambda e: e.memset(prev_s[:, :], 0.0))

    with contextlib.ExitStack() as scopeA:
        def sba(name, shape, dt):
            return scopeA.enter_context(nc.sbuf_tensor(name, list(shape), dt))

        winx_s = sba("winx_s", (128, KD * XBCD), BF16)
        uta_s = sba("uta_s", (128, 2 * KD * 259), BF16)
        lh2_s = sba("lh2_s", (2, 4096), F32)
        rh2_s = sba("rh2_s", (2, 4096), F32)
        acc_s = sba("acc_s", (128, 512), F32)
        xcv_s = sba("xcv_s", (128, 512), F32)
        bc_s = sba("bc_s", (128, 256), F32)
        bt_s = sba("bt_s", (128, 128), F32)
        bb_s = sba("bb_s", (DS, 256), BF16)
        xt_s = sba("xt_s", (128, 2 * DI), BF16)
        t16_s = sba("t16_s", (16, 256), F32)
        dtv_s = sba("dtv_s", (16, 256), F32)
        a_s = sba("a_s", (16, 256), F32)
        acum_s = sba("acum_s", (16, 256), F32)
        nacum_s = sba("nacum_s", (16, 256), F32)
        ws_s = sba("ws_s", (16, 256), F32)
        zz16_s = sba("zz16_s", (16, 256), F32)
        dtvT_s = sba("dtvT_s", (128, 32), F32)
        wsT_s = sba("wsT_s", (128, 32), F32)
        sc_s = sba("sc_s", (128, 512), F32)
        d0_s = sba("d0_s", (128, 2 * 768), F32)
        e_s = sba("e_s", (128, 2 * 768), F32)
        mt_s = sba("mt_s", (128, 4 * 384), BF16)
        ya_s = sba("ya_s", (128, 2 * DI), BF16)
        ss_s = sba("ss_s", (DS, NH * HD), BF16)
        wbt_s = sba("wbt_s", (128, 128), BF16)
        cdr_s = sba("cdr_s", (1, 16), F32)
        alrow_s = sba("alrow_s", (1, 2048), F32)

        P.dma("sync", winx_s[:, :].rearrange("p (k c) -> p k c", k=KD), winxv[:, :, :])
        P.dma("sync", lh2_s[0:1, :], onesfh[:, :])
        P.dma("sync", rh2_s[1:2, :], onesfh[:, :])
        q_wx = P.qchan("sync", 0)
        vz16 = P.stage("vector", lambda e: e.memset(zz16_s[:, :], 0.0))
        P.stage("vector", lambda e: e.memset(alrow_s[:, :], 0.0))

        def uta_dst(c):
            o = (c % 2) * (KD * 259)
            return uta_s[:, o:o + KD * 259].rearrange("p (k t) -> p k t", k=KD)

        q_uta = [0] * (NCH + 1)
        q_uta[0] = P.dma("sync", uta_dst(0), uTv[:, :, 0:259], ch=5)

        uta_tguard = [0, 0]
        ya_qd = [0] * NCH
        ss_qd = [0] * NCH
        pack_tguard = 0
        xt_tguard = 0
        bb_tguard = 0
        sm_tr_guard = 0
        prev_cdr_q = 0
        prev_pk1_q = 0
        v_acum_readers = 0

        for c in range(NCH):
            off = (c % 2) * (KD * 259)
            if c + 1 < NCH:
                q_uta[c + 1] = P.dma("sync", uta_dst(c + 1),
                                     uTv[:, :, (c + 1) * 256:(c + 1) * 256 + 259],
                                     waits=[S("tensor", uta_tguard[(c + 1) % 2])],
                                     ch=5 + (c + 1) % 2)

            trc = 0
            ssl_guard = [0, 0]
            xcv_tguard = [0, 0]
            last_bc_v = 0
            pend = None
            tdt = None

            def plan_transposes(ct, half, ssl):
                nonlocal trc, xt_tguard, bb_tguard, last_bc_v
                if ct < 16:
                    def mm_tr(e, half=half):
                        e.transpose(pB[:, 0:128],
                                    xcv_s[:, half * 256:half * 256 + 128],
                                    idf_s[:, :])
                        return e.transpose(pB[:, 512:640],
                                           xcv_s[:, half * 256 + 128:half * 256 + 256],
                                           idf_s[:, :])

                    ttr = P.stage("tensor", mm_tr,
                                  waits=P.gwait(bank("B", 0), bank("B", 512)) +
                                  [S("scalar", ssl)])
                    vtr = P.stage("scalar",
                                  lambda e, ct=ct: e.copy(
                                      xt_s[:, :].rearrange("p (b x) -> p b x", x=DI)
                                      [:, :, ct * 128:(ct + 1) * 128],
                                      pB[:, :].rearrange("p (b x) -> p b x", x=512)
                                      [:, :, 0:128]),
                                  waits=[S("tensor", ttr), S("tensor", xt_tguard)])
                    P.gset(bank("B", 0), "scalar", vtr)
                    P.gset(bank("B", 512), "scalar", vtr)
                    xcv_tguard[half] = ttr
                    trc += 2
                else:
                    vbc = P.stage("vector",
                                  lambda e: e.tensor_copy(bb_s[:, :], bc_s[0:64, :]),
                                  waits=[S("scalar", ssl), S("tensor", bb_tguard)])
                    vcc = P.stage("vector",
                                  lambda e, c=c: e.tensor_copy(
                                      Cb_s[:, c * CK:(c + 1) * CK], bc_s[64:128, :]),
                                  waits=[S("scalar", ssl)])

                    def mm_trb(e):
                        e.transpose(pB[0:128, 0:64], bc_s[0:64, 0:128],
                                    idf_s[0:64, 0:64])
                        return e.transpose(pB[0:128, 512:576], bc_s[0:64, 128:256],
                                           idf_s[0:64, 0:64])

                    ttr = P.stage("tensor", mm_trb,
                                  waits=P.gwait(bank("B", 0), bank("B", 512)) +
                                  [S("scalar", ssl)])
                    vtr = P.stage("vector",
                                  lambda e: (e.tensor_copy(bt_s[:, 0:64],
                                                           pB[0:128, 0:64]),
                                             e.tensor_copy(bt_s[:, 64:128],
                                                           pB[0:128, 512:576]))[-1],
                                  waits=[S("tensor", ttr)])
                    P.gset(bank("B", 0), "vector", vtr)
                    P.gset(bank("B", 512), "vector", vtr)
                    bb_tguard = ttr
                    last_bc_v = max(vbc, vcc)
                    trc += 2

            for ct in range(18):
                half = ct % 2
                if ct < 17:
                    def mm_ip(e, ct=ct, off=off, half=half):
                        for k in range(KD):
                            inst = e.matmul(
                                pA[:, half * 512:half * 512 + 259],
                                winx_s[:, k * XBCD + ct * 128:k * XBCD + (ct + 1) * 128],
                                uta_s[:, off + k * 259:off + (k + 1) * 259],
                                start=(k == 0), stop=(k == KD - 1))
                        return inst

                    tmm = P.stage("tensor", mm_ip,
                                  waits=P.gwait(bank("A", half * 512)) +
                                  [Q("sync", q_uta[c], 5 + c % 2), Q("sync", q_wx)])
                    uta_tguard[c % 2] = tmm
                else:
                    def mm_dt(e, off=off):
                        for k in range(KD):
                            inst = e.matmul(
                                pA[0:16, 512:512 + 256],
                                winx_s[:, k * XBCD + 2176:k * XBCD + 2192],
                                uta_s[:, off + k * 259 + 3:off + (k + 1) * 259],
                                start=(k == 0), stop=(k == KD - 1))
                        return inst

                    tdt = P.stage("tensor", mm_dt, waits=P.gwait(bank("A", 512)))
                    uta_tguard[c % 2] = tdt
                if pend is not None:
                    plan_transposes(*pend)
                    pend = None
                if ct >= 17:
                    continue
                ps = pA[:, half * 512:half * 512 + 259]
                ac = acc_s[:, half * 256:half * 256 + 256]
                P.stage("vector",
                        lambda e, ps=ps, ac=ac, ct=ct: e.tensor_scalar(
                            ac, ps[:, 0:256], cw_s[:, ct:ct + 1],
                            cb_s[:, ct:ct + 1], ALU.mult, ALU.add),
                        waits=[S("tensor", tmm), S("scalar", ssl_guard[half])])
                for w in (1, 2, 3):
                    vconv = P.stage("vector",
                                    lambda e, ps=ps, ac=ac, ct=ct, w=w:
                                    e.scalar_tensor_tensor(
                                        ac, ps[:, w:w + 256],
                                        cw_s[:, w * 17 + ct:w * 17 + ct + 1],
                                        ac, ALU.mult, ALU.add))
                P.gset(bank("A", half * 512), "vector", vconv)
                if ct < 16:
                    ssl = P.stage("scalar",
                                  lambda e, half=half: e.activation(
                                      xcv_s[:, half * 256:half * 256 + 256],
                                      acc_s[:, half * 256:half * 256 + 256], AF.Silu),
                                  waits=[S("vector", vconv),
                                         S("tensor", xcv_tguard[half])])
                else:
                    ssl = P.stage("scalar",
                                  lambda e, half=half: e.activation(
                                      bc_s[:, :],
                                      acc_s[:, half * 256:half * 256 + 256], AF.Silu),
                                  waits=[S("vector", vconv), S("tensor", bb_tguard)])
                ssl_guard[half] = ssl
                pend = (ct, half, ssl)

            def mm_dt(e, off=off):
                for k in range(KD):
                    inst = e.matmul(
                        pA[0:16, 512:512 + 256],
                        winx_s[:, k * XBCD + 2176:k * XBCD + 2192],
                        uta_s[:, off + k * 259 + 3:off + (k + 1) * 259],
                        start=(k == 0), stop=(k == KD - 1))
                return inst

            tdt = P.stage("tensor", mm_dt, waits=P.gwait(bank("A", 512)))
            uta_tguard[c % 2] = tdt

            s_sp1 = P.stage("scalar",
                            lambda e: e.activation(t16_s[:, :], pA[0:16, 512:512 + 256],
                                                   AF.Exp, bias=dtb_s[:, 0:1]),
                            waits=[S("tensor", tdt), Q("gpsimd", prev_cdr_q)])
            P.gset(bank("A", 512), "scalar", s_sp1)
            v_sp = P.stage("vector",
                           lambda e: e.tensor_scalar(t16_s[:, :], t16_s[:, :],
                                                     1.0, None, ALU.add),
                           waits=[S("scalar", s_sp1)])
            s_sp2 = P.stage("scalar",
                            lambda e: e.activation(dtv_s[:, :], t16_s[:, :], AF.Ln),
                            waits=[S("vector", v_sp), S("tensor", sm_tr_guard)])
            v_a = P.stage("vector",
                          lambda e: e.tensor_scalar(a_s[:, :], dtv_s[:, :],
                                                    A_s[:, 0:1], None, ALU.mult),
                          waits=[S("scalar", s_sp2), S("vector", vA1)])
            v_cum = P.stage("vector",
                            lambda e: e.tensor_tensor_scan(acum_s[:, :], a_s[:, :],
                                                           zz16_s[:, :], 0.0,
                                                           ALU.add, ALU.add),
                            waits=[S("vector", vz16), S("tensor", sm_tr_guard),
                                   Q("gpsimd", v_acum_readers)])
            v_nac = P.stage("vector",
                            lambda e: e.tensor_scalar(nacum_s[:, :], acum_s[:, :],
                                                      -1.0, None, ALU.mult),
                            waits=[Q("gpsimd", prev_pk1_q)])
            s_ws = P.stage("scalar",
                           lambda e: e.activation(ws_s[:, :], acum_s[:, :], AF.Exp,
                                                  bias=acum_s[:, 255:256],
                                                  scale=ko_s[0:16, 1:2]),
                           waits=[S("vector", v_cum), S("tensor", sm_tr_guard)])
            v_ws = P.stage("vector",
                           lambda e: e.tensor_tensor(ws_s[:, :], ws_s[:, :],
                                                     dtv_s[:, :], ALU.mult),
                           waits=[S("scalar", s_ws)])
            s_cd = P.stage("scalar",
                           lambda e: e.activation(t16_s[:, 0:1], acum_s[:, 255:256],
                                                  AF.Exp),
                           waits=[S("vector", v_cum), S("vector", v_sp)])
            v_al = P.stage("vector",
                           lambda e: e.tensor_tensor(alacc_s[:, :], alacc_s[:, :],
                                                     acum_s[:, 255:256], ALU.add),
                           waits=[S("vector", vz_al)])
            q_cdr = P.dma("gpsimd", cdr_s[0:1, :], t16_s[:, 0:1],
                          waits=[S("scalar", s_cd)])
            prev_cdr_q = q_cdr
            t_cd = P.stage("tensor",
                           lambda e: e.matmul(pC[0:64, 512:528], onec_s[0:1, 0:64],
                                              cdr_s[0:1, 0:16], start=True, stop=True),
                           waits=P.gwait(bank("C", 512)) + [Q("gpsimd", q_cdr)])
            v_cd = P.stage("vector",
                           lambda e, c=c: e.tensor_copy(cdbc_s[:, c * NH:(c + 1) * NH],
                                                        pC[0:64, 512:528]),
                           waits=[S("tensor", t_cd)])
            P.gset(bank("C", 512), "vector", v_cd)

            q_pk1 = P.dma("gpsimd",
                          lh2_s[1:2, :].rearrange("o (h t) -> o h t", h=16),
                          nacum_s[:, :],
                          waits=[S("vector", v_nac), S("tensor", pack_tguard)])
            q_pk2 = P.dma("gpsimd",
                          rh2_s[0:1, :].rearrange("o (h t) -> o h t", h=16),
                          acum_s[:, :],
                          waits=[S("vector", v_cum), S("tensor", pack_tguard)])
            q_pack = q_pk2
            v_acum_readers = q_pk2
            prev_pk1_q = q_pk1

            sm_done = {}
            for name, srcT in (("dtv", dtv_s), ("ws", ws_s), ("acum", acum_s)):
                for qt in range(2):
                    tb = trc % 2
                    wdeps = {"dtv": [S("scalar", s_sp2)],
                             "ws": [S("vector", v_ws)],
                             "acum": [S("vector", v_cum)]}[name]
                    ttr = P.stage("tensor",
                                  lambda e, srcT=srcT, qt=qt, tb=tb: e.transpose(
                                      pB[0:128, tb * 512 + 256:tb * 512 + 256 + 16],
                                      srcT[:, qt * 128:qt * 128 + 128],
                                      idf_s[0:16, 0:16]),
                                  waits=P.gwait(bank("B", tb * 512)) + wdeps)
                    sm_tr_guard = ttr
                    if name == "acum":
                        ev = P.stage("scalar",
                                     lambda e, c=c, qt=qt, tb=tb: e.activation(
                                         eaT_s[:, (c * 2 + qt) * 16:(c * 2 + qt) * 16 + 16],
                                         pB[0:128, tb * 512 + 256:tb * 512 + 272],
                                         AF.Exp),
                                     waits=[S("tensor", ttr)])
                        P.gset(bank("B", tb * 512), "scalar", ev)
                    else:
                        dst = (dtvT_s if name == "dtv" else wsT_s)
                        ev = P.stage("vector",
                                     lambda e, dst=dst, qt=qt, tb=tb: e.tensor_copy(
                                         dst[:, qt * 16:qt * 16 + 16],
                                         pB[0:128, tb * 512 + 256:tb * 512 + 272]),
                                     waits=[S("tensor", ttr)])
                        P.gset(bank("B", tb * 512), "vector", ev)
                    sm_done[(name, qt)] = ev
                    trc += 1

            def mm_sc(e, c=c):
                e.matmul(pC[:, 0:256], bb_s[:, 0:128],
                         Cb_s[:, c * CK:(c + 1) * CK], start=True, stop=True)
                return e.matmul(pC[:, 256:512], bb_s[:, 128:256],
                                Cb_s[:, c * CK:(c + 1) * CK], start=True, stop=True)

            t_sc = P.stage("tensor", mm_sc,
                           waits=P.gwait(bank("C", 0)) + [S("vector", last_bc_v)])
            bb_tguard = t_sc
            v_sc = P.stage("vector",
                           lambda e: e.tensor_tensor(sc_s[:, 0:512], pC[:, 0:512],
                                                     mask_s[:, 0:512], ALU.mult),
                           waits=[S("tensor", t_sc)])
            P.gset(bank("C", 0), "vector", v_sc)

            mt_tguard = [0, 0, 0, 0]
            sexp_guard = [0, 0]
            vd0_sguard = [0, 0]
            last_vya = 0
            last_vst = 0
            pend_mt = None

            def plan_ydiag_states(h0, vmts):
                nonlocal last_vya, last_vst, xt_tguard
                for hh in range(2):
                    h = h0 + hh
                    mo = h % 4 * 384
                    py = pA[:, hh * 512:hh * 512 + 256]

                    def mm_yd(e, h=h, mo=mo, py=py):
                        e.matmul(py[:, 0:128], mt_s[:, mo:mo + 128],
                                 xt_s[:, h * 128:(h + 1) * 128], start=True, stop=True)
                        e.matmul(py[:, 128:256], mt_s[:, mo + 128:mo + 256],
                                 xt_s[:, h * 128:(h + 1) * 128], start=True, stop=False)
                        return e.matmul(py[:, 128:256], mt_s[:, mo + 256:mo + 384],
                                        xt_s[:, DI + h * 128:DI + (h + 1) * 128],
                                        start=False, stop=True)

                    t_yd = P.stage("tensor", mm_yd,
                                   waits=P.gwait(bank("A", hh * 512)) +
                                   [S("vector", vmts[hh])])
                    mt_tguard[h % 4] = t_yd
                    xt_tguard = t_yd
                    v_ya = P.stage("vector",
                                   lambda e, h=h, py=py: e.scalar_tensor_tensor(
                                       ya_s[:, :].rearrange("p (b x) -> p b x", x=DI)
                                       [:, :, h * 128:(h + 1) * 128],
                                       xt_s[:, :].rearrange("p (b x) -> p b x", x=DI)
                                       [:, :, h * 128:(h + 1) * 128],
                                       dpbc_s[:, h:h + 1],
                                       py.rearrange("p (b x) -> p b x", x=128),
                                       ALU.mult, ALU.add),
                                   waits=[S("tensor", t_yd), S("vector", vDp),
                                          Q("gpsimd", ya_qd[c - 1] if c >= 1 else 0, 1)])
                    P.gset(bank("A", hh * 512), "vector", v_ya)
                    last_vya = v_ya

                    v_wb = P.stage("vector",
                                   lambda e, h=h: (
                                       e.tensor_scalar(wbt_s[:, 0:64], bt_s[:, 0:64],
                                                       wsT_s[:, h:h + 1], None, ALU.mult),
                                       e.tensor_scalar(wbt_s[:, 64:128], bt_s[:, 64:128],
                                                       wsT_s[:, 16 + h:16 + h + 1],
                                                       None, ALU.mult))[-1],
                                   waits=[S("vector", sm_done[("ws", 1)]),
                                          S("vector", last_vst),
                                          S("tensor", P.cnt["tensor"])])

                    pst = pB[0:64, hh * 512:hh * 512 + 128]

                    def mm_st(e, h=h, pst=pst):
                        e.matmul(pst, wbt_s[:, 0:64], xt_s[:, h * 128:(h + 1) * 128],
                                 start=True, stop=False)
                        return e.matmul(pst, wbt_s[:, 64:128],
                                        xt_s[:, DI + h * 128:DI + (h + 1) * 128],
                                        start=False, stop=True)

                    t_st = P.stage("tensor", mm_st,
                                   waits=P.gwait(bank("B", hh * 512)) +
                                   [S("vector", v_wb)])
                    xt_tguard = t_st
                    v_st = P.stage("vector",
                                   lambda e, h=h, pst=pst, c=c: (
                                       e.tensor_copy(ss_s[:, h * 128:(h + 1) * 128], pst),
                                       e.scalar_tensor_tensor(
                                           prev_s[:, h * 128:(h + 1) * 128],
                                           prev_s[:, h * 128:(h + 1) * 128],
                                           cdbc_s[:, c * NH + h:c * NH + h + 1],
                                           pst, ALU.mult, ALU.add))[-1],
                                   waits=[S("tensor", t_st), S("vector", v_cd),
                                          S("vector", vz_pr),
                                          Q("gpsimd", ss_qd[c - 1] if c >= 1 else 0, 2)])
                    P.gset(bank("B", hh * 512), "vector", v_st)
                    last_vst = v_st

            for pr in range(NH // 2):
                h0 = 2 * pr
                pp = pr % 2
                pt = pC if pp == 0 else pD
                tn = "C" if pp == 0 else "D"
                ptv = pt[:, :].rearrange("p (b x) -> p b x", x=512)[:, :, 0:384]

                def mm_diff(e, h0=h0, pt=pt):
                    for hh in range(2):
                        h = h0 + hh
                        e.matmul(pt[:, hh * 512:hh * 512 + 256],
                                 lh2_s[0:2, h * 256:h * 256 + 128],
                                 rh2_s[0:2, h * 256:h * 256 + 256],
                                 start=True, stop=True)
                        inst = e.matmul(pt[:, hh * 512 + 256:hh * 512 + 384],
                                        lh2_s[0:2, h * 256 + 128:h * 256 + 256],
                                        rh2_s[0:2, h * 256 + 128:h * 256 + 256],
                                        start=True, stop=True)
                    return inst

                t_df = P.stage("tensor", mm_diff,
                               waits=P.gwait(bank(tn, 0), bank(tn, 512)) +
                               [Q("gpsimd", q_pack), Q("sync", q_wx)])
                pack_tguard = t_df
                if pend_mt is not None:
                    plan_ydiag_states(*pend_mt)
                    pend_mt = None
                v_d0 = P.stage("vector",
                               lambda e, ptv=ptv, pp=pp: e.tensor_scalar(
                                   d0_s[:, pp * 768:pp * 768 + 768]
                                   .rearrange("p (b x) -> p b x", x=384),
                                   ptv, 0.0, None, ALU.min),
                               waits=[S("tensor", t_df), S("scalar", vd0_sguard[pp])])
                P.gset(bank(tn, 0), "vector", v_d0)
                P.gset(bank(tn, 512), "vector", v_d0)
                s_ex = P.stage("scalar",
                               lambda e, pp=pp: e.activation(
                                   e_s[:, pp * 768:pp * 768 + 768],
                                   d0_s[:, pp * 768:pp * 768 + 768], AF.Exp),
                               waits=[S("vector", v_d0), S("vector", sexp_guard[pp])])
                vd0_sguard[pp] = s_ex
                vmts = []
                for hh in range(2):
                    h = h0 + hh
                    eo = pp * 768 + hh * 384
                    v_mt = P.stage("vector",
                                   lambda e, h=h, eo=eo: (
                                       e.scalar_tensor_tensor(
                                           mt_s[:, h % 4 * 384:h % 4 * 384 + 256],
                                           e_s[:, eo:eo + 256],
                                           dtvT_s[:, h:h + 1], sc_s[:, 0:256],
                                           ALU.mult, ALU.mult),
                                       e.scalar_tensor_tensor(
                                           mt_s[:, h % 4 * 384 + 256:h % 4 * 384 + 384],
                                           e_s[:, eo + 256:eo + 384],
                                           dtvT_s[:, 16 + h:16 + h + 1], sc_s[:, 384:512],
                                           ALU.mult, ALU.mult))[-1],
                                   waits=[S("scalar", s_ex), S("vector", v_sc),
                                          S("vector", sm_done[("dtv", 1)]),
                                          S("tensor", mt_tguard[h % 4])])
                    vmts.append(v_mt)
                sexp_guard[pp] = vmts[-1]
                pend_mt = (h0, vmts)
            plan_ydiag_states(*pend_mt)
            pend_mt = None

            ya_qd[c] = P.dma("gpsimd",
                             yacch[c * 2:(c + 1) * 2, :, :].rearrange("a q d -> q a d"),
                             ya_s[:, :].rearrange("q (a d) -> q a d", a=2),
                             waits=[S("vector", last_vya)], ch=1)
            ss_qd[c] = P.dma("gpsimd", sh[c, :, :], ss_s[:, :],
                             waits=[S("vector", last_vst)], ch=2)

        q_cc1 = P.dma("gpsimd", ccin[0:64, :], prev_s[:, :],
                      waits=[S("vector", P.cnt["vector"])])
        P.dma("gpsimd", alrow_s[0:1, 0:16], alacc_s[:, 0:1],
              waits=[S("vector", P.cnt["vector"])])
        q_cc2 = P.dma("gpsimd", ccin[64:65, :], alrow_s[:, :])
        g_cc = P.stage("gpsimd",
                       lambda e: e.collective_compute(
                           "AllGather", ALU.bypass,
                           replica_groups=[[0, 1, 2, 3], [4, 5, 6, 7]],
                           ins=[ccin.opt()], outs=[ccout.opt()]),
                       waits=[Q("gpsimd", max(q_cc1, q_cc2))])

    P.barrier()

    with contextlib.ExitStack() as scopeB:
        def sbb(name, shape, dt):
            return scopeB.enter_context(nc.sbuf_tensor(name, list(shape), dt))

        winz_s = sbb("winz_s", (128, KD * DI), BF16)
        utb_s = sbb("utb_s", (128, 2 * KD * 256), BF16)
        wout_s = sbb("wout_s", (128, 16 * 1024), BF16)
        sz_s = sbb("sz_s", (128, DI), F32)
        y_s = sbb("y_s", (128, DI), F32)
        g_s = sbb("g_s", (128, DI), F32)
        gt_s = sbb("gt_s", (128, DI), BF16)
        ur_s = sbb("ur_s", (128, 2 * 2 * D), F32)
        yin_s = sbb("yin_s", (128, 2 * 2 * DI), BF16)
        sin_s = sbb("sin_s", (DS, 2 * NH * HD), BF16)
        xpre_s = sbb("xpre_s", (128, D), F32)
        col_s = sbb("col_s", (128, 8), F32)
        alogs_s = sbb("alogs_s", (4, 16), F32)
        coef_s = sbb("coef_s", (4, 16), F32)
        coefr_s = sbb("coefr_s", (1, 16), F32)
        coefbc_s = sbb("coefbc_s", (DS, 4 * 16), F32)

        P.dma("sync", winz_s[:, :].rearrange("p (k c) -> p k c", k=KD), winzv[:, :, :])
        P.dma("sync", wout_s[:, :].rearrange("p (k d) -> p k d", k=16), woutv[:, :, :])
        q_wzo = P.qchan("sync", 0)

        g_after_cc = P.cnt["gpsimd"]
        for j in range(4):
            P.dma("gpsimd", alogs_s[j:j + 1, :], ccout[65 * j + 64:65 * j + 65, 0:16],
                  waits=[S("gpsimd", g_after_cc)])
        q_al = P.qchan("gpsimd", 0)
        t_co = P.stage("tensor",
                       lambda e: e.matmul(pC[0:4, 0:16], pmat_s[:, :], alogs_s[:, :],
                                          start=True, stop=True),
                       waits=P.gwait(bank("C", 0)) + [Q("gpsimd", q_al)])
        s_co = P.stage("scalar",
                       lambda e: e.activation(coef_s[:, :], pC[0:4, 0:16], AF.Exp),
                       waits=[S("tensor", t_co)])
        P.gset(bank("C", 0), "scalar", s_co)
        v_co = P.stage("vector",
                       lambda e: e.tensor_scalar(coef_s[:, :], coef_s[:, :],
                                                 mvec_s[:, 0:1], None, ALU.mult),
                       waits=[S("scalar", s_co)])
        v_pz = P.stage("vector", lambda e: e.memset(prev_s[:, :], 0.0),
                       waits=[Q("gpsimd", q_al)])
        prev_v = v_pz
        q_sj_prev = 0
        for j in range(4):
            q_cr = P.dma("gpsimd", coefr_s[:, :], coef_s[j:j + 1, :],
                         waits=[S("vector", v_co),
                                S("tensor", P.cnt["tensor"])])
            t_bc = P.stage("tensor",
                           lambda e: e.matmul(pC[0:64, 32:48], onec_s[0:1, 0:64],
                                              coefr_s[0:1, :], start=True, stop=True),
                           waits=P.gwait(bank("C", 0)) + [Q("gpsimd", q_cr)])
            v_bc = P.stage("vector",
                           lambda e, j=j: e.tensor_copy(
                               coefbc_s[:, j * 16:(j + 1) * 16], pC[0:64, 32:48]),
                           waits=[S("tensor", t_bc)])
            P.gset(bank("C", 0), "vector", v_bc)
            q_sj = P.dma("gpsimd", y_s[0:64, 0:2048], ccout[65 * j:65 * j + 64, :],
                         waits=[S("gpsimd", g_after_cc), S("vector", prev_v)])

            def mk_carry(j=j):
                def body(e):
                    for h in range(NH):
                        inst = e.scalar_tensor_tensor(
                            prev_s[:, h * 128:(h + 1) * 128],
                            y_s[0:64, h * 128:(h + 1) * 128],
                            coefbc_s[:, j * 16 + h:j * 16 + h + 1],
                            prev_s[:, h * 128:(h + 1) * 128], ALU.mult, ALU.add)
                    return inst
                return body

            prev_v = P.stage("vector", mk_carry(),
                             waits=[S("vector", max(prev_v, v_bc)), Q("gpsimd", q_sj)])
        prevb_v = P.stage("vector",
                          lambda e: e.tensor_copy(prevb_s[:, :], prev_s[:, :]),
                          waits=[S("vector", prev_v)])

        def utb_dst(c):
            o = (c % 2) * (KD * 256)
            return utb_s[:, o:o + KD * 256].rearrange("p (k t) -> p k t", k=KD)

        q_utb = [0] * NCH
        q_yin = [0] * NCH
        q_sin = [0] * NCH
        q_ur = [0] * NCH
        q_utb[0] = P.dma("sync", utb_dst(0), uTv[:, :, 3:259], ch=5)
        q_yin[0] = P.dma("gpsimd",
                         yin_s[:, 0:2 * DI].rearrange("q (a d) -> q a d", a=2),
                         yacch[0:2, :, :].rearrange("a q d -> q a d"),
                         waits=[Q("gpsimd", P.qchan("gpsimd", 1), 1)], ch=3)
        q_sin[0] = P.dma("gpsimd", sin_s[:, 0:NH * HD], sh[0, :, :],
                         waits=[Q("gpsimd", P.qchan("gpsimd", 2), 2)], ch=5)
        q_ur[0] = P.dma("sync",
                        ur_s[:, 0:2 * D].rearrange("p (a d) -> p a d", a=2),
                        ures[0:256, :].rearrange("(a p) d -> p a d", p=128), ch=7)

        utb_tguard = [0, 0]
        yin_vguard = [0, 0]
        sin_vguard = [0, 0]
        ur_vguard = [0, 0]
        v_g_prev = 0
        s_sq_prev = 0
        s_xq_prev = 0
        v_xp_prev = 0
        v_x1_prev = 0
        t_gt_prev = 0
        t_op_prev = 0
        for c in range(NCH):
            par = c % 2
            if c + 1 < NCH:
                np_ = (c + 1) % 2
                q_utb[c + 1] = P.dma("sync", utb_dst(c + 1),
                                     uTv[:, :, (c + 1) * 256 + 3:(c + 1) * 256 + 259],
                                     waits=[S("tensor", utb_tguard[np_])], ch=5 + np_)
                q_yin[c + 1] = P.dma("gpsimd",
                                     yin_s[:, np_ * 2 * DI:(np_ + 1) * 2 * DI]
                                     .rearrange("q (a d) -> q a d", a=2),
                                     yacch[(c + 1) * 2:(c + 2) * 2, :, :]
                                     .rearrange("a q d -> q a d"),
                                     waits=[S("vector", yin_vguard[np_])], ch=3 + np_)
                q_sin[c + 1] = P.dma("gpsimd",
                                     sin_s[:, np_ * NH * HD:(np_ + 1) * NH * HD],
                                     sh[c + 1, :, :],
                                     waits=[S("vector", sin_vguard[np_])], ch=5 + np_)
                q_ur[c + 1] = P.dma("sync",
                                    ur_s[:, np_ * 2 * D:(np_ + 1) * 2 * D]
                                    .rearrange("p (a d) -> p a d", a=2),
                                    ures[(c + 1) * 256:(c + 2) * 256, :]
                                    .rearrange("(a p) d -> p a d", p=128),
                                    waits=[S("vector", ur_vguard[np_])], ch=7 + np_)

            for qt in range(2):
                last_sz = 0
                for eb in range(4):
                    ebp = eb % 2

                    def mm_z(e, qt=qt, eb=eb, par=par, ebp=ebp):
                        for k in range(KD):
                            inst = e.matmul(
                                pA[:, ebp * 512:ebp * 512 + 512],
                                utb_s[:, par * KD * 256 + k * 256 + qt * 128:
                                      par * KD * 256 + k * 256 + qt * 128 + 128],
                                winz_s[:, k * DI + eb * 512:k * DI + (eb + 1) * 512],
                                start=(k == 0), stop=(k == KD - 1))
                        return inst

                    t_z = P.stage("tensor", mm_z,
                                  waits=P.gwait(bank("A", ebp * 512)) +
                                  [Q("sync", q_utb[c], 5 + par), Q("sync", q_wzo)])
                    utb_tguard[par] = t_z
                    s_z = P.stage("scalar",
                                  lambda e, eb=eb, ebp=ebp: e.activation(
                                      sz_s[:, eb * 512:(eb + 1) * 512],
                                      pA[:, ebp * 512:ebp * 512 + 512], AF.Silu),
                                  waits=[S("tensor", t_z), S("vector", v_g_prev)])
                    P.gset(bank("A", ebp * 512), "scalar", s_z)
                    last_sz = s_z

                def mm_yo(e, qt=qt, c=c):
                    for seg in range(4):
                        pt = pB if seg < 2 else pC
                        off = (seg % 2) * 512
                        inst = e.matmul(
                            pt[:, off:off + 512],
                            Cb_s[:, c * CK + qt * 128:c * CK + qt * 128 + 128],
                            prevb_s[:, seg * 512:(seg + 1) * 512],
                            start=True, stop=True)
                    return inst

                t_yo = P.stage("tensor", mm_yo,
                               waits=P.gwait(bank("B", 0), bank("B", 512),
                                             bank("C", 0), bank("C", 512)) +
                               [S("vector", prevb_v)])
                last_vy = 0
                for h in range(NH):
                    seg = h // 4
                    pt = pB if seg < 2 else pC
                    off = (seg % 2) * 512 + (h % 4) * 128
                    v_y = P.stage("vector",
                                  lambda e, h=h, qt=qt, c=c, par=par, pt=pt, off=off:
                                  e.scalar_tensor_tensor(
                                      y_s[:, h * 128:(h + 1) * 128],
                                      pt[:, off:off + 128],
                                      eaT_s[:, (c * 2 + qt) * 16 + h:(c * 2 + qt) * 16 + h + 1],
                                      yin_s[:, par * 2 * DI + qt * DI + h * 128:
                                            par * 2 * DI + qt * DI + (h + 1) * 128],
                                      ALU.mult, ALU.add),
                                  waits=[S("tensor", t_yo),
                                         Q("gpsimd", q_yin[c], 3 + par),
                                         S("scalar", s_sq_prev)])
                    last_vy = v_y
                for tn2 in ("B", "C"):
                    P.gset(bank(tn2, 0), "vector", last_vy)
                    P.gset(bank(tn2, 512), "vector", last_vy)
                if qt == 1:
                    yin_vguard[par] = last_vy

                v_g = P.stage("vector",
                              lambda e: e.tensor_tensor(g_s[:, :], y_s[:, :],
                                                        sz_s[:, :], ALU.mult),
                              waits=[S("vector", last_vy), S("scalar", last_sz),
                                     S("scalar", s_sq_prev),
                                     S("tensor", t_gt_prev)])
                v_g_prev = v_g
                s_sq = P.stage("scalar",
                               lambda e, qt=qt: e.activation(
                                   y_s[:, :], g_s[:, :], AF.Square,
                                   accum_out=sumsq_s[:, qt:qt + 1]),
                               waits=[S("vector", v_g)])
                s_sq_prev = s_sq
                last_gt = 0
                for et in range(16):
                    ep = et % 2
                    t_gt = P.stage("tensor",
                                   lambda e, et=et, ep=ep: e.transpose(
                                       pC[:, ep * 512:ep * 512 + 128],
                                       g_s[:, et * 128:(et + 1) * 128], idf_s[:, :]),
                                   waits=P.gwait(bank("C", ep * 512)) +
                                   [S("vector", v_g)])
                    v_gt = P.stage("vector",
                                   lambda e, et=et, ep=ep: e.tensor_scalar(
                                       gt_s[:, et * 128:(et + 1) * 128],
                                       pC[:, ep * 512:ep * 512 + 128],
                                       nw_s[:, et:et + 1], None, ALU.mult),
                                   waits=[S("tensor", t_gt), S("tensor", t_op_prev)])
                    P.gset(bank("C", ep * 512), "vector", v_gt)
                    last_gt = v_gt
                t_gt_prev = P.cnt["tensor"]

                def mm_op(e):
                    for db in range(2):
                        for k in range(16):
                            inst = e.matmul(
                                pD[:, db * 512:(db + 1) * 512],
                                gt_s[:, k * 128:(k + 1) * 128],
                                wout_s[:, k * 1024 + db * 512:k * 1024 + (db + 1) * 512],
                                start=(k == 0), stop=(k == 15))
                    return inst

                t_op = P.stage("tensor", mm_op,
                               waits=P.gwait(bank("D", 0), bank("D", 512)) +
                               [S("vector", last_gt), Q("sync", q_wzo)])
                t_op_prev = t_op

                s_rm1 = P.stage("scalar",
                                lambda e, qt=qt: e.activation(
                                    col_s[:, 0:1], sumsq_s[:, qt:qt + 1], AF.Ln,
                                    bias=ko_s[:, 0:1], scale=ko_s[:, 3:4]),
                                waits=[S("scalar", s_sq), S("vector", v_x1_prev)])
                s_rm2 = P.stage("scalar",
                                lambda e: e.activation(col_s[:, 1:2], col_s[:, 0:1],
                                                       AF.Exp, scale=ko_s[:, 2:3]),
                                waits=[S("scalar", s_rm1)])
                v_xp = P.stage("vector",
                               lambda e, qt=qt, par=par: e.scalar_tensor_tensor(
                                   xpre_s[:, :], pD[:, 0:1024], col_s[:, 1:2],
                                   ur_s[:, par * 2 * D + qt * D:par * 2 * D + (qt + 1) * D],
                                   ALU.mult, ALU.add, accum_out=col_s[:, 2:3]),
                               waits=[S("tensor", t_op), S("scalar", s_rm2),
                                      Q("sync", q_ur[c], 7 + par),
                                      S("scalar", s_xq_prev)])
                P.gset(bank("D", 0), "vector", v_xp)
                P.gset(bank("D", 512), "vector", v_xp)
                v_xp_prev = v_xp
                if qt == 1:
                    ur_vguard[par] = v_xp
                s_xq = P.stage("scalar",
                               lambda e: e.activation(sz_s[:, 0:1024], xpre_s[:, :],
                                                      AF.Square,
                                                      accum_out=col_s[:, 3:4]),
                               waits=[S("vector", v_xp)])
                s_xq_prev = s_xq
                P.stage("vector",
                        lambda e: e.tensor_scalar(col_s[:, 4:5], col_s[:, 2:3],
                                                  1.0 / D, None, ALU.mult),
                        waits=[S("scalar", s_xq)])
                P.stage("vector",
                        lambda e: e.tensor_scalar(col_s[:, 5:6], col_s[:, 3:4],
                                                  1.0 / D, EPS, ALU.mult, ALU.add))
                P.stage("vector",
                        lambda e: e.tensor_scalar(col_s[:, 6:7], col_s[:, 4:5],
                                                  col_s[:, 4:5], None, ALU.mult))
                v_c1 = P.stage("vector",
                               lambda e: e.tensor_tensor(col_s[:, 5:6], col_s[:, 5:6],
                                                         col_s[:, 6:7], ALU.subtract))
                s_l1 = P.stage("scalar",
                               lambda e: e.activation(col_s[:, 6:7], col_s[:, 5:6],
                                                      AF.Ln),
                               waits=[S("vector", v_c1)])
                s_l2 = P.stage("scalar",
                               lambda e: e.activation(col_s[:, 7:8], col_s[:, 6:7],
                                                      AF.Exp, scale=ko_s[:, 2:3]),
                               waits=[S("scalar", s_l1)])
                v_x1 = P.stage("vector",
                               lambda e, c=c, qt=qt: e.tensor_scalar(
                                   x1_s[:, (c * 2 + qt) * D:(c * 2 + qt + 1) * D],
                                   xpre_s[:, :], col_s[:, 4:5], col_s[:, 7:8],
                                   ALU.subtract, ALU.mult),
                               waits=[S("scalar", s_l2)])
                v_x1_prev = v_x1

            def mk_pu(c=c, par=par):
                def body(e):
                    for h in range(NH):
                        inst = e.scalar_tensor_tensor(
                            prev_s[:, h * 128:(h + 1) * 128],
                            prev_s[:, h * 128:(h + 1) * 128],
                            cdbc_s[:, c * NH + h:c * NH + h + 1],
                            sin_s[:, par * NH * HD + h * 128:par * NH * HD + (h + 1) * 128],
                            ALU.mult, ALU.add)
                    return inst
                return body

            v_pu = P.stage("vector", mk_pu(),
                           waits=[S("tensor", P.cnt["tensor"]),
                                  Q("gpsimd", q_sin[c], 5 + par)])
            sin_vguard[par] = v_pu
            prevb_v = P.stage("vector",
                              lambda e: e.tensor_copy(prevb_s[:, :], prev_s[:, :]),
                              waits=[S("vector", v_pu)])

    P.barrier()

    with contextlib.ExitStack() as scopeF:
        def sbf(name, shape, dt):
            return scopeF.enter_context(nc.sbuf_tensor(name, list(shape), dt))

        xf_s = sbf("xf_s", (128, KD * HALF), BF16)
        h_s = sbf("h_s", (128, KH * HALF), BF16)
        w13_s = sbf("w13_s", (128, 2 * 2 * KD * 256), BF16)
        w2b_s = sbf("w2b_s", (128, 2 * 4 * 1024), BF16)
        yf_s = sbf("yf_s", (128, 8 * D), F32)
        sl_s = sbf("sl_s", (128, 1024), F32)
        comb_s = sbf("comb_s", (128, 64), F32)
        sm_s = sbf("sm_s", (128, 8), F32)
        m8_s = sbf("m8_s", (128, 8), F32)
        wrk_s = sbf("wrk_s", (128, 8), F32)
        colf_s = sbf("colf_s", (128, 8), F32)
        xw_s = sbf("xw_s", (128, D), F32)
        x2_s = sbf("x2_s", (128, D), F32)

        def w13_dst(buf, mat):
            o = (buf * 2 + mat) * KD * 256
            return w13_s[:, o:o + KD * 256].rearrange("p (k t) -> p k t", k=KD)

        def w13_sl(buf, mat, k, lo, sz):
            o = (buf * 2 + mat) * KD * 256
            return w13_s[:, o + k * 256 + lo:o + k * 256 + lo + sz]

        def w2_dst(buf, nk):
            o = buf * 4 * 1024
            return w2b_s[:, o:o + nk * 1024].rearrange("p (k t) -> p k t", k=nk)

        w13_tguard = [0, 0]
        w2_tguard = [0, 0]
        vh_guard = 0
        s_si_prev = 0
        v_h_prev = 0
        s_g_prev = 0
        v_x2a_sguard = 0
        q_out_prev = 0
        xf_tguard = 0
        last_h_v = 0

        for hf in range(2):
            last_xf = 0
            trc = 0
            for tt in range(8):
                gtile = hf * 8 + tt
                for k in range(KD):
                    tb = trc % 2
                    t_tx = P.stage("tensor",
                                   lambda e, gtile=gtile, k=k, tb=tb: e.transpose(
                                       pA[:, tb * 512:tb * 512 + 64].bitcast(BF16),
                                       x1_s[:, gtile * D + k * 128:gtile * D + (k + 1) * 128],
                                       idb_s[:, :]),
                                   waits=P.gwait(bank("A", tb * 512)))
                    v_tx = P.stage("vector",
                                   lambda e, tt=tt, k=k, tb=tb: e.tensor_copy(
                                       xf_s[:, k * HALF + tt * 128:k * HALF + (tt + 1) * 128],
                                       pA[:, tb * 512:tb * 512 + 64].bitcast(BF16)),
                                   waits=[S("tensor", t_tx), S("tensor", xf_tguard)])
                    P.gset(bank("A", tb * 512), "vector", v_tx)
                    last_xf = v_tx
                    trc += 1

            for tt in range(8):
                def mm_g(e, tt=tt):
                    for k in range(KD):
                        inst = e.matmul(
                            pC[:, 0:8],
                            xf_s[:, k * HALF + tt * 128:k * HALF + (tt + 1) * 128],
                            gatet_s[:, k * 8:(k + 1) * 8],
                            start=(k == 0), stop=(k == KD - 1))
                    return inst

                t_g = P.stage("tensor", mm_g,
                              waits=P.gwait(bank("C", 0)) + [S("vector", last_xf)])
                v_g1 = P.stage("vector",
                               lambda e: e.reduce_max(colf_s[:, 0:1], pC[:, 0:8],
                                                      AXX, negate=True),
                               waits=[S("tensor", t_g), S("scalar", s_g_prev)])
                s_g1 = P.stage("scalar",
                               lambda e: e.activation(sm_s[:, :], pC[:, 0:8], AF.Exp,
                                                      bias=colf_s[:, 0:1],
                                                      accum_out=colf_s[:, 1:2]),
                               waits=[S("vector", v_g1)])
                s_g_prev = s_g1
                P.gset(bank("C", 0), "scalar", s_g1)
                P.stage("vector",
                        lambda e: e.reciprocal(colf_s[:, 2:3], colf_s[:, 1:2]),
                        waits=[S("scalar", s_g1)])
                P.stage("vector",
                        lambda e: e.tensor_scalar(sm_s[:, :], sm_s[:, :],
                                                  colf_s[:, 2:3], None, ALU.mult))
                P.stage("vector", lambda e: e.max(out=m8_s[:, :], in_=sm_s[:, :]))
                P.stage("vector", lambda e: e.memset(m8_s[:, 2:8], NEG))
                P.stage("vector",
                        lambda e: e.match_replace(out=wrk_s[:, :],
                                                  in_to_replace=m8_s[:, :],
                                                  in_values=sm_s[:, :],
                                                  imm_value=NEG))
                P.stage("vector",
                        lambda e: e.tensor_single_scalar(wrk_s[:, :], wrk_s[:, :],
                                                         NEG, ALU.is_equal))
                v_g2 = P.stage("vector",
                               lambda e, tt=tt: e.tensor_tensor(
                                   comb_s[:, tt * 8:(tt + 1) * 8],
                                   wrk_s[:, :], sm_s[:, :], ALU.mult))
            last_comb = P.cnt["vector"]

            v_yz = P.stage("vector", lambda e: e.memset(yf_s[:, :], 0.0))

            for ei in list(range(8)) + ["shared"]:
                nht = KH if ei != "shared" else KSH
                ngrp = (nht + 1) // 2

                def w13_load(g, buf, ei=ei, tg=None):
                    tgv = w13_tguard[buf] if tg is None else tg
                    a, b = (sw1v, sw3v) if ei == "shared" else (w1v[ei], w3v[ei])
                    q1 = P.dma("sync", w13_dst(buf, 0),
                               a[:, :, g * 256:g * 256 + 256],
                               waits=[S("tensor", tgv)], ch=1 + buf)
                    q3 = P.dma("sync", w13_dst(buf, 1),
                               b[:, :, g * 256:g * 256 + 256], ch=1 + buf)
                    return q3

                qld = [0] * ngrp
                qld[0] = w13_load(0, 0)
                if ngrp > 1:
                    qld[1] = w13_load(1, 1)
                pab = 0
                for g in range(ngrp):
                    for hi in range(2):
                        ht = g * 2 + hi
                        if ht >= nht:
                            break
                        if pab % 2 == 0:
                            pt1, pt3, n1, n3 = pA, pB, "A", "B"
                        else:
                            pt1, pt3, n1, n3 = pC, pD, "C", "D"

                        def mm_h(e, hi=hi, g=g, pt1=pt1, pt3=pt3):
                            buf = g % 2
                            for mat, pt in ((0, pt1), (1, pt3)):
                                for k in range(KD):
                                    for blk in range(2):
                                        inst = e.matmul(
                                            pt[:, blk * 512:blk * 512 + 512],
                                            w13_sl(buf, mat, k, hi * 128, 128),
                                            xf_s[:, k * HALF + blk * 512:k * HALF + (blk + 1) * 512],
                                            start=(k == 0), stop=(k == KD - 1),
                                            skip_group_check=True)
                            return inst

                        t_mh = P.stage("tensor", mm_h,
                                       waits=P.gwait(bank(n1, 0), bank(n1, 512),
                                                     bank(n3, 0), bank(n3, 512)) +
                                       [Q("sync", qld[g], 1 + g % 2),
                                        S("vector", last_xf)])
                        s_si = P.stage("scalar",
                                       lambda e, pt1=pt1: e.activation(
                                           sl_s[:, 0:1024], pt1[:, 0:1024], AF.Silu),
                                       waits=[S("tensor", t_mh),
                                              S("vector", v_h_prev)])
                        P.gset(bank(n1, 0), "scalar", s_si)
                        P.gset(bank(n1, 512), "scalar", s_si)
                        s_si_prev = s_si
                        v_h = P.stage("vector",
                                      lambda e, ht=ht, pt3=pt3: e.tensor_tensor(
                                          h_s[:, ht * HALF:(ht + 1) * HALF],
                                          sl_s[:, 0:1024], pt3[:, 0:1024], ALU.mult),
                                      waits=[S("scalar", s_si),
                                             S("tensor", vh_guard)])
                        P.gset(bank(n3, 0), "vector", v_h)
                        P.gset(bank(n3, 512), "vector", v_h)
                        v_h_prev = v_h
                        pab += 1
                    if g + 2 < ngrp:
                        qld[g + 2] = w13_load(g + 2, g % 2, tg=P.cnt["tensor"])
                w13_tguard[0] = P.cnt["tensor"]
                w13_tguard[1] = P.cnt["tensor"]
                last_h_v = v_h_prev

                kgroups = []
                k0 = 0
                while k0 < nht:
                    kgroups.append((k0, min(4, nht - k0)))
                    k0 += 4
                for th in range(2):
                    qw2 = [0] * len(kgroups)
                    qw2[0] = P.dma("sync", w2_dst(0, kgroups[0][1]),
                                   (sw2v if ei == "shared" else w2v[ei])
                                   [:, kgroups[0][0]:kgroups[0][0] + kgroups[0][1], :],
                                   waits=[S("tensor", w2_tguard[0])], ch=3)
                    if len(kgroups) > 1:
                        qw2[1] = P.dma("sync", w2_dst(1, kgroups[1][1]),
                                       (sw2v if ei == "shared" else w2v[ei])
                                       [:, kgroups[1][0]:kgroups[1][0] + kgroups[1][1], :],
                                       waits=[S("tensor", w2_tguard[1])], ch=4)
                    pts = (pA, pB, pC, pD)
                    for gi, (k0g, nk) in enumerate(kgroups):
                        buf = gi % 2

                        def mm_y(e, gi=gi, k0g=k0g, nk=nk, buf=buf, th=th, nht=nht):
                            for tt4 in range(4):
                                tt = th * 4 + tt4
                                pt = pts[tt4]
                                for ki in range(nk):
                                    k = k0g + ki
                                    st = (k == 0)
                                    sp = (k == nht - 1)
                                    e.matmul(pt[:, 0:512],
                                             h_s[:, k * HALF + tt * 128:k * HALF + (tt + 1) * 128],
                                             w2b_s[:, buf * 4096 + ki * 1024:buf * 4096 + ki * 1024 + 512],
                                             start=st, stop=sp,
                                             skip_group_check=True)
                                    inst = e.matmul(pt[:, 512:1024],
                                                    h_s[:, k * HALF + tt * 128:k * HALF + (tt + 1) * 128],
                                                    w2b_s[:, buf * 4096 + ki * 1024 + 512:buf * 4096 + (ki + 1) * 1024],
                                                    start=st, stop=sp,
                                                    skip_group_check=True)
                            return inst

                        gw = []
                        if gi == 0:
                            for pt_ in ("A", "B", "C", "D"):
                                gw += P.gwait(bank(pt_, 0), bank(pt_, 512))
                            gw += [S("vector", last_h_v)]
                        t_my = P.stage("tensor", mm_y,
                                       waits=gw + [Q("sync", qw2[gi], 3 + gi % 2)])
                        w2_tguard[buf] = t_my
                        vh_guard = t_my
                        if gi + 2 < len(kgroups):
                            qw2[gi + 2] = P.dma(
                                "sync", w2_dst(buf, kgroups[gi + 2][1]),
                                (sw2v if ei == "shared" else w2v[ei])
                                [:, kgroups[gi + 2][0]:kgroups[gi + 2][0] + kgroups[gi + 2][1], :],
                                waits=[S("tensor", t_my)], ch=3 + buf)
                    for tt4 in range(4):
                        tt = th * 4 + tt4
                        pt = pts[tt4]
                        tn = "ABCD"[tt4]
                        if ei == "shared":
                            v_yy = P.stage("vector",
                                           lambda e, tt=tt, pt=pt: e.scalar_tensor_tensor(
                                               yf_s[:, tt * D:(tt + 1) * D],
                                               pt[:, 0:1024], 1.0,
                                               yf_s[:, tt * D:(tt + 1) * D],
                                               ALU.mult, ALU.add),
                                           waits=[S("tensor", t_my),
                                                  S("vector", max(v_yz, last_comb))])
                        else:
                            v_yy = P.stage("vector",
                                           lambda e, tt=tt, pt=pt, ei=ei: e.scalar_tensor_tensor(
                                               yf_s[:, tt * D:(tt + 1) * D],
                                               pt[:, 0:1024],
                                               comb_s[:, tt * 8 + ei:tt * 8 + ei + 1],
                                               yf_s[:, tt * D:(tt + 1) * D],
                                               ALU.mult, ALU.add),
                                           waits=[S("tensor", t_my),
                                                  S("vector", max(v_yz, last_comb))])
                        P.gset(bank(tn, 0), "vector", v_yy)
                        P.gset(bank(tn, 512), "vector", v_yy)

            for tt in range(8):
                gtile = hf * 8 + tt
                v_x2a = P.stage("vector",
                                lambda e, tt=tt, gtile=gtile: e.scalar_tensor_tensor(
                                    xw_s[:, :], x1_s[:, gtile * D:(gtile + 1) * D],
                                    1.0, yf_s[:, tt * D:(tt + 1) * D],
                                    ALU.mult, ALU.add, accum_out=colf_s[:, 3:4]),
                                waits=[S("scalar", v_x2a_sguard)])
                s_x2 = P.stage("scalar",
                               lambda e: e.activation(sl_s[:, 0:1024], xw_s[:, :],
                                                      AF.Square,
                                                      accum_out=colf_s[:, 4:5]),
                               waits=[S("vector", v_x2a)])
                v_x2a_sguard = s_x2
                P.stage("vector",
                        lambda e: e.tensor_scalar(colf_s[:, 5:6], colf_s[:, 3:4],
                                                  1.0 / D, None, ALU.mult),
                        waits=[S("scalar", s_x2)])
                P.stage("vector",
                        lambda e: e.tensor_scalar(colf_s[:, 6:7], colf_s[:, 4:5],
                                                  1.0 / D, EPS, ALU.mult, ALU.add))
                P.stage("vector",
                        lambda e: e.tensor_scalar(colf_s[:, 7:8], colf_s[:, 5:6],
                                                  colf_s[:, 5:6], None, ALU.mult))
                v_x2b = P.stage("vector",
                                lambda e: e.tensor_tensor(colf_s[:, 6:7], colf_s[:, 6:7],
                                                          colf_s[:, 7:8], ALU.subtract))
                s_x3 = P.stage("scalar",
                               lambda e: e.activation(colf_s[:, 7:8], colf_s[:, 6:7],
                                                      AF.Ln),
                               waits=[S("vector", v_x2b)])
                s_x4 = P.stage("scalar",
                               lambda e: e.activation(colf_s[:, 6:7], colf_s[:, 7:8],
                                                      AF.Exp, scale=ko_s[:, 2:3]),
                               waits=[S("scalar", s_x3)])
                v_x2c = P.stage("vector",
                                lambda e: e.tensor_scalar(
                                    x2_s[:, :], xw_s[:, :], colf_s[:, 5:6],
                                    colf_s[:, 6:7], ALU.subtract, ALU.mult),
                                waits=[S("scalar", s_x4),
                                       Q("gpsimd", q_out_prev, 7)])
                q_out_prev = P.dma("gpsimd", outh[gtile * 128:(gtile + 1) * 128, :],
                                   x2_s[:, :], waits=[S("vector", v_x2c)], ch=7)
            xf_tguard = P.cnt["tensor"]

        P.wait("gpsimd", waits=[("q", k, v) for k, v in P.qcnt.items()
                                if k[0] == "gpsimd"])

    stack.close()

    emstack = contextlib.ExitStack()
    sems = {e: emstack.enter_context(nc.semaphore("se_" + e)) for e in ENGS}
    qsems = {k: emstack.enter_context(nc.semaphore("qu_%s_%d" % k))
             for k in sorted(P.qcnt.keys())}
    with nc.Block() as block:

        @block.sync
        def _(eng):
            for f in P.ops["sync"]:
                f(eng, sems, qsems)

        @block.scalar
        def _(eng):
            for f in P.ops["scalar"]:
                f(eng, sems, qsems)

        @block.vector
        def _(eng):
            for f in P.ops["vector"]:
                f(eng, sems, qsems)

        @block.tensor
        def _(eng):
            for f in P.ops["tensor"]:
                f(eng, sems, qsems)

        @block.gpsimd
        def _(eng):
            for f in P.ops["gpsimd"]:
                f(eng, sems, qsems)

    emstack.close()
    return nc


def kernel(L0_attn_u, L0_attn_in_proj_w, L0_attn_conv_w, L0_attn_conv_b,
           L0_attn_dt_bias, L0_attn_A_log, L0_attn_Dp, L0_attn_norm_w,
           L0_attn_out_proj_w, L0_ffn_gate_w, L0_ffn_w1, L0_ffn_w2, L0_ffn_w3,
           L0_ffn_sw1, L0_ffn_sw2, L0_ffn_sw3):
    u = np.asarray(L0_attn_u, np.float32)
    Win = np.asarray(L0_attn_in_proj_w, np.float32)
    WinT = np.ascontiguousarray(Win.T)
    mk = np.zeros((128, 512), np.float32)
    qq = np.arange(256)
    for k in range(128):
        mk[k, 0:256] = (qq >= k)
        mk[k, 256:512] = (qq >= 128 + k)
    shared = {
        "winx": np.ascontiguousarray(WinT[:, 2048:4240].astype(BF)),
        "winz": np.ascontiguousarray(WinT[:, 0:2048].astype(BF)),
        "woutT": np.ascontiguousarray(np.asarray(L0_attn_out_proj_w, np.float32).T.astype(BF)),
        "gateT": np.ascontiguousarray(np.asarray(L0_ffn_gate_w, np.float32).T.astype(BF)),
        "cwh": np.ascontiguousarray(
            np.asarray(L0_attn_conv_w, np.float32).reshape(17, 128, 4)
            .transpose(1, 2, 0).reshape(128, 68)),
        "cbh": np.ascontiguousarray(np.asarray(L0_attn_conv_b, np.float32).reshape(17, 128).T),
        "nwh": np.ascontiguousarray(np.asarray(L0_attn_norm_w, np.float32).reshape(16, 128).T),
        "dtbh": np.asarray(L0_attn_dt_bias, np.float32).reshape(16, 1),
        "alogh": np.asarray(L0_attn_A_log, np.float32).reshape(16, 1),
        "dprh": np.asarray(L0_attn_Dp, np.float32).reshape(1, 16),
        "identfh": np.eye(128, dtype=np.float32),
        "identbh": np.eye(128, dtype=BF),
        "onesfh": np.ones((1, 4096), np.float32),
        "onesch": np.ones((1, 128), np.float32),
        "konsth": np.stack([np.full(128, EPS, np.float32),
                            np.full(128, -1.0, np.float32),
                            np.full(128, -0.5, np.float32),
                            np.full(128, 1.0 / DI, np.float32)], axis=1),
        "maskh": mk,
        "w1h": np.ascontiguousarray(np.asarray(L0_ffn_w1, np.float32).astype(BF)),
        "w2h": np.ascontiguousarray(np.asarray(L0_ffn_w2, np.float32).astype(BF)),
        "w3h": np.ascontiguousarray(np.asarray(L0_ffn_w3, np.float32).astype(BF)),
        "sw1h": np.ascontiguousarray(np.asarray(L0_ffn_sw1, np.float32).astype(BF)),
        "sw2h": np.ascontiguousarray(np.asarray(L0_ffn_sw2, np.float32).astype(BF)),
        "sw3h": np.ascontiguousarray(np.asarray(L0_ffn_sw3, np.float32).astype(BF)),
    }

    in_maps = []
    for core in range(8):
        b_, q_ = core // 4, core % 4
        t0 = q_ * TOK
        useg = u[b_, t0:t0 + TOK]
        halo = u[b_, t0 - 3:t0] if q_ else np.zeros((3, D), np.float32)
        uh = np.concatenate([halo, useg], axis=0)
        pm = np.zeros((4, 4), np.float32)
        mv = np.zeros((4, 1), np.float32)
        for j in range(4):
            if j < q_:
                mv[j, 0] = 1.0
            for i in range(4):
                if j < i < q_:
                    pm[i, j] = 1.0
        m = dict(shared)
        m["uTb"] = np.ascontiguousarray(uh.T.astype(BF))
        m["ures"] = np.ascontiguousarray(useg)
        m["pmath"] = pm
        m["mvech"] = mv
        in_maps.append(m)

    nc = build_program()
    res = run_bass_kernel_spmd(nc, in_maps, core_ids=list(range(8)))
    out = np.zeros((2, 8192, D), np.float32)
    for core in range(8):
        b_, q_ = core // 4, core % 4
        out[b_, q_ * TOK:(q_ + 1) * TOK] = res.results[core]["out"]
    return out


# version 07
# speedup vs baseline: 1.1101x; 1.0094x over previous
import contextlib
import numpy as np
import ml_dtypes
import concourse.bass as bass
import concourse.mybir as mybir
from concourse.bass_utils import run_bass_kernel_spmd

F32 = mybir.dt.float32
BF16 = mybir.dt.bfloat16
AF = mybir.ActivationFunctionType
ALU = mybir.AluOpType
AXX = mybir.AxisListType.X
BF = ml_dtypes.bfloat16

TOK = 2048
CK = 256
NCH = 8
NH = 16
HD = 128
DS = 64
D = 1024
KD = 8
DI = 2048
XBCD = 2192
NEG = -1.0e30
EPS = 1e-5
EXPH = 2816
KH = 22
SHH = 1536
KSH = 12
HALF = 1024

ENGS = ("sync", "scalar", "vector", "tensor", "gpsimd")


class Plan:

    def __init__(self):
        self.ops = {e: [] for e in ENGS}
        self.cnt = {e: 0 for e in ENGS}
        self.qcnt = {}
        self.guards = {}

    def gwait(self, *keys):
        out = []
        for k in keys:
            out += self.guards.get(k, [])
        return out

    def gset(self, key, eng, val, kind="s"):
        self.guards[key] = [(kind, eng, val)]

    def _mk(self, W, body, eng):
        def run(e, sems, qsems):
            for item in W:
                kind, en, v = item
                if v > 0:
                    e.wait_ge((sems if kind == "s" else qsems)[en], v)
            if body is not None:
                body(e).then_inc((sems if eng[0] == "s" else qsems)[eng[1]],
                                 1 if eng[0] == "s" else 16)
        return run

    def qchan(self, eng, ch):
        return self.qcnt.get((eng, ch), 0)

    def stage(self, eng, body, waits=()):
        prev = self.cnt[eng]
        self.cnt[eng] += 1
        tgt = self.cnt[eng]
        W = tuple(waits)
        if eng != "tensor" and prev > 0:
            W = W + (("s", eng, prev),)
        self.ops[eng].append(self._mk(W, body, ("s", eng)))
        return tgt

    def dma(self, eng, out, in_, waits=(), ch=0):
        key = (eng, ch)
        prev = self.qcnt.get(key, 0)
        self.qcnt[key] = prev + 16
        tgt = self.qcnt[key]

        def body(e, out=out, in_=in_):
            return e.dma_start(out=out, in_=in_)

        W = tuple(waits) + ((("q", key, prev),) if prev > 0 else ())
        self.ops[eng].append(self._mk(W, body, ("q", key)))
        return tgt

    def wait(self, eng, waits=()):
        self.ops[eng].append(self._mk(tuple(waits), None, ("s", eng)))

    def barrier(self):
        snap = [("s", x, self.cnt[x]) for x in ENGS]
        qsnap = [("q", k, v) for k, v in self.qcnt.items()]
        for e in ENGS:
            self.wait(e, waits=[w for w in snap if w[1] != e] + qsnap)


def S(eng, v):
    return ("s", eng, v)


def Q(eng, v, ch=0):
    return ("q", (eng, ch), v)


def build_program():
    nc = bass.Bass("TRN2", target_bir_lowering=False)
    P = Plan()

    def din(name, shape, dt):
        return nc.dram_tensor(name, list(shape), dt, kind="ExternalInput").ap()

    uTb = din("uTb", (D, TOK + 3), BF16)
    ures = din("ures", (TOK, D), F32)
    winx = din("winx", (D, XBCD), BF16)
    winz = din("winz", (D, DI), BF16)
    woutT = din("woutT", (DI, D), BF16)
    gateT = din("gateT", (D, 8), BF16)
    cwh = din("cwh", (128, 68), F32)
    cbh = din("cbh", (128, 17), F32)
    nwh = din("nwh", (128, 16), F32)
    dtbh = din("dtbh", (16, 1), F32)
    alogh = din("alogh", (16, 1), F32)
    dprh = din("dprh", (1, 16), F32)
    maskh = din("maskh", (128, 512), F32)
    identfh = din("identfh", (128, 128), F32)
    identbh = din("identbh", (128, 128), BF16)
    onesfh = din("onesfh", (1, 4096), F32)
    onesch = din("onesch", (1, 128), F32)
    pmath = din("pmath", (4, 4), F32)
    mvech = din("mvech", (4, 1), F32)
    konsth = din("konsth", (128, 4), F32)
    w1h = din("w1h", (8, D, EXPH), BF16)
    w2h = din("w2h", (8, EXPH, D), BF16)
    w3h = din("w3h", (8, D, EXPH), BF16)
    sw1h = din("sw1h", (D, SHH), BF16)
    sw2h = din("sw2h", (SHH, D), BF16)
    sw3h = din("sw3h", (D, SHH), BF16)

    outh = nc.dram_tensor("out", [TOK, D], F32, kind="ExternalOutput").ap()

    yacch = nc.dram_tensor("yacch", [16, 128, DI], BF16).ap()
    sh = nc.dram_tensor("sh", [NCH, DS, NH * HD], BF16).ap()
    ccin = nc.dram_tensor("ccin", [65, 2048], F32).ap()
    ccout = nc.dram_tensor("ccout", [260, 2048], F32).ap()

    w1v = w1h.rearrange("e (k p) h -> e p k h", p=128)
    w3v = w3h.rearrange("e (k p) h -> e p k h", p=128)
    w2v = w2h.rearrange("e (k p) d -> e p k d", p=128)
    sw1v = sw1h.rearrange("(k p) h -> p k h", p=128)
    sw3v = sw3h.rearrange("(k p) h -> p k h", p=128)
    sw2v = sw2h.rearrange("(k p) d -> p k d", p=128)
    uTv = uTb.rearrange("(k p) t -> p k t", p=128)
    winxv = winx.rearrange("(k p) c -> p k c", p=128)
    winzv = winz.rearrange("(k p) c -> p k c", p=128)
    woutv = woutT.rearrange("(k p) d -> p k d", p=128)
    gatev = gateT.rearrange("(k p) e -> p k e", p=128)

    stack = contextlib.ExitStack()

    def sb(name, shape, dt):
        return stack.enter_context(nc.sbuf_tensor(name, list(shape), dt))

    cw_s = sb("cw_s", (128, 68), F32)
    cb_s = sb("cb_s", (128, 17), F32)
    nw_s = sb("nw_s", (128, 16), F32)
    dtb_s = sb("dtb_s", (16, 1), F32)
    A_s = sb("A_s", (16, 1), F32)
    tA_s = sb("tA_s", (16, 1), F32)
    dpr_s = sb("dpr_s", (1, 16), F32)
    dpbc_s = sb("dpbc_s", (128, 16), F32)
    mask_s = sb("mask_s", (128, 512), F32)
    idf_s = sb("idf_s", (128, 128), F32)
    idb_s = sb("idb_s", (128, 128), BF16)
    onec_s = sb("onec_s", (1, 128), F32)
    gatet_s = sb("gatet_s", (128, 64), BF16)
    x1_s = sb("x1_s", (128, 16 * D), BF16)
    prev_s = sb("prev_s", (DS, NH * HD), F32)
    prevb_s = sb("prevb_s", (DS, NH * HD), BF16)
    cdbc_s = sb("cdbc_s", (DS, NCH * NH), F32)
    eaT_s = sb("eaT_s", (128, 16 * NH), F32)
    alacc_s = sb("alacc_s", (16, 1), F32)
    Cb_s = sb("Cb_s", (DS, NCH * CK), BF16)
    sumsq_s = sb("sumsq_s", (128, 2), F32)
    pmat_s = sb("pmat_s", (4, 4), F32)
    ko_s = sb("ko_s", (128, 4), F32)
    mvec_s = sb("mvec_s", (4, 1), F32)

    pA = stack.enter_context(nc.psum_tensor("pA", [128, 1024], F32))
    pB = stack.enter_context(nc.psum_tensor("pB", [128, 1024], F32))
    pC = stack.enter_context(nc.psum_tensor("pC", [128, 1024], F32))
    pD = stack.enter_context(nc.psum_tensor("pD", [128, 1024], F32))
    PT = {"A": pA, "B": pB, "C": pC, "D": pD}

    def bank(t, lo):
        return (t, lo // 512)

    for dst, src in ((cw_s, cwh), (cb_s, cbh), (nw_s, nwh), (dtb_s, dtbh),
                     (mask_s, maskh), (idf_s, identfh), (idb_s, identbh),
                     (onec_s, onesch), (pmat_s, pmath), (mvec_s, mvech),
                     (tA_s, alogh), (dpr_s, dprh), (ko_s, konsth)):
        P.dma("sync", dst[:, :], src[:, :])
    P.dma("sync", gatet_s[:, :].rearrange("p (k e) -> p k e", k=KD), gatev[:, :, :])
    q_init = P.qchan("sync", 0)

    sA1 = P.stage("scalar",
                  lambda e: e.activation(A_s[:, :], tA_s[:, :], AF.Exp),
                  waits=[Q("sync", q_init)])
    vA1 = P.stage("vector",
                  lambda e: e.tensor_scalar(A_s[:, :], A_s[:, :], -1.0, None, ALU.mult),
                  waits=[S("scalar", sA1)])
    tDp = P.stage("tensor",
                  lambda e: e.matmul(pC[0:128, 0:16], onec_s[0:1, 0:128],
                                     dpr_s[0:1, 0:16], start=True, stop=True),
                  waits=[Q("sync", q_init)])
    vDp = P.stage("vector", lambda e: e.tensor_copy(dpbc_s[:, :], pC[0:128, 0:16]),
                  waits=[S("tensor", tDp)])
    P.gset(bank("C", 0), "vector", vDp)
    vz_al = P.stage("vector", lambda e: e.memset(alacc_s[:, :], 0.0))
    vz_pr = P.stage("vector", lambda e: e.memset(prev_s[:, :], 0.0))

    with contextlib.ExitStack() as scopeA:
        def sba(name, shape, dt):
            return scopeA.enter_context(nc.sbuf_tensor(name, list(shape), dt))

        winx_s = sba("winx_s", (128, KD * XBCD), BF16)
        uta_s = sba("uta_s", (128, 2 * KD * 259), BF16)
        lh2_s = sba("lh2_s", (2, 4096), F32)
        rh2_s = sba("rh2_s", (2, 4096), F32)
        acc_s = sba("acc_s", (128, 512), F32)
        xcv_s = sba("xcv_s", (128, 512), F32)
        bc_s = sba("bc_s", (128, 256), F32)
        bt_s = sba("bt_s", (128, 128), F32)
        bb_s = sba("bb_s", (DS, 256), BF16)
        xt_s = sba("xt_s", (128, 2 * DI), BF16)
        t16_s = sba("t16_s", (16, 256), F32)
        dtv_s = sba("dtv_s", (16, 256), F32)
        a_s = sba("a_s", (16, 256), F32)
        acum_s = sba("acum_s", (16, 256), F32)
        nacum_s = sba("nacum_s", (16, 256), F32)
        ws_s = sba("ws_s", (16, 256), F32)
        zz16_s = sba("zz16_s", (16, 256), F32)
        dtvT_s = sba("dtvT_s", (128, 32), F32)
        wsT_s = sba("wsT_s", (128, 32), F32)
        sc_s = sba("sc_s", (128, 512), F32)
        d0_s = sba("d0_s", (128, 2 * 768), F32)
        e_s = sba("e_s", (128, 2 * 768), F32)
        mt_s = sba("mt_s", (128, 4 * 384), BF16)
        ya_s = sba("ya_s", (128, 2 * DI), BF16)
        ss_s = sba("ss_s", (DS, NH * HD), BF16)
        wbt_s = sba("wbt_s", (128, 128), BF16)
        cdr_s = sba("cdr_s", (1, 16), F32)
        alrow_s = sba("alrow_s", (1, 2048), F32)

        P.dma("sync", winx_s[:, :].rearrange("p (k c) -> p k c", k=KD), winxv[:, :, :])
        P.dma("sync", lh2_s[0:1, :], onesfh[:, :])
        P.dma("sync", rh2_s[1:2, :], onesfh[:, :])
        q_wx = P.qchan("sync", 0)
        vz16 = P.stage("vector", lambda e: e.memset(zz16_s[:, :], 0.0))
        P.stage("vector", lambda e: e.memset(alrow_s[:, :], 0.0))

        def uta_dst(c):
            o = (c % 2) * (KD * 259)
            return uta_s[:, o:o + KD * 259].rearrange("p (k t) -> p k t", k=KD)

        q_uta = [0] * (NCH + 1)
        q_uta[0] = P.dma("sync", uta_dst(0), uTv[:, :, 0:259], ch=5)

        uta_tguard = [0, 0]
        ya_qd = [0] * NCH
        ss_qd = [0] * NCH
        pack_tguard = 0
        xt_tguard = 0
        bb_tguard = 0
        sm_tr_guard = 0
        prev_cdr_q = 0
        prev_pk1_q = 0
        v_acum_readers = 0

        for c in range(NCH):
            off = (c % 2) * (KD * 259)
            if c + 1 < NCH:
                q_uta[c + 1] = P.dma("sync", uta_dst(c + 1),
                                     uTv[:, :, (c + 1) * 256:(c + 1) * 256 + 259],
                                     waits=[S("tensor", uta_tguard[(c + 1) % 2])],
                                     ch=5 + (c + 1) % 2)

            trc = 0
            ssl_guard = [0, 0]
            xcv_tguard = [0, 0]
            last_bc_v = 0
            pend = None
            tdt = None

            def plan_transposes(ct, half, ssl):
                nonlocal trc, xt_tguard, bb_tguard, last_bc_v
                if ct < 16:
                    def mm_tr(e, half=half):
                        e.transpose(pB[:, 0:128],
                                    xcv_s[:, half * 256:half * 256 + 128],
                                    idf_s[:, :])
                        return e.transpose(pB[:, 512:640],
                                           xcv_s[:, half * 256 + 128:half * 256 + 256],
                                           idf_s[:, :])

                    ttr = P.stage("tensor", mm_tr,
                                  waits=P.gwait(bank("B", 0), bank("B", 512)) +
                                  [S("scalar", ssl)])
                    vtr = P.stage("scalar",
                                  lambda e, ct=ct: e.copy(
                                      xt_s[:, :].rearrange("p (b x) -> p b x", x=DI)
                                      [:, :, ct * 128:(ct + 1) * 128],
                                      pB[:, :].rearrange("p (b x) -> p b x", x=512)
                                      [:, :, 0:128]),
                                  waits=[S("tensor", ttr), S("tensor", xt_tguard)])
                    P.gset(bank("B", 0), "scalar", vtr)
                    P.gset(bank("B", 512), "scalar", vtr)
                    xcv_tguard[half] = ttr
                    trc += 2
                else:
                    vbc = P.stage("vector",
                                  lambda e: e.tensor_copy(bb_s[:, :], bc_s[0:64, :]),
                                  waits=[S("scalar", ssl), S("tensor", bb_tguard)])
                    vcc = P.stage("vector",
                                  lambda e, c=c: e.tensor_copy(
                                      Cb_s[:, c * CK:(c + 1) * CK], bc_s[64:128, :]),
                                  waits=[S("scalar", ssl)])

                    def mm_trb(e):
                        e.transpose(pB[0:128, 0:64], bc_s[0:64, 0:128],
                                    idf_s[0:64, 0:64])
                        return e.transpose(pB[0:128, 512:576], bc_s[0:64, 128:256],
                                           idf_s[0:64, 0:64])

                    ttr = P.stage("tensor", mm_trb,
                                  waits=P.gwait(bank("B", 0), bank("B", 512)) +
                                  [S("scalar", ssl)])
                    vtr = P.stage("vector",
                                  lambda e: (e.tensor_copy(bt_s[:, 0:64],
                                                           pB[0:128, 0:64]),
                                             e.tensor_copy(bt_s[:, 64:128],
                                                           pB[0:128, 512:576]))[-1],
                                  waits=[S("tensor", ttr)])
                    P.gset(bank("B", 0), "vector", vtr)
                    P.gset(bank("B", 512), "vector", vtr)
                    bb_tguard = ttr
                    last_bc_v = max(vbc, vcc)
                    trc += 2

            for ct in range(18):
                half = ct % 2
                if ct < 17:
                    def mm_ip(e, ct=ct, off=off, half=half):
                        for k in range(KD):
                            inst = e.matmul(
                                pA[:, half * 512:half * 512 + 259],
                                winx_s[:, k * XBCD + ct * 128:k * XBCD + (ct + 1) * 128],
                                uta_s[:, off + k * 259:off + (k + 1) * 259],
                                start=(k == 0), stop=(k == KD - 1))
                        return inst

                    tmm = P.stage("tensor", mm_ip,
                                  waits=P.gwait(bank("A", half * 512)) +
                                  [Q("sync", q_uta[c], 5 + c % 2), Q("sync", q_wx)])
                    uta_tguard[c % 2] = tmm
                else:
                    def mm_dt(e, off=off):
                        for k in range(KD):
                            inst = e.matmul(
                                pA[0:16, 512:512 + 256],
                                winx_s[:, k * XBCD + 2176:k * XBCD + 2192],
                                uta_s[:, off + k * 259 + 3:off + (k + 1) * 259],
                                start=(k == 0), stop=(k == KD - 1))
                        return inst

                    tdt = P.stage("tensor", mm_dt, waits=P.gwait(bank("A", 512)))
                    uta_tguard[c % 2] = tdt
                if pend is not None:
                    plan_transposes(*pend)
                    pend = None
                if ct >= 17:
                    continue
                ps = pA[:, half * 512:half * 512 + 259]
                ac = acc_s[:, half * 256:half * 256 + 256]
                P.stage("vector",
                        lambda e, ps=ps, ac=ac, ct=ct: e.tensor_scalar(
                            ac, ps[:, 0:256], cw_s[:, ct:ct + 1],
                            cb_s[:, ct:ct + 1], ALU.mult, ALU.add),
                        waits=[S("tensor", tmm), S("scalar", ssl_guard[half])])
                for w in (1, 2, 3):
                    vconv = P.stage("vector",
                                    lambda e, ps=ps, ac=ac, ct=ct, w=w:
                                    e.scalar_tensor_tensor(
                                        ac, ps[:, w:w + 256],
                                        cw_s[:, w * 17 + ct:w * 17 + ct + 1],
                                        ac, ALU.mult, ALU.add))
                P.gset(bank("A", half * 512), "vector", vconv)
                if ct < 16:
                    ssl = P.stage("scalar",
                                  lambda e, half=half: e.activation(
                                      xcv_s[:, half * 256:half * 256 + 256],
                                      acc_s[:, half * 256:half * 256 + 256], AF.Silu),
                                  waits=[S("vector", vconv),
                                         S("tensor", xcv_tguard[half])])
                else:
                    ssl = P.stage("scalar",
                                  lambda e, half=half: e.activation(
                                      bc_s[:, :],
                                      acc_s[:, half * 256:half * 256 + 256], AF.Silu),
                                  waits=[S("vector", vconv), S("tensor", bb_tguard)])
                ssl_guard[half] = ssl
                pend = (ct, half, ssl)

            def mm_dt(e, off=off):
                for k in range(KD):
                    inst = e.matmul(
                        pA[0:16, 512:512 + 256],
                        winx_s[:, k * XBCD + 2176:k * XBCD + 2192],
                        uta_s[:, off + k * 259 + 3:off + (k + 1) * 259],
                        start=(k == 0), stop=(k == KD - 1))
                return inst

            tdt = P.stage("tensor", mm_dt, waits=P.gwait(bank("A", 512)))
            uta_tguard[c % 2] = tdt

            s_sp1 = P.stage("scalar",
                            lambda e: e.activation(t16_s[:, :], pA[0:16, 512:512 + 256],
                                                   AF.Exp, bias=dtb_s[:, 0:1]),
                            waits=[S("tensor", tdt), Q("gpsimd", prev_cdr_q)])
            P.gset(bank("A", 512), "scalar", s_sp1)
            v_sp = P.stage("vector",
                           lambda e: e.tensor_scalar(t16_s[:, :], t16_s[:, :],
                                                     1.0, None, ALU.add),
                           waits=[S("scalar", s_sp1)])
            s_sp2 = P.stage("scalar",
                            lambda e: e.activation(dtv_s[:, :], t16_s[:, :], AF.Ln),
                            waits=[S("vector", v_sp), S("tensor", sm_tr_guard)])
            v_a = P.stage("vector",
                          lambda e: e.tensor_scalar(a_s[:, :], dtv_s[:, :],
                                                    A_s[:, 0:1], None, ALU.mult),
                          waits=[S("scalar", s_sp2), S("vector", vA1)])
            v_cum = P.stage("vector",
                            lambda e: e.tensor_tensor_scan(acum_s[:, :], a_s[:, :],
                                                           zz16_s[:, :], 0.0,
                                                           ALU.add, ALU.add),
                            waits=[S("vector", vz16), S("tensor", sm_tr_guard),
                                   Q("gpsimd", v_acum_readers)])
            v_nac = P.stage("vector",
                            lambda e: e.tensor_scalar(nacum_s[:, :], acum_s[:, :],
                                                      -1.0, None, ALU.mult),
                            waits=[Q("gpsimd", prev_pk1_q)])
            s_ws = P.stage("scalar",
                           lambda e: e.activation(ws_s[:, :], acum_s[:, :], AF.Exp,
                                                  bias=acum_s[:, 255:256],
                                                  scale=ko_s[0:16, 1:2]),
                           waits=[S("vector", v_cum), S("tensor", sm_tr_guard)])
            v_ws = P.stage("vector",
                           lambda e: e.tensor_tensor(ws_s[:, :], ws_s[:, :],
                                                     dtv_s[:, :], ALU.mult),
                           waits=[S("scalar", s_ws)])
            s_cd = P.stage("scalar",
                           lambda e: e.activation(t16_s[:, 0:1], acum_s[:, 255:256],
                                                  AF.Exp),
                           waits=[S("vector", v_cum), S("vector", v_sp)])
            v_al = P.stage("vector",
                           lambda e: e.tensor_tensor(alacc_s[:, :], alacc_s[:, :],
                                                     acum_s[:, 255:256], ALU.add),
                           waits=[S("vector", vz_al)])
            q_cdr = P.dma("gpsimd", cdr_s[0:1, :], t16_s[:, 0:1],
                          waits=[S("scalar", s_cd)])
            prev_cdr_q = q_cdr
            t_cd = P.stage("tensor",
                           lambda e: e.matmul(pC[0:64, 512:528], onec_s[0:1, 0:64],
                                              cdr_s[0:1, 0:16], start=True, stop=True),
                           waits=P.gwait(bank("C", 512)) + [Q("gpsimd", q_cdr)])
            v_cd = P.stage("vector",
                           lambda e, c=c: e.tensor_copy(cdbc_s[:, c * NH:(c + 1) * NH],
                                                        pC[0:64, 512:528]),
                           waits=[S("tensor", t_cd)])
            P.gset(bank("C", 512), "vector", v_cd)

            q_pk1 = P.dma("gpsimd",
                          lh2_s[1:2, :].rearrange("o (h t) -> o h t", h=16),
                          nacum_s[:, :],
                          waits=[S("vector", v_nac), S("tensor", pack_tguard)])
            q_pk2 = P.dma("gpsimd",
                          rh2_s[0:1, :].rearrange("o (h t) -> o h t", h=16),
                          acum_s[:, :],
                          waits=[S("vector", v_cum), S("tensor", pack_tguard)])
            q_pack = q_pk2
            v_acum_readers = q_pk2
            prev_pk1_q = q_pk1

            sm_done = {}
            for name, srcT in (("dtv", dtv_s), ("ws", ws_s), ("acum", acum_s)):
                for qt in range(2):
                    tb = trc % 2
                    wdeps = {"dtv": [S("scalar", s_sp2)],
                             "ws": [S("vector", v_ws)],
                             "acum": [S("vector", v_cum)]}[name]
                    ttr = P.stage("tensor",
                                  lambda e, srcT=srcT, qt=qt, tb=tb: e.transpose(
                                      pB[0:128, tb * 512 + 256:tb * 512 + 256 + 16],
                                      srcT[:, qt * 128:qt * 128 + 128],
                                      idf_s[0:16, 0:16]),
                                  waits=P.gwait(bank("B", tb * 512)) + wdeps)
                    sm_tr_guard = ttr
                    if name == "acum":
                        ev = P.stage("scalar",
                                     lambda e, c=c, qt=qt, tb=tb: e.activation(
                                         eaT_s[:, (c * 2 + qt) * 16:(c * 2 + qt) * 16 + 16],
                                         pB[0:128, tb * 512 + 256:tb * 512 + 272],
                                         AF.Exp),
                                     waits=[S("tensor", ttr)])
                        P.gset(bank("B", tb * 512), "scalar", ev)
                    else:
                        dst = (dtvT_s if name == "dtv" else wsT_s)
                        ev = P.stage("vector",
                                     lambda e, dst=dst, qt=qt, tb=tb: e.tensor_copy(
                                         dst[:, qt * 16:qt * 16 + 16],
                                         pB[0:128, tb * 512 + 256:tb * 512 + 272]),
                                     waits=[S("tensor", ttr)])
                        P.gset(bank("B", tb * 512), "vector", ev)
                    sm_done[(name, qt)] = ev
                    trc += 1

            def mm_sc(e, c=c):
                e.matmul(pC[:, 0:256], bb_s[:, 0:128],
                         Cb_s[:, c * CK:(c + 1) * CK], start=True, stop=True)
                return e.matmul(pC[:, 256:512], bb_s[:, 128:256],
                                Cb_s[:, c * CK:(c + 1) * CK], start=True, stop=True)

            t_sc = P.stage("tensor", mm_sc,
                           waits=P.gwait(bank("C", 0)) + [S("vector", last_bc_v)])
            bb_tguard = t_sc
            v_sc = P.stage("vector",
                           lambda e: e.tensor_tensor(sc_s[:, 0:512], pC[:, 0:512],
                                                     mask_s[:, 0:512], ALU.mult),
                           waits=[S("tensor", t_sc)])
            P.gset(bank("C", 0), "vector", v_sc)

            mt_tguard = [0, 0, 0, 0]
            sexp_guard = [0, 0]
            vd0_sguard = [0, 0]
            last_vya = 0
            last_vst = 0
            pend_mt = None

            def plan_ydiag_states(h0, vmts):
                nonlocal last_vya, last_vst, xt_tguard
                for hh in range(2):
                    h = h0 + hh
                    mo = h % 4 * 384
                    py = pA[:, hh * 512:hh * 512 + 256]

                    def mm_yd(e, h=h, mo=mo, py=py):
                        e.matmul(py[:, 0:128], mt_s[:, mo:mo + 128],
                                 xt_s[:, h * 128:(h + 1) * 128], start=True, stop=True)
                        e.matmul(py[:, 128:256], mt_s[:, mo + 128:mo + 256],
                                 xt_s[:, h * 128:(h + 1) * 128], start=True, stop=False)
                        return e.matmul(py[:, 128:256], mt_s[:, mo + 256:mo + 384],
                                        xt_s[:, DI + h * 128:DI + (h + 1) * 128],
                                        start=False, stop=True)

                    t_yd = P.stage("tensor", mm_yd,
                                   waits=P.gwait(bank("A", hh * 512)) +
                                   [S("vector", vmts[hh])])
                    mt_tguard[h % 4] = t_yd
                    xt_tguard = t_yd
                    v_ya = P.stage("vector",
                                   lambda e, h=h, py=py: e.scalar_tensor_tensor(
                                       ya_s[:, :].rearrange("p (b x) -> p b x", x=DI)
                                       [:, :, h * 128:(h + 1) * 128],
                                       xt_s[:, :].rearrange("p (b x) -> p b x", x=DI)
                                       [:, :, h * 128:(h + 1) * 128],
                                       dpbc_s[:, h:h + 1],
                                       py.rearrange("p (b x) -> p b x", x=128),
                                       ALU.mult, ALU.add),
                                   waits=[S("tensor", t_yd), S("vector", vDp),
                                          Q("gpsimd", ya_qd[c - 1] if c >= 1 else 0, 1)])
                    P.gset(bank("A", hh * 512), "vector", v_ya)
                    last_vya = v_ya

                    v_wb = P.stage("vector",
                                   lambda e, h=h: (
                                       e.tensor_scalar(wbt_s[:, 0:64], bt_s[:, 0:64],
                                                       wsT_s[:, h:h + 1], None, ALU.mult),
                                       e.tensor_scalar(wbt_s[:, 64:128], bt_s[:, 64:128],
                                                       wsT_s[:, 16 + h:16 + h + 1],
                                                       None, ALU.mult))[-1],
                                   waits=[S("vector", sm_done[("ws", 1)]),
                                          S("vector", last_vst),
                                          S("tensor", P.cnt["tensor"])])

                    pst = pB[0:64, hh * 512:hh * 512 + 128]

                    def mm_st(e, h=h, pst=pst):
                        e.matmul(pst, wbt_s[:, 0:64], xt_s[:, h * 128:(h + 1) * 128],
                                 start=True, stop=False)
                        return e.matmul(pst, wbt_s[:, 64:128],
                                        xt_s[:, DI + h * 128:DI + (h + 1) * 128],
                                        start=False, stop=True)

                    t_st = P.stage("tensor", mm_st,
                                   waits=P.gwait(bank("B", hh * 512)) +
                                   [S("vector", v_wb)])
                    xt_tguard = t_st
                    v_st = P.stage("vector",
                                   lambda e, h=h, pst=pst, c=c: (
                                       e.tensor_copy(ss_s[:, h * 128:(h + 1) * 128], pst),
                                       e.scalar_tensor_tensor(
                                           prev_s[:, h * 128:(h + 1) * 128],
                                           prev_s[:, h * 128:(h + 1) * 128],
                                           cdbc_s[:, c * NH + h:c * NH + h + 1],
                                           pst, ALU.mult, ALU.add))[-1],
                                   waits=[S("tensor", t_st), S("vector", v_cd),
                                          S("vector", vz_pr),
                                          Q("gpsimd", ss_qd[c - 1] if c >= 1 else 0, 2)])
                    P.gset(bank("B", hh * 512), "vector", v_st)
                    last_vst = v_st

            for pr in range(NH // 2):
                h0 = 2 * pr
                pp = pr % 2
                pt = pC if pp == 0 else pD
                tn = "C" if pp == 0 else "D"
                ptv = pt[:, :].rearrange("p (b x) -> p b x", x=512)[:, :, 0:384]

                def mm_diff(e, h0=h0, pt=pt):
                    for hh in range(2):
                        h = h0 + hh
                        e.matmul(pt[:, hh * 512:hh * 512 + 256],
                                 lh2_s[0:2, h * 256:h * 256 + 128],
                                 rh2_s[0:2, h * 256:h * 256 + 256],
                                 start=True, stop=True)
                        inst = e.matmul(pt[:, hh * 512 + 256:hh * 512 + 384],
                                        lh2_s[0:2, h * 256 + 128:h * 256 + 256],
                                        rh2_s[0:2, h * 256 + 128:h * 256 + 256],
                                        start=True, stop=True)
                    return inst

                t_df = P.stage("tensor", mm_diff,
                               waits=P.gwait(bank(tn, 0), bank(tn, 512)) +
                               [Q("gpsimd", q_pack), Q("sync", q_wx)])
                pack_tguard = t_df
                if pend_mt is not None:
                    plan_ydiag_states(*pend_mt)
                    pend_mt = None
                v_d0 = P.stage("vector",
                               lambda e, ptv=ptv, pp=pp: e.tensor_scalar(
                                   d0_s[:, pp * 768:pp * 768 + 768]
                                   .rearrange("p (b x) -> p b x", x=384),
                                   ptv, 0.0, None, ALU.min),
                               waits=[S("tensor", t_df), S("scalar", vd0_sguard[pp])])
                P.gset(bank(tn, 0), "vector", v_d0)
                P.gset(bank(tn, 512), "vector", v_d0)
                s_ex = P.stage("scalar",
                               lambda e, pp=pp: e.activation(
                                   e_s[:, pp * 768:pp * 768 + 768],
                                   d0_s[:, pp * 768:pp * 768 + 768], AF.Exp),
                               waits=[S("vector", v_d0), S("vector", sexp_guard[pp])])
                vd0_sguard[pp] = s_ex
                vmts = []
                for hh in range(2):
                    h = h0 + hh
                    eo = pp * 768 + hh * 384
                    v_mt = P.stage("vector",
                                   lambda e, h=h, eo=eo: (
                                       e.scalar_tensor_tensor(
                                           mt_s[:, h % 4 * 384:h % 4 * 384 + 256],
                                           e_s[:, eo:eo + 256],
                                           dtvT_s[:, h:h + 1], sc_s[:, 0:256],
                                           ALU.mult, ALU.mult),
                                       e.scalar_tensor_tensor(
                                           mt_s[:, h % 4 * 384 + 256:h % 4 * 384 + 384],
                                           e_s[:, eo + 256:eo + 384],
                                           dtvT_s[:, 16 + h:16 + h + 1], sc_s[:, 384:512],
                                           ALU.mult, ALU.mult))[-1],
                                   waits=[S("scalar", s_ex), S("vector", v_sc),
                                          S("vector", sm_done[("dtv", 1)]),
                                          S("tensor", mt_tguard[h % 4])])
                    vmts.append(v_mt)
                sexp_guard[pp] = vmts[-1]
                pend_mt = (h0, vmts)
            plan_ydiag_states(*pend_mt)
            pend_mt = None

            ya_qd[c] = P.dma("gpsimd",
                             yacch[c * 2:(c + 1) * 2, :, :].rearrange("a q d -> q a d"),
                             ya_s[:, :].rearrange("q (a d) -> q a d", a=2),
                             waits=[S("vector", last_vya)], ch=1)
            ss_qd[c] = P.dma("gpsimd", sh[c, :, :], ss_s[:, :],
                             waits=[S("vector", last_vst)], ch=2)

        q_cc1 = P.dma("gpsimd", ccin[0:64, :], prev_s[:, :],
                      waits=[S("vector", P.cnt["vector"])])
        P.dma("gpsimd", alrow_s[0:1, 0:16], alacc_s[:, 0:1],
              waits=[S("vector", P.cnt["vector"])])
        q_cc2 = P.dma("gpsimd", ccin[64:65, :], alrow_s[:, :])
        g_cc = P.stage("gpsimd",
                       lambda e: e.collective_compute(
                           "AllGather", ALU.bypass,
                           replica_groups=[[0, 1, 2, 3], [4, 5, 6, 7]],
                           ins=[ccin.opt()], outs=[ccout.opt()]),
                       waits=[Q("gpsimd", max(q_cc1, q_cc2))])

    P.barrier()

    with contextlib.ExitStack() as scopeB:
        def sbb(name, shape, dt):
            return scopeB.enter_context(nc.sbuf_tensor(name, list(shape), dt))

        winz_s = sbb("winz_s", (128, KD * DI), BF16)
        utb_s = sbb("utb_s", (128, 2 * KD * 256), BF16)
        wout_s = sbb("wout_s", (128, 16 * 1024), BF16)
        sz_s = sbb("sz_s", (128, DI), F32)
        y_s = sbb("y_s", (128, DI), F32)
        g_s = sbb("g_s", (128, DI), F32)
        gt_s = sbb("gt_s", (128, DI), BF16)
        ur_s = sbb("ur_s", (128, 2 * 2 * D), F32)
        yin_s = sbb("yin_s", (128, 2 * 2 * DI), BF16)
        sin_s = sbb("sin_s", (DS, 2 * NH * HD), BF16)
        xpre_s = sbb("xpre_s", (128, D), F32)
        sqx_s = sbb("sqx_s", (128, D), F32)
        col_s = sbb("col_s", (128, 8), F32)
        alogs_s = sbb("alogs_s", (4, 16), F32)
        coef_s = sbb("coef_s", (4, 16), F32)
        coefr_s = sbb("coefr_s", (1, 16), F32)
        coefbc_s = sbb("coefbc_s", (DS, 4 * 16), F32)

        P.dma("sync", winz_s[:, :].rearrange("p (k c) -> p k c", k=KD), winzv[:, :, :])
        P.dma("sync", wout_s[:, :].rearrange("p (k d) -> p k d", k=16), woutv[:, :, :])
        q_wzo = P.qchan("sync", 0)

        g_after_cc = P.cnt["gpsimd"]
        for j in range(4):
            P.dma("gpsimd", alogs_s[j:j + 1, :], ccout[65 * j + 64:65 * j + 65, 0:16],
                  waits=[S("gpsimd", g_after_cc)])
        q_al = P.qchan("gpsimd", 0)
        t_co = P.stage("tensor",
                       lambda e: e.matmul(pC[0:4, 0:16], pmat_s[:, :], alogs_s[:, :],
                                          start=True, stop=True),
                       waits=P.gwait(bank("C", 0)) + [Q("gpsimd", q_al)])
        s_co = P.stage("scalar",
                       lambda e: e.activation(coef_s[:, :], pC[0:4, 0:16], AF.Exp),
                       waits=[S("tensor", t_co)])
        P.gset(bank("C", 0), "scalar", s_co)
        v_co = P.stage("vector",
                       lambda e: e.tensor_scalar(coef_s[:, :], coef_s[:, :],
                                                 mvec_s[:, 0:1], None, ALU.mult),
                       waits=[S("scalar", s_co)])
        v_pz = P.stage("vector", lambda e: e.memset(prev_s[:, :], 0.0),
                       waits=[Q("gpsimd", q_al)])
        prev_v = v_pz
        q_sj_prev = 0
        for j in range(4):
            q_cr = P.dma("gpsimd", coefr_s[:, :], coef_s[j:j + 1, :],
                         waits=[S("vector", v_co),
                                S("tensor", P.cnt["tensor"])])
            t_bc = P.stage("tensor",
                           lambda e: e.matmul(pC[0:64, 32:48], onec_s[0:1, 0:64],
                                              coefr_s[0:1, :], start=True, stop=True),
                           waits=P.gwait(bank("C", 0)) + [Q("gpsimd", q_cr)])
            v_bc = P.stage("vector",
                           lambda e, j=j: e.tensor_copy(
                               coefbc_s[:, j * 16:(j + 1) * 16], pC[0:64, 32:48]),
                           waits=[S("tensor", t_bc)])
            P.gset(bank("C", 0), "vector", v_bc)
            q_sj = P.dma("gpsimd", y_s[0:64, 0:2048], ccout[65 * j:65 * j + 64, :],
                         waits=[S("gpsimd", g_after_cc), S("vector", prev_v)])

            def mk_carry(j=j):
                def body(e):
                    for h in range(NH):
                        inst = e.scalar_tensor_tensor(
                            prev_s[:, h * 128:(h + 1) * 128],
                            y_s[0:64, h * 128:(h + 1) * 128],
                            coefbc_s[:, j * 16 + h:j * 16 + h + 1],
                            prev_s[:, h * 128:(h + 1) * 128], ALU.mult, ALU.add)
                    return inst
                return body

            prev_v = P.stage("vector", mk_carry(),
                             waits=[S("vector", max(prev_v, v_bc)), Q("gpsimd", q_sj)])
        prevb_v = P.stage("vector",
                          lambda e: e.tensor_copy(prevb_s[:, :], prev_s[:, :]),
                          waits=[S("vector", prev_v)])

        def utb_dst(c):
            o = (c % 2) * (KD * 256)
            return utb_s[:, o:o + KD * 256].rearrange("p (k t) -> p k t", k=KD)

        q_utb = [0] * NCH
        q_yin = [0] * NCH
        q_sin = [0] * NCH
        q_ur = [0] * NCH
        q_utb[0] = P.dma("sync", utb_dst(0), uTv[:, :, 3:259], ch=5)
        q_yin[0] = P.dma("gpsimd",
                         yin_s[:, 0:2 * DI].rearrange("q (a d) -> q a d", a=2),
                         yacch[0:2, :, :].rearrange("a q d -> q a d"),
                         waits=[Q("gpsimd", P.qchan("gpsimd", 1), 1)], ch=3)
        q_sin[0] = P.dma("gpsimd", sin_s[:, 0:NH * HD], sh[0, :, :],
                         waits=[Q("gpsimd", P.qchan("gpsimd", 2), 2)], ch=5)
        q_ur[0] = P.dma("sync",
                        ur_s[:, 0:2 * D].rearrange("p (a d) -> p a d", a=2),
                        ures[0:256, :].rearrange("(a p) d -> p a d", p=128), ch=7)

        utb_tguard = [0, 0]
        yin_vguard = [0, 0]
        sin_vguard = [0, 0]
        ur_vguard = [0, 0]
        v_g_prev = 0
        s_sq_prev = 0
        s_xq_prev = 0
        v_xp_prev = 0
        v_x1_prev = 0
        t_gt_prev = 0
        t_op_prev = 0
        for c in range(NCH):
            par = c % 2
            if c + 1 < NCH:
                np_ = (c + 1) % 2
                q_utb[c + 1] = P.dma("sync", utb_dst(c + 1),
                                     uTv[:, :, (c + 1) * 256 + 3:(c + 1) * 256 + 259],
                                     waits=[S("tensor", utb_tguard[np_])], ch=5 + np_)
                q_yin[c + 1] = P.dma("gpsimd",
                                     yin_s[:, np_ * 2 * DI:(np_ + 1) * 2 * DI]
                                     .rearrange("q (a d) -> q a d", a=2),
                                     yacch[(c + 1) * 2:(c + 2) * 2, :, :]
                                     .rearrange("a q d -> q a d"),
                                     waits=[S("vector", yin_vguard[np_])], ch=3 + np_)
                q_sin[c + 1] = P.dma("gpsimd",
                                     sin_s[:, np_ * NH * HD:(np_ + 1) * NH * HD],
                                     sh[c + 1, :, :],
                                     waits=[S("vector", sin_vguard[np_])], ch=5 + np_)
                q_ur[c + 1] = P.dma("sync",
                                    ur_s[:, np_ * 2 * D:(np_ + 1) * 2 * D]
                                    .rearrange("p (a d) -> p a d", a=2),
                                    ures[(c + 1) * 256:(c + 2) * 256, :]
                                    .rearrange("(a p) d -> p a d", p=128),
                                    waits=[S("vector", ur_vguard[np_])], ch=7 + np_)

            t_z_eb = [0, 0, 0, 0]

            def zpt(eb):
                pt = pA if eb < 2 else pD
                return pt, ("A" if eb < 2 else "D"), (eb % 2) * 512

            def plan_z_mms(qt):
                nonlocal utb_tguard
                for eb in range(4):
                    pt, tn2, off2 = zpt(eb)

                    def mm_z(e, qt=qt, eb=eb, par=par, pt=pt, off2=off2):
                        for k in range(KD):
                            inst = e.matmul(
                                pt[:, off2:off2 + 512],
                                utb_s[:, par * KD * 256 + k * 256 + qt * 128:
                                      par * KD * 256 + k * 256 + qt * 128 + 128],
                                winz_s[:, k * DI + eb * 512:k * DI + (eb + 1) * 512],
                                start=(k == 0), stop=(k == KD - 1))
                        return inst

                    t_z = P.stage("tensor", mm_z,
                                  waits=P.gwait(bank(tn2, off2)) +
                                  [Q("sync", q_utb[c], 5 + par), Q("sync", q_wzo)])
                    utb_tguard[par] = t_z
                    t_z_eb[eb] = t_z

            def plan_z_silu(qt, wv):
                last_sz = 0
                for eb in range(4):
                    pt, tn2, off2 = zpt(eb)
                    s_z = P.stage("scalar",
                                  lambda e, eb=eb, pt=pt, off2=off2: e.activation(
                                      sz_s[:, eb * 512:(eb + 1) * 512],
                                      pt[:, off2:off2 + 512], AF.Silu),
                                  waits=[S("tensor", t_z_eb[eb]), S("vector", wv)])
                    P.gset(bank(tn2, off2), "scalar", s_z)
                    last_sz = s_z
                return last_sz

            def plan_yo(qt):
                def mm_yo(e, qt=qt, c=c):
                    for seg in range(4):
                        pt = pB if seg < 2 else pC
                        off = (seg % 2) * 512
                        inst = e.matmul(
                            pt[:, off:off + 512],
                            Cb_s[:, c * CK + qt * 128:c * CK + qt * 128 + 128],
                            prevb_s[:, seg * 512:(seg + 1) * 512],
                            start=True, stop=True)
                    return inst

                return P.stage("tensor", mm_yo,
                               waits=P.gwait(bank("B", 0), bank("B", 512),
                                             bank("C", 0), bank("C", 512)) +
                               [S("vector", prevb_v)])

            def plan_ymerge(qt, t_yo):
                last_vy = 0
                for h in range(NH):
                    seg = h // 4
                    pt = pB if seg < 2 else pC
                    off = (seg % 2) * 512 + (h % 4) * 128
                    v_y = P.stage("vector",
                                  lambda e, h=h, qt=qt, c=c, par=par, pt=pt, off=off:
                                  e.scalar_tensor_tensor(
                                      y_s[:, h * 128:(h + 1) * 128],
                                      pt[:, off:off + 128],
                                      eaT_s[:, (c * 2 + qt) * 16 + h:(c * 2 + qt) * 16 + h + 1],
                                      yin_s[:, par * 2 * DI + qt * DI + h * 128:
                                            par * 2 * DI + qt * DI + (h + 1) * 128],
                                      ALU.mult, ALU.add),
                                  waits=[S("tensor", t_yo),
                                         Q("gpsimd", q_yin[c], 3 + par),
                                         S("scalar", s_sq_prev)])
                    last_vy = v_y
                for tn2 in ("B", "C"):
                    P.gset(bank(tn2, 0), "vector", last_vy)
                    P.gset(bank(tn2, 512), "vector", last_vy)
                return last_vy

            def plan_tail(qt, last_vy, last_sz):
                nonlocal v_g_prev, s_sq_prev, s_xq_prev, v_xp_prev, v_x1_prev
                nonlocal t_gt_prev, t_op_prev
                v_g = P.stage("vector",
                              lambda e: e.tensor_tensor(g_s[:, :], y_s[:, :],
                                                        sz_s[:, :], ALU.mult),
                              waits=[S("vector", last_vy), S("scalar", last_sz),
                                     S("scalar", s_sq_prev),
                                     S("tensor", t_gt_prev)])
                v_g_prev = v_g
                s_sq = P.stage("scalar",
                               lambda e, qt=qt: e.activation(
                                   y_s[:, :], g_s[:, :], AF.Square,
                                   accum_out=sumsq_s[:, qt:qt + 1]),
                               waits=[S("vector", v_g)])
                s_sq_prev = s_sq
                return v_g, s_sq

            def plan_gt_op(qt, v_g, s_sq):
                nonlocal t_gt_prev, t_op_prev, v_xp_prev, s_xq_prev, v_x1_prev
                last_gt = 0
                for et in range(16):
                    ep = et % 2
                    t_gt = P.stage("tensor",
                                   lambda e, et=et, ep=ep: e.transpose(
                                       pC[:, ep * 512:ep * 512 + 128],
                                       g_s[:, et * 128:(et + 1) * 128], idf_s[:, :]),
                                   waits=P.gwait(bank("C", ep * 512)) +
                                   [S("vector", v_g)])
                    v_gt = P.stage("vector",
                                   lambda e, et=et, ep=ep: e.tensor_scalar(
                                       gt_s[:, et * 128:(et + 1) * 128],
                                       pC[:, ep * 512:ep * 512 + 128],
                                       nw_s[:, et:et + 1], None, ALU.mult),
                                   waits=[S("tensor", t_gt), S("tensor", t_op_prev)])
                    P.gset(bank("C", ep * 512), "vector", v_gt)
                    last_gt = v_gt
                t_gt_prev = P.cnt["tensor"]

                def mm_op(e):
                    for db in range(2):
                        for k in range(16):
                            inst = e.matmul(
                                pD[:, db * 512:(db + 1) * 512],
                                gt_s[:, k * 128:(k + 1) * 128],
                                wout_s[:, k * 1024 + db * 512:k * 1024 + (db + 1) * 512],
                                start=(k == 0), stop=(k == 15))
                    return inst

                t_op = P.stage("tensor", mm_op,
                               waits=P.gwait(bank("D", 0), bank("D", 512)) +
                               [S("vector", last_gt), Q("sync", q_wzo)])
                t_op_prev = t_op

                s_rm1 = P.stage("scalar",
                                lambda e, qt=qt: e.activation(
                                    col_s[:, 0:1], sumsq_s[:, qt:qt + 1], AF.Ln,
                                    bias=ko_s[:, 0:1], scale=ko_s[:, 3:4]),
                                waits=[S("scalar", s_sq), S("vector", v_x1_prev)])
                s_rm2 = P.stage("scalar",
                                lambda e: e.activation(col_s[:, 1:2], col_s[:, 0:1],
                                                       AF.Exp, scale=ko_s[:, 2:3]),
                                waits=[S("scalar", s_rm1)])
                v_xp = P.stage("vector",
                               lambda e, qt=qt, par=par: e.scalar_tensor_tensor(
                                   xpre_s[:, :], pD[:, 0:1024], col_s[:, 1:2],
                                   ur_s[:, par * 2 * D + qt * D:par * 2 * D + (qt + 1) * D],
                                   ALU.mult, ALU.add, accum_out=col_s[:, 2:3]),
                               waits=[S("tensor", t_op), S("scalar", s_rm2),
                                      Q("sync", q_ur[c], 7 + par),
                                      S("scalar", s_xq_prev)])
                P.gset(bank("D", 0), "vector", v_xp)
                P.gset(bank("D", 512), "vector", v_xp)
                v_xp_prev = v_xp
                if qt == 1:
                    ur_vguard[par] = v_xp
                s_xq = P.stage("scalar",
                               lambda e: e.activation(sqx_s[:, :], xpre_s[:, :],
                                                      AF.Square,
                                                      accum_out=col_s[:, 3:4]),
                               waits=[S("vector", v_xp)])
                s_xq_prev = s_xq
                P.stage("vector",
                        lambda e: e.tensor_scalar(col_s[:, 4:5], col_s[:, 2:3],
                                                  1.0 / D, None, ALU.mult),
                        waits=[S("scalar", s_xq)])
                P.stage("vector",
                        lambda e: e.tensor_scalar(col_s[:, 5:6], col_s[:, 3:4],
                                                  1.0 / D, EPS, ALU.mult, ALU.add))
                P.stage("vector",
                        lambda e: e.tensor_scalar(col_s[:, 6:7], col_s[:, 4:5],
                                                  col_s[:, 4:5], None, ALU.mult))
                v_c1 = P.stage("vector",
                               lambda e: e.tensor_tensor(col_s[:, 5:6], col_s[:, 5:6],
                                                         col_s[:, 6:7], ALU.subtract))
                s_l1 = P.stage("scalar",
                               lambda e: e.activation(col_s[:, 6:7], col_s[:, 5:6],
                                                      AF.Ln),
                               waits=[S("vector", v_c1)])
                s_l2 = P.stage("scalar",
                               lambda e: e.activation(col_s[:, 7:8], col_s[:, 6:7],
                                                      AF.Exp, scale=ko_s[:, 2:3]),
                               waits=[S("scalar", s_l1)])
                v_x1 = P.stage("vector",
                               lambda e, c=c, qt=qt: e.tensor_scalar(
                                   x1_s[:, (c * 2 + qt) * D:(c * 2 + qt + 1) * D],
                                   xpre_s[:, :], col_s[:, 4:5], col_s[:, 7:8],
                                   ALU.subtract, ALU.mult),
                               waits=[S("scalar", s_l2)])
                v_x1_prev = v_x1

            plan_z_mms(0)
            sz0 = plan_z_silu(0, v_g_prev)
            t_yo0 = plan_yo(0)
            vy0 = plan_ymerge(0, t_yo0)
            plan_z_mms(1)
            vg0, sq0 = plan_tail(0, vy0, sz0)
            sz1 = plan_z_silu(1, vg0)
            plan_gt_op(0, vg0, sq0)
            t_yo1 = plan_yo(1)
            vy1 = plan_ymerge(1, t_yo1)
            yin_vguard[par] = vy1
            vg1, sq1 = plan_tail(1, vy1, sz1)
            plan_gt_op(1, vg1, sq1)

            def mk_pu(c=c, par=par):
                def body(e):
                    for h in range(NH):
                        inst = e.scalar_tensor_tensor(
                            prev_s[:, h * 128:(h + 1) * 128],
                            prev_s[:, h * 128:(h + 1) * 128],
                            cdbc_s[:, c * NH + h:c * NH + h + 1],
                            sin_s[:, par * NH * HD + h * 128:par * NH * HD + (h + 1) * 128],
                            ALU.mult, ALU.add)
                    return inst
                return body

            v_pu = P.stage("vector", mk_pu(),
                           waits=[S("tensor", P.cnt["tensor"]),
                                  Q("gpsimd", q_sin[c], 5 + par)])
            sin_vguard[par] = v_pu
            prevb_v = P.stage("vector",
                              lambda e: e.tensor_copy(prevb_s[:, :], prev_s[:, :]),
                              waits=[S("vector", v_pu)])

    P.barrier()

    with contextlib.ExitStack() as scopeF:
        def sbf(name, shape, dt):
            return scopeF.enter_context(nc.sbuf_tensor(name, list(shape), dt))

        xf_s = sbf("xf_s", (128, KD * HALF), BF16)
        h_s = sbf("h_s", (128, KH * HALF), BF16)
        w13_s = sbf("w13_s", (128, 2 * 2 * KD * 256), BF16)
        w2b_s = sbf("w2b_s", (128, 2 * 4 * 1024), BF16)
        yf_s = sbf("yf_s", (128, 8 * D), F32)
        sl_s = sbf("sl_s", (128, 1024), F32)
        comb_s = sbf("comb_s", (128, 64), F32)
        sm_s = sbf("sm_s", (128, 8), F32)
        m8_s = sbf("m8_s", (128, 8), F32)
        wrk_s = sbf("wrk_s", (128, 8), F32)
        colf_s = sbf("colf_s", (128, 8), F32)
        xw_s = sbf("xw_s", (128, D), F32)
        x2_s = sbf("x2_s", (128, D), F32)

        def w13_dst(buf, mat):
            o = (buf * 2 + mat) * KD * 256
            return w13_s[:, o:o + KD * 256].rearrange("p (k t) -> p k t", k=KD)

        def w13_sl(buf, mat, k, lo, sz):
            o = (buf * 2 + mat) * KD * 256
            return w13_s[:, o + k * 256 + lo:o + k * 256 + lo + sz]

        def w2_dst(buf, nk):
            o = buf * 4 * 1024
            return w2b_s[:, o:o + nk * 1024].rearrange("p (k t) -> p k t", k=nk)

        w13_tguard = [0, 0]
        w2_tguard = [0, 0]
        vh_guard = 0
        s_si_prev = 0
        v_h_prev = 0
        s_g_prev = 0
        v_x2a_sguard = 0
        q_out_prev = 0
        xf_tguard = 0
        last_h_v = 0

        for hf in range(2):
            last_xf = 0
            trc = 0
            for tt in range(8):
                gtile = hf * 8 + tt
                for k in range(KD):
                    tb = trc % 2
                    t_tx = P.stage("tensor",
                                   lambda e, gtile=gtile, k=k, tb=tb: e.transpose(
                                       pA[:, tb * 512:tb * 512 + 64].bitcast(BF16),
                                       x1_s[:, gtile * D + k * 128:gtile * D + (k + 1) * 128],
                                       idb_s[:, :]),
                                   waits=P.gwait(bank("A", tb * 512)))
                    v_tx = P.stage("vector",
                                   lambda e, tt=tt, k=k, tb=tb: e.tensor_copy(
                                       xf_s[:, k * HALF + tt * 128:k * HALF + (tt + 1) * 128],
                                       pA[:, tb * 512:tb * 512 + 64].bitcast(BF16)),
                                   waits=[S("tensor", t_tx), S("tensor", xf_tguard)])
                    P.gset(bank("A", tb * 512), "vector", v_tx)
                    last_xf = v_tx
                    trc += 1

            for tt in range(8):
                def mm_g(e, tt=tt):
                    for k in range(KD):
                        inst = e.matmul(
                            pC[:, 0:8],
                            xf_s[:, k * HALF + tt * 128:k * HALF + (tt + 1) * 128],
                            gatet_s[:, k * 8:(k + 1) * 8],
                            start=(k == 0), stop=(k == KD - 1))
                    return inst

                t_g = P.stage("tensor", mm_g,
                              waits=P.gwait(bank("C", 0)) + [S("vector", last_xf)])
                v_g1 = P.stage("vector",
                               lambda e: e.reduce_max(colf_s[:, 0:1], pC[:, 0:8],
                                                      AXX, negate=True),
                               waits=[S("tensor", t_g), S("scalar", s_g_prev)])
                s_g1 = P.stage("scalar",
                               lambda e: e.activation(sm_s[:, :], pC[:, 0:8], AF.Exp,
                                                      bias=colf_s[:, 0:1],
                                                      accum_out=colf_s[:, 1:2]),
                               waits=[S("vector", v_g1)])
                s_g_prev = s_g1
                P.gset(bank("C", 0), "scalar", s_g1)
                P.stage("vector",
                        lambda e: e.reciprocal(colf_s[:, 2:3], colf_s[:, 1:2]),
                        waits=[S("scalar", s_g1)])
                P.stage("vector",
                        lambda e: e.tensor_scalar(sm_s[:, :], sm_s[:, :],
                                                  colf_s[:, 2:3], None, ALU.mult))
                P.stage("vector", lambda e: e.max(out=m8_s[:, :], in_=sm_s[:, :]))
                P.stage("vector", lambda e: e.memset(m8_s[:, 2:8], NEG))
                P.stage("vector",
                        lambda e: e.match_replace(out=wrk_s[:, :],
                                                  in_to_replace=m8_s[:, :],
                                                  in_values=sm_s[:, :],
                                                  imm_value=NEG))
                P.stage("vector",
                        lambda e: e.tensor_single_scalar(wrk_s[:, :], wrk_s[:, :],
                                                         NEG, ALU.is_equal))
                v_g2 = P.stage("vector",
                               lambda e, tt=tt: e.tensor_tensor(
                                   comb_s[:, tt * 8:(tt + 1) * 8],
                                   wrk_s[:, :], sm_s[:, :], ALU.mult))
            last_comb = P.cnt["vector"]

            v_yz = P.stage("vector", lambda e: e.memset(yf_s[:, :], 0.0))

            for ei in list(range(8)) + ["shared"]:
                nht = KH if ei != "shared" else KSH
                ngrp = (nht + 1) // 2

                def w13_load(g, buf, ei=ei, tg=None):
                    tgv = w13_tguard[buf] if tg is None else tg
                    a, b = (sw1v, sw3v) if ei == "shared" else (w1v[ei], w3v[ei])
                    q1 = P.dma("sync", w13_dst(buf, 0),
                               a[:, :, g * 256:g * 256 + 256],
                               waits=[S("tensor", tgv)], ch=1 + buf)
                    q3 = P.dma("sync", w13_dst(buf, 1),
                               b[:, :, g * 256:g * 256 + 256], ch=1 + buf)
                    return q3

                qld = [0] * ngrp
                qld[0] = w13_load(0, 0)
                if ngrp > 1:
                    qld[1] = w13_load(1, 1)
                pab = 0
                for g in range(ngrp):
                    for hi in range(2):
                        ht = g * 2 + hi
                        if ht >= nht:
                            break
                        if pab % 2 == 0:
                            pt1, pt3, n1, n3 = pA, pB, "A", "B"
                        else:
                            pt1, pt3, n1, n3 = pC, pD, "C", "D"

                        def mm_h(e, hi=hi, g=g, pt1=pt1, pt3=pt3):
                            buf = g % 2
                            for mat, pt in ((0, pt1), (1, pt3)):
                                for k in range(KD):
                                    for blk in range(2):
                                        inst = e.matmul(
                                            pt[:, blk * 512:blk * 512 + 512],
                                            w13_sl(buf, mat, k, hi * 128, 128),
                                            xf_s[:, k * HALF + blk * 512:k * HALF + (blk + 1) * 512],
                                            start=(k == 0), stop=(k == KD - 1),
                                            skip_group_check=True)
                            return inst

                        t_mh = P.stage("tensor", mm_h,
                                       waits=P.gwait(bank(n1, 0), bank(n1, 512),
                                                     bank(n3, 0), bank(n3, 512)) +
                                       [Q("sync", qld[g], 1 + g % 2),
                                        S("vector", last_xf)])
                        s_si = P.stage("scalar",
                                       lambda e, pt1=pt1: e.activation(
                                           sl_s[:, 0:1024], pt1[:, 0:1024], AF.Silu),
                                       waits=[S("tensor", t_mh),
                                              S("vector", v_h_prev)])
                        P.gset(bank(n1, 0), "scalar", s_si)
                        P.gset(bank(n1, 512), "scalar", s_si)
                        s_si_prev = s_si
                        v_h = P.stage("vector",
                                      lambda e, ht=ht, pt3=pt3: e.tensor_tensor(
                                          h_s[:, ht * HALF:(ht + 1) * HALF],
                                          sl_s[:, 0:1024], pt3[:, 0:1024], ALU.mult),
                                      waits=[S("scalar", s_si),
                                             S("tensor", vh_guard)])
                        P.gset(bank(n3, 0), "vector", v_h)
                        P.gset(bank(n3, 512), "vector", v_h)
                        v_h_prev = v_h
                        pab += 1
                    if g + 2 < ngrp:
                        qld[g + 2] = w13_load(g + 2, g % 2, tg=P.cnt["tensor"])
                w13_tguard[0] = P.cnt["tensor"]
                w13_tguard[1] = P.cnt["tensor"]
                last_h_v = v_h_prev

                kgroups = []
                k0 = 0
                while k0 < nht:
                    kgroups.append((k0, min(4, nht - k0)))
                    k0 += 4
                for th in range(2):
                    qw2 = [0] * len(kgroups)
                    qw2[0] = P.dma("sync", w2_dst(0, kgroups[0][1]),
                                   (sw2v if ei == "shared" else w2v[ei])
                                   [:, kgroups[0][0]:kgroups[0][0] + kgroups[0][1], :],
                                   waits=[S("tensor", w2_tguard[0])], ch=3)
                    if len(kgroups) > 1:
                        qw2[1] = P.dma("sync", w2_dst(1, kgroups[1][1]),
                                       (sw2v if ei == "shared" else w2v[ei])
                                       [:, kgroups[1][0]:kgroups[1][0] + kgroups[1][1], :],
                                       waits=[S("tensor", w2_tguard[1])], ch=4)
                    pts = (pA, pB, pC, pD)
                    for gi, (k0g, nk) in enumerate(kgroups):
                        buf = gi % 2

                        def mm_y(e, gi=gi, k0g=k0g, nk=nk, buf=buf, th=th, nht=nht):
                            for tt4 in range(4):
                                tt = th * 4 + tt4
                                pt = pts[tt4]
                                for ki in range(nk):
                                    k = k0g + ki
                                    st = (k == 0)
                                    sp = (k == nht - 1)
                                    e.matmul(pt[:, 0:512],
                                             h_s[:, k * HALF + tt * 128:k * HALF + (tt + 1) * 128],
                                             w2b_s[:, buf * 4096 + ki * 1024:buf * 4096 + ki * 1024 + 512],
                                             start=st, stop=sp,
                                             skip_group_check=True)
                                    inst = e.matmul(pt[:, 512:1024],
                                                    h_s[:, k * HALF + tt * 128:k * HALF + (tt + 1) * 128],
                                                    w2b_s[:, buf * 4096 + ki * 1024 + 512:buf * 4096 + (ki + 1) * 1024],
                                                    start=st, stop=sp,
                                                    skip_group_check=True)
                            return inst

                        gw = []
                        if gi == 0:
                            for pt_ in ("A", "B", "C", "D"):
                                gw += P.gwait(bank(pt_, 0), bank(pt_, 512))
                            gw += [S("vector", last_h_v)]
                        t_my = P.stage("tensor", mm_y,
                                       waits=gw + [Q("sync", qw2[gi], 3 + gi % 2)])
                        w2_tguard[buf] = t_my
                        vh_guard = t_my
                        if gi + 2 < len(kgroups):
                            qw2[gi + 2] = P.dma(
                                "sync", w2_dst(buf, kgroups[gi + 2][1]),
                                (sw2v if ei == "shared" else w2v[ei])
                                [:, kgroups[gi + 2][0]:kgroups[gi + 2][0] + kgroups[gi + 2][1], :],
                                waits=[S("tensor", t_my)], ch=3 + buf)
                    for tt4 in range(4):
                        tt = th * 4 + tt4
                        pt = pts[tt4]
                        tn = "ABCD"[tt4]
                        if ei == "shared":
                            v_yy = P.stage("vector",
                                           lambda e, tt=tt, pt=pt: e.scalar_tensor_tensor(
                                               yf_s[:, tt * D:(tt + 1) * D],
                                               pt[:, 0:1024], 1.0,
                                               yf_s[:, tt * D:(tt + 1) * D],
                                               ALU.mult, ALU.add),
                                           waits=[S("tensor", t_my),
                                                  S("vector", max(v_yz, last_comb))])
                        else:
                            v_yy = P.stage("vector",
                                           lambda e, tt=tt, pt=pt, ei=ei: e.scalar_tensor_tensor(
                                               yf_s[:, tt * D:(tt + 1) * D],
                                               pt[:, 0:1024],
                                               comb_s[:, tt * 8 + ei:tt * 8 + ei + 1],
                                               yf_s[:, tt * D:(tt + 1) * D],
                                               ALU.mult, ALU.add),
                                           waits=[S("tensor", t_my),
                                                  S("vector", max(v_yz, last_comb))])
                        P.gset(bank(tn, 0), "vector", v_yy)
                        P.gset(bank(tn, 512), "vector", v_yy)

            for tt in range(8):
                gtile = hf * 8 + tt
                v_x2a = P.stage("vector",
                                lambda e, tt=tt, gtile=gtile: e.scalar_tensor_tensor(
                                    xw_s[:, :], x1_s[:, gtile * D:(gtile + 1) * D],
                                    1.0, yf_s[:, tt * D:(tt + 1) * D],
                                    ALU.mult, ALU.add, accum_out=colf_s[:, 3:4]),
                                waits=[S("scalar", v_x2a_sguard)])
                s_x2 = P.stage("scalar",
                               lambda e: e.activation(sl_s[:, 0:1024], xw_s[:, :],
                                                      AF.Square,
                                                      accum_out=colf_s[:, 4:5]),
                               waits=[S("vector", v_x2a)])
                v_x2a_sguard = s_x2
                P.stage("vector",
                        lambda e: e.tensor_scalar(colf_s[:, 5:6], colf_s[:, 3:4],
                                                  1.0 / D, None, ALU.mult),
                        waits=[S("scalar", s_x2)])
                P.stage("vector",
                        lambda e: e.tensor_scalar(colf_s[:, 6:7], colf_s[:, 4:5],
                                                  1.0 / D, EPS, ALU.mult, ALU.add))
                P.stage("vector",
                        lambda e: e.tensor_scalar(colf_s[:, 7:8], colf_s[:, 5:6],
                                                  colf_s[:, 5:6], None, ALU.mult))
                v_x2b = P.stage("vector",
                                lambda e: e.tensor_tensor(colf_s[:, 6:7], colf_s[:, 6:7],
                                                          colf_s[:, 7:8], ALU.subtract))
                s_x3 = P.stage("scalar",
                               lambda e: e.activation(colf_s[:, 7:8], colf_s[:, 6:7],
                                                      AF.Ln),
                               waits=[S("vector", v_x2b)])
                s_x4 = P.stage("scalar",
                               lambda e: e.activation(colf_s[:, 6:7], colf_s[:, 7:8],
                                                      AF.Exp, scale=ko_s[:, 2:3]),
                               waits=[S("scalar", s_x3)])
                v_x2c = P.stage("vector",
                                lambda e: e.tensor_scalar(
                                    x2_s[:, :], xw_s[:, :], colf_s[:, 5:6],
                                    colf_s[:, 6:7], ALU.subtract, ALU.mult),
                                waits=[S("scalar", s_x4),
                                       Q("gpsimd", q_out_prev, 7)])
                q_out_prev = P.dma("gpsimd", outh[gtile * 128:(gtile + 1) * 128, :],
                                   x2_s[:, :], waits=[S("vector", v_x2c)], ch=7)
            xf_tguard = P.cnt["tensor"]

        P.wait("gpsimd", waits=[("q", k, v) for k, v in P.qcnt.items()
                                if k[0] == "gpsimd"])

    stack.close()

    emstack = contextlib.ExitStack()
    sems = {e: emstack.enter_context(nc.semaphore("se_" + e)) for e in ENGS}
    qsems = {k: emstack.enter_context(nc.semaphore("qu_%s_%d" % k))
             for k in sorted(P.qcnt.keys())}
    with nc.Block() as block:

        @block.sync
        def _(eng):
            for f in P.ops["sync"]:
                f(eng, sems, qsems)

        @block.scalar
        def _(eng):
            for f in P.ops["scalar"]:
                f(eng, sems, qsems)

        @block.vector
        def _(eng):
            for f in P.ops["vector"]:
                f(eng, sems, qsems)

        @block.tensor
        def _(eng):
            for f in P.ops["tensor"]:
                f(eng, sems, qsems)

        @block.gpsimd
        def _(eng):
            for f in P.ops["gpsimd"]:
                f(eng, sems, qsems)

    emstack.close()
    return nc


def kernel(L0_attn_u, L0_attn_in_proj_w, L0_attn_conv_w, L0_attn_conv_b,
           L0_attn_dt_bias, L0_attn_A_log, L0_attn_Dp, L0_attn_norm_w,
           L0_attn_out_proj_w, L0_ffn_gate_w, L0_ffn_w1, L0_ffn_w2, L0_ffn_w3,
           L0_ffn_sw1, L0_ffn_sw2, L0_ffn_sw3):
    u = np.asarray(L0_attn_u, np.float32)
    Win = np.asarray(L0_attn_in_proj_w, np.float32)
    WinT = np.ascontiguousarray(Win.T)
    mk = np.zeros((128, 512), np.float32)
    qq = np.arange(256)
    for k in range(128):
        mk[k, 0:256] = (qq >= k)
        mk[k, 256:512] = (qq >= 128 + k)
    shared = {
        "winx": np.ascontiguousarray(WinT[:, 2048:4240].astype(BF)),
        "winz": np.ascontiguousarray(WinT[:, 0:2048].astype(BF)),
        "woutT": np.ascontiguousarray(np.asarray(L0_attn_out_proj_w, np.float32).T.astype(BF)),
        "gateT": np.ascontiguousarray(np.asarray(L0_ffn_gate_w, np.float32).T.astype(BF)),
        "cwh": np.ascontiguousarray(
            np.asarray(L0_attn_conv_w, np.float32).reshape(17, 128, 4)
            .transpose(1, 2, 0).reshape(128, 68)),
        "cbh": np.ascontiguousarray(np.asarray(L0_attn_conv_b, np.float32).reshape(17, 128).T),
        "nwh": np.ascontiguousarray(np.asarray(L0_attn_norm_w, np.float32).reshape(16, 128).T),
        "dtbh": np.asarray(L0_attn_dt_bias, np.float32).reshape(16, 1),
        "alogh": np.asarray(L0_attn_A_log, np.float32).reshape(16, 1),
        "dprh": np.asarray(L0_attn_Dp, np.float32).reshape(1, 16),
        "identfh": np.eye(128, dtype=np.float32),
        "identbh": np.eye(128, dtype=BF),
        "onesfh": np.ones((1, 4096), np.float32),
        "onesch": np.ones((1, 128), np.float32),
        "konsth": np.stack([np.full(128, EPS, np.float32),
                            np.full(128, -1.0, np.float32),
                            np.full(128, -0.5, np.float32),
                            np.full(128, 1.0 / DI, np.float32)], axis=1),
        "maskh": mk,
        "w1h": np.ascontiguousarray(np.asarray(L0_ffn_w1, np.float32).astype(BF)),
        "w2h": np.ascontiguousarray(np.asarray(L0_ffn_w2, np.float32).astype(BF)),
        "w3h": np.ascontiguousarray(np.asarray(L0_ffn_w3, np.float32).astype(BF)),
        "sw1h": np.ascontiguousarray(np.asarray(L0_ffn_sw1, np.float32).astype(BF)),
        "sw2h": np.ascontiguousarray(np.asarray(L0_ffn_sw2, np.float32).astype(BF)),
        "sw3h": np.ascontiguousarray(np.asarray(L0_ffn_sw3, np.float32).astype(BF)),
    }

    in_maps = []
    for core in range(8):
        b_, q_ = core // 4, core % 4
        t0 = q_ * TOK
        useg = u[b_, t0:t0 + TOK]
        halo = u[b_, t0 - 3:t0] if q_ else np.zeros((3, D), np.float32)
        uh = np.concatenate([halo, useg], axis=0)
        pm = np.zeros((4, 4), np.float32)
        mv = np.zeros((4, 1), np.float32)
        for j in range(4):
            if j < q_:
                mv[j, 0] = 1.0
            for i in range(4):
                if j < i < q_:
                    pm[i, j] = 1.0
        m = dict(shared)
        m["uTb"] = np.ascontiguousarray(uh.T.astype(BF))
        m["ures"] = np.ascontiguousarray(useg)
        m["pmath"] = pm
        m["mvech"] = mv
        in_maps.append(m)

    nc = build_program()
    res = run_bass_kernel_spmd(nc, in_maps, core_ids=list(range(8)))
    out = np.zeros((2, 8192, D), np.float32)
    for core in range(8):
        b_, q_ = core // 4, core % 4
        out[b_, q_ * TOK:(q_ + 1) * TOK] = res.results[core]["out"]
    return out
